# version 09
# baseline (speedup 1.0000x reference)
import contextlib
import numpy as np

import concourse.bass as bass
import concourse.mybir as mybir
from concourse.bass import IndirectOffsetOnAxis
from concourse.bass_utils import run_bass_kernel_spmd

F32 = mybir.dt.float32
BF16 = mybir.dt.bfloat16
AX = mybir.AxisListType.X
AF = mybir.ActivationFunctionType
ALU = mybir.AluOpType

D = 1024
EXP = 8
II = 2048
SCALE = 1.0 / 32.0
NBUF = 3
CAP = 512
ENGS = ("tensor", "vector", "scalar", "sync", "gpsimd")


class K:

    def __init__(self):
        self.plan = {e: [] for e in ENGS}
        self.cnt = {}
        self.sems = {}

    def sem(self, name, stack, nc):
        self.sems[name] = stack.enter_context(nc.semaphore(name))
        self.cnt[name] = 0

    def w(self, eng, evt):
        if evt is None:
            return
        s, v = evt
        sem = self.sems[s]
        self.plan[eng].append(lambda E, sem=sem, v=v: E.wait_ge(sem, v))

    def wmax(self, eng, evts):
        by = {}
        for e in evts:
            if e is None:
                continue
            by[e[0]] = max(by.get(e[0], 0), e[1])
        for s, v in by.items():
            self.w(eng, (s, v))

    def e(self, eng, fn, inc=None, amt=1):
        if eng == "vector":
            self.w("vector", ("ev", self.cnt["ev"]))
            if inc is None:
                inc, amt = "ev", 1
        if inc is None:
            self.plan[eng].append(fn)
            return None
        self.cnt[inc] += amt
        v = self.cnt[inc]
        sem = self.sems[inc]
        self.plan[eng].append(lambda E, fn=fn, sem=sem, amt=amt: fn(E).then_inc(sem, amt))
        return (inc, v)


class Banks:

    def __init__(self, k, ap, nbanks, width):
        self.k, self.ap, self.n, self.w = k, ap, nbanks, width
        self.free = [[] for _ in range(nbanks)]
        self.rr = 0

    def acq(self, cols=None):
        i = self.rr % self.n
        self.rr += 1
        self.k.wmax("tensor", self.free[i])
        self.free[i] = []
        sl = self.ap[:, i * self.w:(i + 1) * self.w if cols is None else i * self.w + cols]
        return i, sl

    def sl(self, i, c0, c1):
        return self.ap[:, i * self.w + c0:i * self.w + c1]

    def rel(self, i, evt):
        self.free[i].append(evt)


def build_nc(tokens, debug=False, stages='ABCDEFGH', a_level=4, f_level=9, g_level=9):
    M = min(1024, tokens)
    NMT = tokens // M
    NT = M // 128
    NN = max(1, M // 512)
    NW = min(512, M)
    SM = 8 * M
    assert tokens >= 1024, "sparse MoE layout assumes M=1024 macro-tiles"

    nc = bass.Bass("TRN2", target_bir_lowering=False,
                   detect_race_conditions=False)
    k = K()

    x_d = nc.dram_tensor("x", [tokens, D], F32, kind="ExternalInput").ap()
    y_d = nc.dram_tensor("y", [tokens, D], F32, kind="ExternalOutput").ap()
    wqkv_d = nc.dram_tensor("wqkv", [D, 3 * D], F32, kind="ExternalInput").ap()
    woutT_d = nc.dram_tensor("woutT", [D, D], F32, kind="ExternalInput").ap()
    gwT_d = nc.dram_tensor("gwT", [D, EXP], F32, kind="ExternalInput").ap()
    w1T_d = nc.dram_tensor("w1T", [EXP, D, D], F32, kind="ExternalInput").ap()
    w2T_d = nc.dram_tensor("w2T", [EXP, D, D], F32, kind="ExternalInput").ap()
    w3T_d = nc.dram_tensor("w3T", [EXP, D, D], F32, kind="ExternalInput").ap()
    sw1T_d = nc.dram_tensor("sw1T", [D, II], F32, kind="ExternalInput").ap()
    sw2T_d = nc.dram_tensor("sw2T", [D, II], F32, kind="ExternalInput").ap()
    sw3T_d = nc.dram_tensor("sw3T", [II, D], F32, kind="ExternalInput").ap()
    b1T_d = nc.dram_tensor("b1T", [D, EXP], F32, kind="ExternalInput").ap()
    b2T_d = nc.dram_tensor("b2T", [D, EXP], F32, kind="ExternalInput").ap()
    b3T_d = nc.dram_tensor("b3T", [D, EXP], F32, kind="ExternalInput").ap()
    bout_d = nc.dram_tensor("bout", [D], F32, kind="ExternalInput").ap()
    xt_d = nc.dram_tensor("xt_scr", [tokens, D], BF16, kind="ExternalOutput").ap()
    yt_d = nc.dram_tensor("yt_scr", [2 * tokens, D], BF16, kind="ExternalOutput").ap()
    dbg = {}
    if debug:
        for nm, shp in [("dbg_xnf", [128, SM]), ("dbg_logits", [128, NT * 8]),
                        ("dbg_srow", [128, 8 * NT]), ("dbg_qf", [128, SM]),
                        ("dbg_aof", [128, SM]), ("dbg_acc", [128, SM]),
                        ("dbg_hsb", [128, SM]), ("dbg_zz", [128, 16 * M])]:
            dbg[nm] = nc.dram_tensor(nm, shp, F32, kind="ExternalOutput").ap()

    with contextlib.ExitStack() as st:
        def sb(name, shape, dt):
            return st.enter_context(nc.sbuf_tensor(name, shape, dt))

        def ps(name, shape, dt):
            return st.enter_context(nc.psum_tensor(name, shape, dt))

        scr = sb("scr", [128, 32 * M], BF16)
        qf = scr[:, 0:SM]
        kf = scr[:, SM:2 * SM]
        vtok = scr[:, 2 * SM:2 * SM + NT * 1024]
        aof = scr[:, 3 * SM:4 * SM]
        sxn_g = scr[:, 0:8 * CAP]
        h_g = scr[:, 8 * CAP:16 * CAP]
        p_mx = scr[:, 16 * CAP:16 * CAP + NT * CAP]
        xng_tok = scr[:, 24 * CAP:32 * CAP]
        sxng_tok = scr[:, 32 * CAP:40 * CAP]
        prodtok = scr[:, 40 * CAP:48 * CAP]
        prodtile = scr[:, 48 * CAP:49 * CAP]
        zero_zone = scr[:, 0:4096]
        h_sb = scr[:, SM:2 * SM]
        zz = scr[:, 2 * SM:4 * SM]
        xn_f = sb("xn_f", [128, SM], BF16)
        acc = sb("acc", [128, SM], F32)
        wsl = [sb(f"wsl{i}", [128, 8 * 512], BF16) for i in range(NBUF)]
        xa = [sb(f"xa{i}", [128, D], F32) for i in range(2)]
        xfin = [sb(f"xfin{i}", [128, D], F32) for i in range(2)]
        otok = [sb(f"otok{i}", [128, D], F32) for i in range(2)]
        xn32 = sb("xn32", [128, D], F32)
        xntok = [sb(f"xntok{i}", [128, D], F32) for i in range(2)]
        s_tokall = sb("s_tokall", [128, 8 * NT], F32)
        slot_tokall = sb("slot_tokall", [128, 8 * NT], F32)
        m_tokall = sb("m_tokall", [128, 8 * NT], F32)
        r_tokall = sb("r_tokall", [128, 8 * NT], F32)
        m_em8 = sb("m_em8", [8, M], BF16)
        rk_em8 = sb("rk_em8", [8, M], F32)
        iotaC = sb("iotaC", [128, CAP], F32)
        idf8 = sb("idf8", [128, 8], mybir.dt.int32)
        idlo_i = sb("idlo_i", [128, 8], mybir.dt.int32)
        c31 = sb("c31", [128, 1], mybir.dt.int32)
        idhi = sb("idhi", [128, 8], BF16)
        idlo = sb("idlo", [128, 8], BF16)
        meta = sb("meta", [128, 32], BF16)
        meta32 = sb("meta32", [4, CAP], F32)
        idx_sb = [sb(f"idx{i}", [128, CAP // 128], mybir.dt.int32) for i in range(2)]
        ixs_sb = [sb(f"ixs{i}", [128, CAP // 128], mybir.dt.int32) for i in range(2)]
        sg_sb = [sb(f"sg{i}", [128, CAP // 128], F32) for i in range(2)]
        ytk_w = [sb(f"ytkw{i}", [128, 2 * D], BF16) for i in range(2)]
        idxf = sb("idxf", [128, 4], F32)
        ids1f = sb("ids1f", [128, 4], F32)
        eqf = sb("eqf", [128, 4], F32)
        mtt = sb("mtt", [128, 16], F32)
        logits = sb("logits", [128, NT * 8], F32)
        sm32 = sb("sm32", [128, 128], F32)
        p32 = sb("p32", [128, 128], F32)
        pn_sb = sb("pn_sb", [128, 128], BF16)
        pt_sb = sb("pt_sb", [128, 256], BF16)
        tmp3 = [sb(f"tmp3{i}", [128, 512], BF16) for i in range(2)]
        tmp2 = [sb(f"tmp2{i}", [128, 512], BF16) for i in range(2)]
        z1t = [sb(f"z1t{i}", [128, 512], BF16) for i in range(2)]
        gw_sb = sb("gw_sb", [128, 8 * EXP], F32)
        b1_sb = sb("b1_sb", [128, 8 * EXP], F32)
        b2_sb = sb("b2_sb", [128, 8 * EXP], F32)
        b3_sb = sb("b3_sb", [128, 8 * EXP], F32)
        bout_sb = sb("bout_sb", [128, 8], F32)
        identf = sb("identf", [128, 128], F32)
        identb = sb("identb", [128, 128], BF16)
        amask = sb("amask", [128, 128], F32)

        zerocol = sb("zerocol", [128, 1], F32)
        colp = sb("colp", [128, 20], F32)
        (CSX, CMEAN, CSXX, CEX2, CM2, CVAR, CSRT, CY0, CT1, CT2, CT3, CRSTD,
         CNRM, CRSUM, CRDEN, C32, CHALF, CT4) = range(18)

        def cp(i):
            return colp[:, i:i + 1]

        pmm = ps("pmm", [128, 4 * 512], F32)
        ptr = ps("ptr", [128, 512], F32)
        pbt = ps("pbt", [128, 2048], BF16)
        psml = ps("psml", [128, 512], F32)

        for s in ("wd0", "wd1", "wd2", "ws", "xa0", "xa1", "xf0",
                  "xf1", "od0", "od1", "dbg", "mm", "ev", "sc", "gp",
                  "xt0", "xt1", "gd", "sd", "zd", "yd0", "yd1"):
            k.sem(s, st, nc)

        mmB = Banks(k, pmm, 4, 512)
        trB = Banks(k, ptr, 1, 512)
        btB = Banks(k, pbt, 2, 1024)

        e1 = k.e("gpsimd", lambda E: E.memset(identf[:], 1.0), inc="gp")
        k.w("gpsimd", e1)
        e_idf = k.e("gpsimd", lambda E: E.affine_select(
            identf[:], identf[:], [[-1, 128]], ALU.is_equal, 0.0,
            base=0, channel_multiplier=1), inc="gp")
        e2 = k.e("gpsimd", lambda E: E.memset(amask[:], 0.0), inc="gp")
        k.w("gpsimd", e2)
        e_msk = k.e("gpsimd", lambda E: E.affine_select(
            amask[:], amask[:], [[-1, 128]], ALU.is_ge, -1.0e30,
            base=0, channel_multiplier=1), inc="gp")
        e_ioc = k.e("gpsimd", lambda E: E.iota(
            iotaC[:], [[1, CAP]], base=0, channel_multiplier=0,
            allow_small_or_imprecise_dtypes=True), inc="gp")
        e_idf8 = k.e("gpsimd", lambda E: E.iota(
            idf8[:], [[128, 8]], base=1, channel_multiplier=1), inc="gp")
        k.w("vector", e_idf)
        e_idb = k.e("vector", lambda E: E.tensor_copy(identb[:], identf[:]), inc="ev")
        k.e("vector", lambda E: E.memset(zerocol[:], 0.0))
        k.e("vector", lambda E: E.memset(cp(C32), 32.0))
        k.e("vector", lambda E: E.memset(cp(CHALF), 0.5))
        k.w("vector", e_idf8)
        k.e("vector", lambda E: E.memset(c31[:], 31))
        k.e("vector", lambda E: E.tensor_single_scalar(
            idlo_i[:], idf8[:], c31[:, 0:1], ALU.bitwise_and))
        k.e("vector", lambda E: E.tensor_copy(m_tokall[:, 0:8], idf8[:]))
        k.e("vector", lambda E: E.tensor_copy(m_tokall[:, 8:16], idlo_i[:]))
        k.e("vector", lambda E: E.tensor_copy(idlo[:], idlo_i[:]))
        k.e("vector", lambda E: E.tensor_sub(
            m_tokall[:, 0:8], m_tokall[:, 0:8], m_tokall[:, 8:16]))
        k.e("vector", lambda E: E.tensor_scalar_mul(idhi[:], m_tokall[:, 0:8], 1.0 / 32.0))


        def sdma(dst, src):
            return k.e("gpsimd", lambda E, d=dst, s=src: E.dma_start(
                out=d, in_=s, allow_slow_non_contiguous=True), inc="ws", amt=16)

        e_gw = sdma(gw_sb[:].rearrange("p (c e) -> p c e", c=8),
                    gwT_d.rearrange("(c p) e -> p c e", p=128))
        e_b1 = sdma(b1_sb[:].rearrange("p (t e) -> p t e", t=8),
                    b1T_d.rearrange("(t p) e -> p t e", p=128))
        e_b2 = sdma(b2_sb[:].rearrange("p (t e) -> p t e", t=8),
                    b2T_d.rearrange("(t p) e -> p t e", p=128))
        e_b3 = sdma(b3_sb[:].rearrange("p (t e) -> p t e", t=8),
                    b3T_d.rearrange("(t p) e -> p t e", p=128))
        e_bo = sdma(bout_sb[:], bout_d.rearrange("(t p) -> p t", p=128))
        e_gw = e_b1 = e_b2 = e_b3 = e_bo = ("ws", k.cnt["ws"])

        e_zz0 = k.e("vector", lambda E: E.memset(zero_zone, 0.0))
        k.w("sync", e_zz0)
        for kk in range(2 * tokens // 512):
            k.e("sync", lambda E, kk=kk: E.dma_start(
                out=yt_d.rearrange("(k g p) n -> k p g n", k=2 * tokens // 512, p=128)[kk],
                in_=zero_zone.rearrange("p (g n) -> p g n", g=4)), inc="zd", amt=16)
        for kk in range(tokens // 512):
            k.e("sync", lambda E, kk=kk: E.dma_start(
                out=xt_d.rearrange("(k g p) n -> k p g n", k=tokens // 512, p=128)[kk],
                in_=zero_zone.rearrange("p (g n) -> p g n", g=4)), inc="zd", amt=16)
        e_ztot = ("zd", k.cnt["zd"])

        bc_reg = nc.alloc_register(mybir.EngineType.Pool, "bc_reg")
        k.e("gpsimd", lambda E: nc.reg_mov(bc_reg, tokens - 1))
        bc2_reg = nc.alloc_register(mybir.EngineType.Pool, "bc2_reg")
        k.e("gpsimd", lambda E: nc.reg_mov(bc2_reg, 2 * tokens - 1))

        class Slabs:
            def __init__(self):
                self.kk = 0
                self.free_evt = [None] * NBUF

            def load(self, src3d):
                buf = self.kk % NBUF
                k.w("gpsimd", self.free_evt[buf])
                evt = k.e("gpsimd", lambda E, a=src3d, b=buf: E.dma_start(
                    out=wsl[b][:].rearrange("p (c n) -> p c n", c=8), in_=a),
                    inc=f"wd{buf}", amt=16)
                self.kk += 1
                return buf, evt

        slabs = Slabs()

        def w2d(ap2d, c0):
            return ap2d[:, c0:c0 + 512].rearrange("(c p) n -> p c n", p=128)

        def w3d(ap3d, e, c0):
            return ap3d[e, :, c0:c0 + 512].rearrange("(c p) n -> p c n", p=128)

        def wrow(ap2d, r0, c0):
            return ap2d[r0:r0 + 1024, c0:c0 + 512].rearrange("(c p) n -> p c n", p=128)

        st_ev = {
            "G_mm_end": None,
            "H_mm_end": None,
            "psml_free": None,
            "attn_mm_end": None,
            "D_mm_end": None,
            "xa_free": [None, None],
            "xfin_free": [None, None],
            "otok_store": [None, None],
            "xntok_free": [None, None],
            "sxn_rd_end": None,
            "tmp_free": [None, None],
            "z1t_free": [None, None],
            "gate_prev": None,
            "xt_evts": [],
            "meta_mm_prev": None,
            "xng_free": None,
            "sxng_rd_end": None,
            "sd_prev": None,
        }

        n_store = [0]

        def stage_A(mt):
            xnf_done = None
            x32evts = []
            xt_evts = st_ev["xt_evts"]
            xt_evts.clear()
            for tt in range(NT):
                row0 = mt * M + tt * 128
                slot = tt % 2
                k.w("sync", st_ev["xa_free"][slot])
                ld = k.e("sync", lambda E, r=row0, s=slot: E.dma_start(
                    out=xa[s][:], in_=x_d[r:r + 128, :]), inc=f"xa{slot}", amt=16)
                k.w("scalar", ld)
                if st_ev["gate_prev"] is not None:
                    k.w("scalar", st_ev["gate_prev"])
                e_sq = k.e("scalar", lambda E, s=slot: E.activation(
                    xn32[:], xa[s][:], AF.Square, accum_out=cp(CSXX)), inc="sc")
                k.w("vector", ld)
                k.e("vector", lambda E, s=slot: E.reduce_sum(cp(CSX), xa[s][:], axis=AX))
                k.e("vector", lambda E: E.tensor_scalar_mul(cp(CMEAN), cp(CSX), 1.0 / D))
                k.w("vector", e_sq)
                k.e("vector", lambda E: E.tensor_scalar_mul(cp(CEX2), cp(CSXX), 1.0 / D))
                k.e("vector", lambda E: E.tensor_mul(cp(CM2), cp(CMEAN), cp(CMEAN)))
                k.e("vector", lambda E: E.tensor_sub(cp(CVAR), cp(CEX2), cp(CM2)))
                e_var = k.e("vector", lambda E: E.tensor_scalar_add(
                    cp(CT1), cp(CVAR), 1e-5), inc="ev")
                k.w("scalar", e_var)
                e_srt = k.e("scalar", lambda E: E.activation(
                    cp(CSRT), cp(CT1), AF.Sqrt), inc="sc")
                k.w("vector", e_srt)
                e_rc = k.e("vector", lambda E: E.reciprocal(cp(CY0), cp(CSRT)),
                           inc="ev")
                k.w("vector", e_rc)
                k.e("vector", lambda E: E.tensor_mul(cp(CT2), cp(CY0), cp(CY0)))
                k.e("vector", lambda E: E.tensor_mul(cp(CT2), cp(CT2), cp(CT1)))
                k.e("vector", lambda E: E.tensor_scalar(cp(CT3), cp(CT2), -0.5, 1.5, ALU.mult, ALU.add))
                k.e("vector", lambda E: E.tensor_mul(cp(CRSTD), cp(CY0), cp(CT3)))
                k.wmax("vector", st_ev["xntok_free"][slot] or [])
                e_xn = k.e("vector", lambda E, s=slot: E.tensor_scalar(
                    xntok[s][:], xa[s][:], cp(CMEAN), cp(CRSTD),
                    ALU.subtract, ALU.mult), inc="ev")
                st_ev["xa_free"][slot] = e_xn
                k.w("gpsimd", e_xn)
                if mt == 0 and tt == 0:
                    k.w("gpsimd", e_ztot)
                e_xt = k.e("gpsimd", lambda E, r=row0, s=slot: E.dma_start(
                    out=xt_d[r:r + 128, :], in_=xntok[s][:]),
                    inc=f"xt{slot}", amt=16)
                xt_evts.append(e_xt)
                if a_level < 2:
                    continue
                k.w("tensor", e_xn)
                if mt == 0 and tt == 0:
                    k.w("tensor", e_idf)
                ev_l = None
                for half in range(2):
                    bi, _ = trB.acq()
                    for cc in range(4):
                        c = half * 4 + cc
                        fn = (lambda E, s=slot, c=c, bi=bi, cc=cc: E.transpose(
                            trB.sl(bi, cc * 128, cc * 128 + 128),
                            xntok[s][:, c * 128:(c + 1) * 128], identf[:]))
                        ev = k.e("tensor", fn, inc="mm" if cc == 3 else None)
                    ev_l = ev
                    if a_level < 3:
                        trB.rel(bi, ev)
                        continue
                    k.w("vector", ev)
                    if half == 0 and st_ev["gate_prev"] is not None:
                        k.w("vector", st_ev["gate_prev"])
                        k.w("vector", xnf_done)
                    e_v = k.e("vector", lambda E, bi=bi, h=half: E.tensor_copy(
                        xn32[:, h * 512:(h + 1) * 512], trB.sl(bi, 0, 512)), inc="ev")
                    k.w("scalar", e_v)
                    if tt == 0 and half == 0 and st_ev["G_mm_end"] is not None:
                        k.w("scalar", st_ev["G_mm_end"])
                    e_s = k.e("scalar", lambda E, h=half, tt=tt: E.activation(
                        xn_f[:, 0:SM].rearrange("p (c t) -> p c t", c=8)
                        [:, h * 4:(h + 1) * 4, tt * 128:(tt + 1) * 128],
                        xn32[:, h * 512:(h + 1) * 512].rearrange("p (c t) -> p c t", c=4),
                        AF.Copy), inc="sc")
                    trB.rel(bi, e_v)
                    x32evts.append(e_v)
                    xnf_done = e_s
                st_ev["xntok_free"][slot] = [ev_l, xt_evts[-1]]
                if a_level < 3:
                    continue
                if a_level < 4:
                    continue
                k.wmax("tensor", x32evts[-2:])
                if mt == 0 and tt == 0:
                    k.w("tensor", e_gw)
                if st_ev["psml_free"] is not None:
                    k.w("tensor", st_ev["psml_free"])
                for c in range(8):
                    ev = k.e("tensor", lambda E, c=c: E.matmul(
                        psml[:, 0:8], xn32[:, c * 128:(c + 1) * 128],
                        gw_sb[:, c * 8:(c + 1) * 8],
                        start=(c == 0), stop=(c == 7)),
                        inc="mm" if c == 7 else None)
                k.w("vector", ev)
                e_lg = k.e("vector", lambda E, tt=tt: E.tensor_copy(
                    logits[:, tt * 8:(tt + 1) * 8], psml[:, 0:8]), inc="ev")
                st_ev["psml_free"] = e_lg
                st_ev["gate_prev"] = ev
            return xnf_done

        def preload_B(mt):
            return [slabs.load(w2d(wqkv_d, j * 512)) for j in range(3)]

        def stage_B(mt, xnf_done, pre):
            qk_done = None
            v_done = None
            for j in range(6):
                buf, rdy = pre[j] if j < len(pre) else slabs.load(w2d(wqkv_d, j * 512))
                k.w("tensor", rdy)
                if j == 0:
                    k.w("tensor", xnf_done)
                last_mm = None
                if j < 4:
                    dest = qf if j < 2 else kf
                    base_oc = (j % 2) * 4
                    for o in range(4):
                        for n in range(NN):
                            bi, bsl = mmB.acq(cols=NW)
                            for c in range(8):
                                ev = k.e("tensor", lambda E, b=buf, c=c, o=o, n=n, bsl=bsl: E.matmul(
                                    bsl, wsl[b][:, c * 512 + o * 128:c * 512 + (o + 1) * 128],
                                    xn_f[:, c * M + n * 512:c * M + n * 512 + NW],
                                    start=(c == 0), stop=(c == 7)),
                                    inc="mm" if c == 7 else None)
                            last_mm = ev
                            k.w("vector", ev)
                            if mt == 0 and j == 0 and o == 0 and n == 0:
                                k.w("vector", e_ztot)
                            oc = base_oc + o
                            e_v = k.e("vector", lambda E, bi=bi, oc=oc, n=n, d=dest: E.tensor_copy(
                                d[:, oc * M + n * 512:oc * M + n * 512 + NW],
                                mmB.sl(bi, 0, NW)), inc="ev")
                            mmB.rel(bi, e_v)
                            qk_done = e_v
                else:
                    for tt in range(NT):
                        bi, bsl = mmB.acq()
                        for c in range(8):
                            ev = k.e("tensor", lambda E, b=buf, c=c, tt=tt, bsl=bsl: E.matmul(
                                bsl, xn_f[:, c * M + tt * 128:c * M + (tt + 1) * 128],
                                wsl[b][:, c * 512:(c + 1) * 512],
                                start=(c == 0), stop=(c == 7)),
                                inc="mm" if c == 7 else None)
                        last_mm = ev
                        k.w("vector", ev)
                        if j == 4 and tt == 0 and st_ev["sd_prev"] is not None:
                            k.w("vector", st_ev["sd_prev"])
                        e_v = k.e("vector", lambda E, bi=bi, tt=tt, j=j: E.tensor_copy(
                            vtok[:, tt * 1024 + (j - 4) * 512:tt * 1024 + (j - 4) * 512 + 512],
                            mmB.sl(bi, 0, 512)), inc="ev")
                        mmB.rel(bi, e_v)
                        v_done = e_v
                slabs.free_evt[buf] = last_mm
            return qk_done, v_done

        def stage_C(mt, qk_done, v_done):
            aof_done = None
            attn_end = None
            for b in range(NT):
                bi, _ = mmB.acq(cols=128)
                if b == 0:
                    k.w("tensor", qk_done)
                for c in range(8):
                    ev = k.e("tensor", lambda E, c=c, b=b, bi=bi: E.matmul(
                        mmB.sl(bi, 0, 128),
                        qf[:, c * M + b * 128:c * M + (b + 1) * 128],
                        kf[:, c * M + b * 128:c * M + (b + 1) * 128],
                        start=(c == 0), stop=(c == 7)),
                        inc="mm" if c == 7 else None)
                attn_end = ev
                k.w("vector", ev)
                if b == 0 and mt == 0:
                    k.w("vector", (("gp", k.cnt["gp"])))
                k.e("vector", lambda E, bi=bi: E.scalar_tensor_tensor(
                    sm32[:], mmB.sl(bi, 0, 128), SCALE, amask[:],
                    ALU.mult, ALU.add))
                e_nm = k.e("vector", lambda E: E.reduce_max(
                    cp(CNRM), sm32[:], axis=AX, negate=True), inc="ev")
                mmB.rel(bi, e_nm)
                k.w("scalar", e_nm)
                e_p = k.e("scalar", lambda E: E.activation(
                    p32[:], sm32[:], AF.Exp, bias=cp(CNRM),
                    accum_out=cp(CRSUM)), inc="sc")
                k.w("vector", e_p)
                k.e("vector", lambda E: E.tensor_scalar_add(cp(CRDEN), cp(CRSUM), 1e-6))
                e_rc = k.e("vector", lambda E: E.reciprocal(cp(CRDEN), cp(CRDEN)),
                           inc="ev")
                k.w("vector", e_rc)
                e_pn = k.e("vector", lambda E: E.tensor_scalar_mul(
                    pn_sb[:], p32[:], cp(CRDEN)), inc="ev")
                sloti = b % 2
                k.w("tensor", e_pn)
                if mt == 0 and b < 2:
                    k.w("tensor", e_idb)
                k.wmax("tensor", btB.free[sloti])
                btB.free[sloti] = []
                e_tp = k.e("tensor", lambda E, s=sloti: E.transpose(
                    pbt[:, s * 1024:s * 1024 + 128], pn_sb[:], identb[:]), inc="mm")
                k.w("vector", e_tp)
                e_pt = k.e("vector", lambda E, s=sloti: E.tensor_copy(
                    pt_sb[:, s * 128:(s + 1) * 128], pbt[:, s * 1024:s * 1024 + 128]),
                    inc="ev")
                btB.free[sloti].append(e_pt)
                k.w("tensor", e_pt)
                if b == 0:
                    k.w("tensor", v_done)
                for half in range(2):
                    bi2, _ = mmB.acq()
                    for dt in range(4):
                        ev = k.e("tensor", lambda E, bi2=bi2, dt=dt, half=half, b=b, s=sloti: E.matmul(
                            mmB.sl(bi2, dt * 128, (dt + 1) * 128),
                            vtok[:, b * 1024 + (half * 4 + dt) * 128:b * 1024 + (half * 4 + dt + 1) * 128],
                            pt_sb[:, s * 128:(s + 1) * 128],
                            start=True, stop=True),
                            inc="mm" if dt == 3 else None)
                    attn_end = ev
                    k.w("scalar", ev)
                    e_a = k.e("scalar", lambda E, bi2=bi2, half=half, b=b: E.activation(
                        aof[:, 0:SM].rearrange("p (c t) -> p c t", c=8)
                        [:, half * 4:(half + 1) * 4, b * 128:(b + 1) * 128],
                        mmB.sl(bi2, 0, 512).rearrange("p (c t) -> p c t", c=4),
                        AF.Copy), inc="sc")
                    mmB.rel(bi2, e_a)
                    aof_done = e_a
            st_ev["attn_mm_end"] = attn_end
            return aof_done

        def stage_D(mt, aof_done):
            D_end = None
            acc_evts = []
            for j in range(2):
                buf, rdy = slabs.load(w2d(woutT_d, j * 512))
                k.w("tensor", rdy)
                if j == 0:
                    k.w("tensor", aof_done)
                last_mm = None
                for o in range(4):
                    for n in range(NN):
                        bi, bsl = mmB.acq(cols=NW)
                        for c in range(8):
                            ev = k.e("tensor", lambda E, b=buf, c=c, o=o, n=n, bsl=bsl: E.matmul(
                                bsl, wsl[b][:, c * 512 + o * 128:c * 512 + (o + 1) * 128],
                                aof[:, c * M + n * 512:c * M + n * 512 + NW],
                                start=(c == 0), stop=(c == 7)),
                                inc="mm" if c == 7 else None)
                        last_mm = ev
                        D_end = ev
                        oc = j * 4 + o
                        k.w("scalar", ev)
                        if j == 0 and o == 0 and n == 0:
                            if st_ev["H_mm_end"] is not None:
                                k.w("scalar", st_ev["H_mm_end"])
                            if mt == 0:
                                k.w("scalar", e_bo)
                        e_s = k.e("scalar", lambda E, bi=bi, oc=oc, n=n: E.activation(
                            acc[:, oc * M + n * 512:oc * M + n * 512 + NW],
                            mmB.sl(bi, 0, NW), AF.Identity,
                            bias=bout_sb[:, oc:oc + 1]), inc="sc")
                        mmB.rel(bi, e_s)
                        acc_evts.append(e_s)
                slabs.free_evt[buf] = last_mm
            st_ev["D_mm_end"] = D_end
            return acc_evts[-1]

        def stage_E(mt):
            srow_done = None
            for tt in range(NT):
                slot = tt % 2
                e_nm = k.e("vector", lambda E, tt=tt: E.reduce_max(
                    cp(CNRM), logits[:, tt * 8:(tt + 1) * 8], axis=AX, negate=True),
                    inc="ev")
                k.w("scalar", e_nm)
                e_p = k.e("scalar", lambda E, tt=tt: E.activation(
                    p32[:, 0:8],
                    logits[:, tt * 8:(tt + 1) * 8], AF.Exp, bias=cp(CNRM),
                    accum_out=cp(CRSUM)), inc="sc")
                k.w("vector", e_p)
                e_rc = k.e("vector", lambda E: E.reciprocal(cp(CRDEN), cp(CRSUM)),
                           inc="ev")
                k.w("vector", e_rc)
                k.e("vector", lambda E: E.tensor_scalar_mul(
                    sm32[:, 0:8], p32[:, 0:8], cp(CRDEN)))
                k.e("vector", lambda E: E.reduce_max(cp(CT1), sm32[:, 0:8], axis=AX))
                k.e("vector", lambda E: E.tensor_single_scalar(
                    sm32[:, 8:16], sm32[:, 0:8], cp(CT1), ALU.is_ge))
                k.e("vector", lambda E: E.scalar_tensor_tensor(
                    sm32[:, 16:24], sm32[:, 8:16], -1.0e30, sm32[:, 0:8],
                    ALU.mult, ALU.add))
                k.e("vector", lambda E: E.reduce_max(cp(CT2), sm32[:, 16:24], axis=AX))
                k.e("vector", lambda E: E.tensor_single_scalar(
                    sm32[:, 8:16], sm32[:, 0:8], cp(CT2), ALU.is_ge))
                e_s = k.e("vector", lambda E, tt=tt: E.tensor_mul(
                    s_tokall[:, tt * 8:tt * 8 + 8], sm32[:, 0:8], sm32[:, 8:16]))
                e_m = k.e("vector", lambda E, tt=tt: E.tensor_single_scalar(
                    m_tokall[:, tt * 8:tt * 8 + 8],
                    s_tokall[:, tt * 8:tt * 8 + 8],
                    zerocol[:, 0:1], ALU.is_gt))
                k.e("vector", lambda E, tt=tt: E.tensor_tensor_scan(
                    slot_tokall[:, tt * 8:tt * 8 + 8],
                    m_tokall[:, tt * 8:tt * 8 + 8],
                    m_tokall[:, tt * 8:tt * 8 + 8], 0.0, ALU.add, ALU.bypass))
                k.e("vector", lambda E, tt=tt: E.tensor_sub(
                    slot_tokall[:, tt * 8:tt * 8 + 8],
                    slot_tokall[:, tt * 8:tt * 8 + 8],
                    m_tokall[:, tt * 8:tt * 8 + 8]))
                k.w("tensor", e_m)
                if st_ev["psml_free"] is not None:
                    k.w("tensor", st_ev["psml_free"])
                ev = k.e("tensor", lambda E, tt=tt: E.transpose(
                    psml[0:8, 0:128], m_tokall[:, tt * 8:tt * 8 + 8],
                    identf[:]), inc="mm")
                k.w("vector", ev)
                e_r = k.e("vector", lambda E, tt=tt: E.tensor_copy(
                    m_em8[0:8, tt * 128:(tt + 1) * 128], psml[0:8, 0:128]))
                st_ev["psml_free"] = e_r
            k.e("vector", lambda E: E.tensor_tensor_scan(
                rk_em8[0:8, :], m_em8[0:8, :], m_em8[0:8, :], 0.0,
                ALU.add, ALU.bypass))
            e_r0 = k.e("vector", lambda E: E.tensor_sub(
                rk_em8[0:8, :], rk_em8[0:8, :], m_em8[0:8, :]))
            rt_done = None
            for tt in range(NT):
                k.w("tensor", e_r0)
                k.w("tensor", st_ev["psml_free"])
                ev = k.e("tensor", lambda E, tt=tt: E.transpose(
                    psml[0:128, 0:8], rk_em8[0:8, tt * 128:(tt + 1) * 128],
                    identf[0:8, 0:8]), inc="mm")
                k.w("vector", ev)
                rt_done = k.e("vector", lambda E, tt=tt: E.tensor_copy(
                    r_tokall[:, tt * 8:(tt + 1) * 8], psml[0:128, 0:8]))
                st_ev["psml_free"] = rt_done
            return rt_done

        def stage_F(mt, routing_done, xnf_done):
            NG = CAP // 128
            sd_after = {}
            for e in range(EXP):
                slot = e % 2
                if e == 0:
                    k.w("vector", routing_done)
                    k.w("vector", st_ev["attn_mm_end"])
                    k.w("vector", st_ev["D_mm_end"])
                else:
                    k.w("vector", st_ev["meta_mm_prev"])
                for tc in range(NT):
                    k.e("vector", lambda E, tc=tc, e=e: E.tensor_single_scalar(
                        p_mx[:, tc * CAP:(tc + 1) * CAP], iotaC[:, 0:CAP],
                        r_tokall[:, tc * 8 + e:tc * 8 + e + 1], ALU.is_equal))
                    e_pm = k.e("vector", lambda E, tc=tc, e=e: E.tensor_scalar_mul(
                        p_mx[:, tc * CAP:(tc + 1) * CAP],
                        p_mx[:, tc * CAP:(tc + 1) * CAP],
                        m_tokall[:, tc * 8 + e:tc * 8 + e + 1]))
                if f_level < 2:
                    continue
                k.e("vector", lambda E: E.tensor_copy(
                    meta[:].rearrange("p (t k) -> p t k", k=4)[:, 0:NT, 0:1],
                    idhi[:, 0:NT].rearrange("p (t u) -> p t u", u=1)))
                k.e("vector", lambda E: E.tensor_copy(
                    meta[:].rearrange("p (t k) -> p t k", k=4)[:, 0:NT, 1:2],
                    idlo[:, 0:NT].rearrange("p (t u) -> p t u", u=1)))
                k.e("vector", lambda E, e=e: E.tensor_copy(
                    meta[:].rearrange("p (t k) -> p t k", k=4)[:, 0:NT, 2:3],
                    s_tokall[:].rearrange("p (t q) -> p t q", q=8)[:, 0:NT, e:e + 1]))
                e_meta = k.e("vector", lambda E, e=e: E.tensor_copy(
                    meta[:].rearrange("p (t k) -> p t k", k=4)[:, 0:NT, 3:4],
                    slot_tokall[:].rearrange("p (t q) -> p t q", q=8)[:, 0:NT, e:e + 1]))
                k.w("tensor", e_pm)
                k.w("tensor", e_meta)
                if st_ev["psml_free"] is not None:
                    k.w("tensor", st_ev["psml_free"])
                for tc in range(NT):
                    mev = k.e("tensor", lambda E, tc=tc: E.matmul(
                        psml[0:4, 0:CAP], meta[:, tc * 4:(tc + 1) * 4],
                        p_mx[:, tc * CAP:(tc + 1) * CAP],
                        start=(tc == 0), stop=(tc == NT - 1)),
                        inc="mm" if tc == NT - 1 else None)
                st_ev["meta_mm_prev"] = mev
                k.w("vector", mev)
                e_m32 = k.e("vector", lambda E: E.tensor_copy(
                    meta32[0:4, 0:CAP], psml[0:4, 0:CAP]))
                if f_level < 3:
                    st_ev["psml_free"] = e_m32
                    continue
                k.w("tensor", e_m32)
                for g in range(NG):
                    tev = k.e("tensor", lambda E, g=g: E.transpose(
                        psml[0:128, g * 4:(g + 1) * 4],
                        meta32[0:4, g * 128:(g + 1) * 128],
                        identf[0:4, 0:4]), inc="mm" if g == NG - 1 else None)
                k.w("vector", tev)
                if e >= 2 and sd_after.get(e - 2) is not None:
                    k.w("vector", sd_after[e - 2])
                elif st_ev["sd_prev"] is not None:
                    k.w("vector", st_ev["sd_prev"])
                k.e("vector", lambda E: E.tensor_copy(
                    mtt[:, 0:4 * NG], psml[:, 0:4 * NG]))
                k.e("vector", lambda E: E.scalar_tensor_tensor(
                    ids1f[:, 0:NG], mtt[:, 0:4 * NG:4], 32.0,
                    mtt[:, 1:4 * NG:4], ALU.mult, ALU.add))
                k.e("vector", lambda E: E.tensor_single_scalar(
                    eqf[:, 0:NG], ids1f[:, 0:NG], cp(CHALF), ALU.is_lt))
                k.e("vector", lambda E, mt=mt: E.tensor_scalar_add(
                    idxf[:, 0:NG], ids1f[:, 0:NG], float(mt * M - 1)))
                k.e("vector", lambda E: E.scalar_tensor_tensor(
                    idxf[:, 0:NG], eqf[:, 0:NG], float(tokens + 1), idxf[:, 0:NG],
                    ALU.mult, ALU.add))
                e_idx = k.e("vector", lambda E, slot=slot: E.tensor_copy(
                    idx_sb[slot][:, 0:NG], idxf[:, 0:NG]))
                k.e("vector", lambda E: E.scalar_tensor_tensor(
                    idxf[:, 0:NG], ids1f[:, 0:NG], 2.0, mtt[:, 3:4 * NG:4],
                    ALU.mult, ALU.add))
                k.e("vector", lambda E, mt=mt: E.tensor_scalar_add(
                    idxf[:, 0:NG], idxf[:, 0:NG], float(2 * mt * M - 2)))
                k.e("vector", lambda E: E.scalar_tensor_tensor(
                    idxf[:, 0:NG], eqf[:, 0:NG], float(2 * tokens + 4), idxf[:, 0:NG],
                    ALU.mult, ALU.add))
                e_ixs = k.e("vector", lambda E, slot=slot: E.tensor_copy(
                    ixs_sb[slot][:, 0:NG], idxf[:, 0:NG]))
                e_sg = k.e("vector", lambda E, slot=slot: E.tensor_copy(
                    sg_sb[slot][:, 0:NG], mtt[:, 2:4 * NG:4]))
                st_ev["psml_free"] = e_sg
                if f_level < 4:
                    continue
                k.w("gpsimd", e_idx)
                k.wmax("gpsimd", st_ev["xt_evts"])
                if st_ev["xng_free"] is not None:
                    k.w("gpsimd", st_ev["xng_free"])
                if e == 0:
                    k.w("gpsimd", st_ev["attn_mm_end"])
                for g in range(NG):
                    k.e("gpsimd", lambda E, g=g, slot=slot: E.indirect_dma_start(
                        out=xng_tok[:, g * 1024:(g + 1) * 1024], out_offset=None,
                        in_=xt_d,
                        in_offset=IndirectOffsetOnAxis(
                            ap=idx_sb[slot][:, g:g + 1], axis=0),
                        bounds_check=bc_reg, oob_is_err=False), inc="gd", amt=16)
                gd_tot = ("gd", k.cnt["gd"])
                k.w("vector", gd_tot)
                if st_ev["sxng_rd_end"] is not None:
                    k.w("vector", st_ev["sxng_rd_end"])
                if e == 0:
                    k.w("vector", st_ev["G_mm_end"])
                e_sc = None
                for g in range(NG):
                    e_sc = k.e("vector", lambda E, g=g, slot=slot: E.tensor_scalar_mul(
                        sxng_tok[:, g * 1024:(g + 1) * 1024],
                        xng_tok[:, g * 1024:(g + 1) * 1024],
                        sg_sb[slot][:, g:g + 1]))
                st_ev["xng_free"] = e_sc
                if f_level < 5:
                    continue
                k.w("tensor", e_sc)
                if e == 0 and mt == 0:
                    k.w("tensor", e_idb)
                last_sxn_wr = None
                for dc in range(8):
                    bi, _ = btB.acq()
                    for g in range(NG):
                        tev = k.e("tensor", lambda E, dc=dc, g=g, bi=bi: E.transpose(
                            pbt[:, bi * 1024 + g * 128:bi * 1024 + (g + 1) * 128],
                            sxng_tok[:, g * 1024 + dc * 128:g * 1024 + (dc + 1) * 128],
                            identb[:]), inc="mm" if g == NG - 1 else None)
                    k.w("vector", tev)
                    if dc == 0 and e == 0:
                        k.w("vector", st_ev["attn_mm_end"])
                    last_sxn_wr = k.e("vector", lambda E, dc=dc, bi=bi: E.tensor_copy(
                        sxn_g[:, dc * CAP:(dc + 1) * CAP],
                        pbt[:, bi * 1024:bi * 1024 + CAP]))
                    btB.rel(bi, last_sxn_wr)
                st_ev["sxng_rd_end"] = tev
                if f_level < 6:
                    continue
                h_done = None
                for j in range(2):
                    buf, rdy = slabs.load(w3d(w1T_d, e, j * 512))
                    k.w("tensor", rdy)
                    if j == 0:
                        k.w("tensor", last_sxn_wr)
                    for o in range(4):
                        oc = j * 4 + o
                        bi, bsl = mmB.acq(cols=CAP)
                        for c in range(8):
                            ev = k.e("tensor", lambda E, b=buf, c=c, o=o, bsl=bsl: E.matmul(
                                bsl, wsl[b][:, c * 512 + o * 128:c * 512 + (o + 1) * 128],
                                sxn_g[:, c * CAP:c * CAP + CAP],
                                start=(c == 0), stop=(c == 7)),
                                inc="mm" if c == 7 else None)
                        k.w("scalar", ev)
                        if mt == 0 and e == 0 and oc == 0:
                            k.w("scalar", e_b1)
                            k.w("scalar", e_b2)
                            k.w("scalar", e_b3)
                            k.w("scalar", st_ev["attn_mm_end"])
                        e_h = k.e("scalar", lambda E, bi=bi, oc=oc, e=e: E.activation(
                            h_g[:, oc * CAP:oc * CAP + CAP],
                            mmB.sl(bi, 0, CAP), AF.Silu,
                            bias=b1_sb[:, oc * 8 + e:oc * 8 + e + 1]), inc="sc")
                        mmB.rel(bi, e_h)
                        h_done = e_h
                    slabs.free_evt[buf] = ev
                if f_level < 7:
                    continue
                w3bufs = {}
                w2bufs = {}
                prod_done = None
                for oc in range(8):
                    j = oc // 4
                    if j not in w3bufs:
                        w3bufs[j] = slabs.load(w3d(w3T_d, e, j * 512))
                        w2bufs[j] = slabs.load(w3d(w2T_d, e, j * 512))
                    b3buf, rdy3 = w3bufs[j]
                    b2buf, rdy2 = w2bufs[j]
                    bi3, bsl3 = mmB.acq(cols=CAP)
                    if oc % 4 == 0:
                        k.w("tensor", rdy3)
                    for c in range(8):
                        ev3 = k.e("tensor", lambda E, b=b3buf, c=c, oc=oc, bsl=bsl3: E.matmul(
                            bsl, wsl[b][:, c * 512 + (oc % 4) * 128:c * 512 + (oc % 4 + 1) * 128],
                            sxn_g[:, c * CAP:c * CAP + CAP],
                            start=(c == 0), stop=(c == 7)),
                            inc="mm" if c == 7 else None)
                    bi2, bsl2 = mmB.acq(cols=CAP)
                    if oc % 4 == 0:
                        k.w("tensor", rdy2)
                    if oc == 0:
                        k.w("tensor", h_done)
                    for c in range(8):
                        ev2 = k.e("tensor", lambda E, b=b2buf, c=c, oc=oc, bsl=bsl2: E.matmul(
                            bsl, wsl[b][:, c * 512 + (oc % 4) * 128:c * 512 + (oc % 4 + 1) * 128],
                            h_g[:, c * CAP:c * CAP + CAP],
                            start=(c == 0), stop=(c == 7)),
                            inc="mm" if c == 7 else None)
                    i2 = oc % 2
                    k.w("scalar", ev3)
                    if st_ev["tmp_free"][i2] is not None:
                        k.w("scalar", st_ev["tmp_free"][i2])
                    e_x3 = k.e("scalar", lambda E, bi3=bi3, oc=oc, e=e, i2=i2: E.activation(
                        tmp3[i2][:, 0:CAP], mmB.sl(bi3, 0, CAP), AF.Identity,
                        bias=b3_sb[:, oc * 8 + e:oc * 8 + e + 1]), inc="sc")
                    mmB.rel(bi3, e_x3)
                    k.w("scalar", ev2)
                    e_x2 = k.e("scalar", lambda E, bi2=bi2, oc=oc, e=e, i2=i2: E.activation(
                        tmp2[i2][:, 0:CAP], mmB.sl(bi2, 0, CAP), AF.Identity,
                        bias=b2_sb[:, oc * 8 + e:oc * 8 + e + 1]), inc="sc")
                    mmB.rel(bi2, e_x2)
                    k.w("vector", e_x3)
                    k.w("vector", e_x2)
                    if oc == 0 and e == 0:
                        k.w("vector", st_ev["D_mm_end"])
                    e_pr = k.e("vector", lambda E, i2=i2: E.tensor_mul(
                        prodtile[:, 0:CAP], tmp3[i2][:, 0:CAP], tmp2[i2][:, 0:CAP]))
                    st_ev["tmp_free"][i2] = e_pr
                    k.w("tensor", e_pr)
                    bi, _ = btB.acq()
                    for g in range(NG):
                        tev = k.e("tensor", lambda E, g=g, bi=bi: E.transpose(
                            pbt[:, bi * 1024 + g * 128:bi * 1024 + (g + 1) * 128],
                            prodtile[:, g * 128:(g + 1) * 128],
                            identb[:]), inc="mm" if g == NG - 1 else None)
                    k.w("vector", tev)
                    if oc == 0:
                        if e == 0:
                            if st_ev["sd_prev"] is not None:
                                k.w("vector", st_ev["sd_prev"])
                        elif sd_after.get(e - 1) is not None:
                            k.w("vector", sd_after[e - 1])
                    prod_done = k.e("vector", lambda E, oc=oc, bi=bi: E.tensor_copy(
                        prodtok[:].rearrange("p (g n) -> p g n", g=NG)
                        [:, :, oc * 128:(oc + 1) * 128],
                        pbt[:, bi * 1024:bi * 1024 + CAP]
                        .rearrange("p (g n) -> p g n", g=NG)))
                    btB.rel(bi, prod_done)
                    if oc % 4 == 3:
                        slabs.free_evt[b3buf] = ev3
                        slabs.free_evt[b2buf] = ev2
                st_ev["sxn_rd_end"] = ev3
                if f_level < 8:
                    continue
                k.w("gpsimd", prod_done)
                k.w("gpsimd", e_ixs)
                if mt == 0 and e == 0:
                    k.w("gpsimd", e_ztot)
                for g in range(NG):
                    k.e("gpsimd", lambda E, g=g, slot=slot: E.indirect_dma_start(
                        out=yt_d,
                        out_offset=IndirectOffsetOnAxis(
                            ap=ixs_sb[slot][:, g:g + 1], axis=0),
                        in_=prodtok[:, g * 1024:(g + 1) * 1024], in_offset=None,
                        bounds_check=bc2_reg, oob_is_err=False), inc="sd", amt=16)
                sd_after[e] = ("sd", k.cnt["sd"])
            st_ev["sd_prev"] = ("sd", k.cnt["sd"])
            return ("sd", k.cnt["sd"])

        def stage_G(mt, sd_tot):
            zi = [0]
            zz_done = None
            for j in range(4):
                b1buf, rdy1 = slabs.load(w2d(sw1T_d, j * 512))
                b2buf, rdy2 = slabs.load(w2d(sw2T_d, j * 512))
                for it_in in range(4):
                    it = j * 4 + it_in
                    for n in range(NN):
                        bia, bsla = mmB.acq(cols=NW)
                        if it_in == 0 and n == 0:
                            k.w("tensor", rdy1)
                        for c in range(8):
                            ev1 = k.e("tensor", lambda E, b=b1buf, c=c, it_in=it_in, n=n, bsl=bsla: E.matmul(
                                bsl, wsl[b][:, c * 512 + it_in * 128:c * 512 + (it_in + 1) * 128],
                                xn_f[:, c * M + n * 512:c * M + n * 512 + NW],
                                start=(c == 0), stop=(c == 7)),
                                inc="mm" if c == 7 else None)
                        bib, bslb = mmB.acq(cols=NW)
                        if it_in == 0 and n == 0:
                            k.w("tensor", rdy2)
                        for c in range(8):
                            ev2 = k.e("tensor", lambda E, b=b2buf, c=c, it_in=it_in, n=n, bsl=bslb: E.matmul(
                                bsl, wsl[b][:, c * 512 + it_in * 128:c * 512 + (it_in + 1) * 128],
                                xn_f[:, c * M + n * 512:c * M + n * 512 + NW],
                                start=(c == 0), stop=(c == 7)),
                                inc="mm" if c == 7 else None)
                        i2 = zi[0] % 2
                        k.w("scalar", ev1)
                        if st_ev["z1t_free"][i2] is not None:
                            k.w("scalar", st_ev["z1t_free"][i2])
                        e_z1 = k.e("scalar", lambda E, bia=bia, i2=i2: E.activation(
                            z1t[i2][:, 0:NW], mmB.sl(bia, 0, NW), AF.Silu), inc="sc")
                        mmB.rel(bia, e_z1)
                        k.w("vector", ev2)
                        k.w("vector", e_z1)
                        if it == 0 and n == 0:
                            k.w("vector", st_ev["D_mm_end"])
                            k.w("vector", sd_tot)
                        e_zz = k.e("vector", lambda E, bib=bib, it=it, n=n, i2=i2: E.tensor_mul(
                            zz[:, it * M + n * 512:it * M + n * 512 + NW],
                            z1t[i2][:, 0:NW], mmB.sl(bib, 0, NW)), inc="ev")
                        mmB.rel(bib, e_zz)
                        st_ev["z1t_free"][i2] = e_zz
                        zz_done = e_zz
                        zi[0] += 1
                slabs.free_evt[b1buf] = ev1
                slabs.free_evt[b2buf] = ev2
            if g_level < 2:
                st_ev["G_mm_end"] = ev2
                return zz_done
            G_end = None
            acc_done = None
            for j in range(2):
                br0, rdy0 = slabs.load(wrow(sw3T_d, 0, j * 512))
                br1, rdy1 = slabs.load(wrow(sw3T_d, 1024, j * 512))
                for o in range(4):
                    oc = j * 4 + o
                    for n in range(NN):
                        bi, bsl = mmB.acq(cols=NW)
                        if o == 0 and n == 0:
                            k.w("tensor", rdy0)
                            k.w("tensor", rdy1)
                            k.w("tensor", zz_done)
                        for half, bb in ((0, br0), (1, br1)):
                            for c in range(8):
                                i = half * 8 + c
                                ev = k.e("tensor", lambda E, b=bb, c=c, o=o, n=n, i=i, bsl=bsl: E.matmul(
                                    bsl, wsl[b][:, c * 512 + o * 128:c * 512 + (o + 1) * 128],
                                    zz[:, i * M + n * 512:i * M + n * 512 + NW],
                                    start=(i == 0), stop=(i == 15)),
                                    inc="mm" if i == 15 else None)
                        G_end = ev
                        k.w("vector", ev)
                        e_a = k.e("vector", lambda E, bi=bi, oc=oc, n=n: E.tensor_add(
                            acc[:, oc * M + n * 512:oc * M + n * 512 + NW],
                            acc[:, oc * M + n * 512:oc * M + n * 512 + NW],
                            mmB.sl(bi, 0, NW)), inc="ev")
                        mmB.rel(bi, e_a)
                        acc_done = e_a
                slabs.free_evt[br0] = G_end
                slabs.free_evt[br1] = G_end
            st_ev["G_mm_end"] = G_end
            return acc_done

        def stage_H(mt, acc_done, sd_tot):
            H_end = None
            for tt in range(NT):
                row0 = mt * M + tt * 128
                slot = tt % 2
                k.w("sync", st_ev["xfin_free"][slot])
                ld = k.e("sync", lambda E, r=row0, s=slot: E.dma_start(
                    out=xfin[s][:], in_=x_d[r:r + 128, :]), inc=f"xf{slot}", amt=16)
                if tt == 0:
                    k.w("sync", sd_tot)
                if st_ev.get("ytk_free", [None, None])[slot] is not None:
                    k.w("sync", st_ev["ytk_free"][slot])
                ld_y = k.e("sync", lambda E, r=row0, s=slot: E.dma_start(
                    out=ytk_w[s][:].rearrange("p (two n) -> p two n", two=2),
                    in_=yt_d.rearrange("(t two) n -> t two n", two=2)[r:r + 128]),
                    inc=f"yd{slot}", amt=16)
                k.w("tensor", acc_done)
                k.w("vector", ld)
                k.w("vector", ld_y)
                if st_ev["otok_store"][slot] is not None:
                    k.w("vector", st_ev["otok_store"][slot])
                for half in range(2):
                    bi, _ = trB.acq()
                    for cc in range(4):
                        c = half * 4 + cc
                        ev = k.e("tensor", lambda E, c=c, tt=tt, bi=bi, cc=cc: E.transpose(
                            trB.sl(bi, cc * 128, cc * 128 + 128),
                            acc[:, c * M + tt * 128:c * M + (tt + 1) * 128],
                            identf[:]), inc="mm" if cc == 3 else None)
                    H_end = ev
                    k.w("vector", ev)
                    k.e("vector", lambda E, bi=bi, h=half, s=slot: E.tensor_add(
                        otok[s][:, h * 512:(h + 1) * 512], trB.sl(bi, 0, 512),
                        xfin[s][:, h * 512:(h + 1) * 512]))
                    k.e("vector", lambda E, h=half, s=slot: E.tensor_add(
                        otok[s][:, h * 512:(h + 1) * 512],
                        otok[s][:, h * 512:(h + 1) * 512],
                        ytk_w[s][:, h * 512:(h + 1) * 512]), inc="ev")
                    e_o = k.e("vector", lambda E, h=half, s=slot: E.tensor_add(
                        otok[s][:, h * 512:(h + 1) * 512],
                        otok[s][:, h * 512:(h + 1) * 512],
                        ytk_w[s][:, 1024 + h * 512:1024 + (h + 1) * 512]), inc="ev")
                    trB.rel(bi, e_o)
                st_ev["xfin_free"][slot] = e_o
                st_ev.setdefault("ytk_free", [None, None])[slot] = e_o
                k.w("sync", e_o)
                st_out = k.e("sync", lambda E, r=row0, s=slot: E.dma_start(
                    out=y_d[r:r + 128, :], in_=otok[s][:]), inc=f"od{slot}", amt=16)
                st_ev["otok_store"][slot] = st_out
                n_store[0] += 1
            st_ev["H_mm_end"] = H_end

        for mt in range(NMT):
            xnf_done = qk_done = v_done = aof_done = rt_done = acc_done = None
            sd_tot = None
            pre = []
            if "B" in stages:
                pre = preload_B(mt)
            if "A" in stages:
                xnf_done = stage_A(mt)
            if "B" in stages:
                qk_done, v_done = stage_B(mt, xnf_done, pre)
            if "C" in stages:
                aof_done = stage_C(mt, qk_done, v_done)
            if "D" in stages:
                stage_D(mt, aof_done)
            if "E" in stages:
                rt_done = stage_E(mt)
            if "F" in stages:
                sd_tot = stage_F(mt, rt_done, xnf_done)
            if "G" in stages:
                acc_done = stage_G(mt, sd_tot)
            if "H" in stages:
                stage_H(mt, acc_done, sd_tot)

        if debug:
            k.w("gpsimd", ("od0", k.cnt["od0"]))
            k.w("gpsimd", ("od1", k.cnt["od1"]))
            k.e("gpsimd", lambda E: E.dma_start(out=dbg["dbg_xnf"], in_=xn_f[:]),
                inc="dbg", amt=16)
            k.e("gpsimd", lambda E: E.dma_start(out=dbg["dbg_logits"], in_=logits[:]),
                inc="dbg", amt=16)
            k.e("gpsimd", lambda E: E.dma_start(out=dbg["dbg_srow"], in_=s_tokall[:]),
                inc="dbg", amt=16)
            k.e("gpsimd", lambda E: E.dma_start(out=dbg["dbg_qf"], in_=scr[:, 0:SM]),
                inc="dbg", amt=16)
            k.e("gpsimd", lambda E: E.dma_start(out=dbg["dbg_aof"], in_=scr[:, 3 * SM:4 * SM]),
                inc="dbg", amt=16)
            k.e("gpsimd", lambda E: E.dma_start(out=dbg["dbg_acc"], in_=acc[:]),
                inc="dbg", amt=16)
            k.e("gpsimd", lambda E: E.dma_start(out=dbg["dbg_hsb"], in_=scr[:, SM:2 * SM]),
                inc="dbg", amt=16)
            k.e("gpsimd", lambda E: E.dma_start(out=dbg["dbg_zz"], in_=scr[:, 2 * SM:4 * SM]),
                inc="dbg", amt=16)
            k.w("gpsimd", ("dbg", k.cnt["dbg"]))

        k.w("sync", ("od0", k.cnt["od0"]))
        k.w("sync", ("od1", k.cnt["od1"]))

        with nc.Block() as block:
            @block.tensor
            def _(E):
                for f in k.plan["tensor"]:
                    f(E)

            @block.vector
            def _(E):
                for f in k.plan["vector"]:
                    f(E)

            @block.scalar
            def _(E):
                for f in k.plan["scalar"]:
                    f(E)

            @block.gpsimd
            def _(E):
                for f in k.plan["gpsimd"]:
                    f(E)

            @block.sync
            def _(E):
                for f in k.plan["sync"]:
                    f(E)

    return nc


def make_inmaps(inputs, tokens_per_core, n_cores):
    x = np.ascontiguousarray(np.asarray(inputs["L0_attn_x"], dtype=np.float32)
                             .reshape(-1, D))
    wqkv = np.ascontiguousarray(np.asarray(inputs["L0_attn_W_qkv"], np.float32))
    woutT = np.ascontiguousarray(np.asarray(inputs["L0_attn_W_out"], np.float32).T)
    bout = np.ascontiguousarray(np.asarray(inputs["L0_attn_b_out"], np.float32))
    gwT = np.ascontiguousarray(np.asarray(inputs["L0_ffn_gate_w"], np.float32).T)
    w1T = np.ascontiguousarray(np.asarray(inputs["L0_ffn_w1"], np.float32).transpose(0, 2, 1))
    w2T = np.ascontiguousarray(np.asarray(inputs["L0_ffn_w2"], np.float32).transpose(0, 2, 1))
    w3T = np.ascontiguousarray(np.asarray(inputs["L0_ffn_w3"], np.float32).transpose(0, 2, 1))
    b1T = np.ascontiguousarray(np.asarray(inputs["L0_ffn_b1"], np.float32).T)
    b2T = np.ascontiguousarray(np.asarray(inputs["L0_ffn_b2"], np.float32).T)
    b3T = np.ascontiguousarray(np.asarray(inputs["L0_ffn_b3"], np.float32).T)
    sw1T = np.ascontiguousarray(np.asarray(inputs["L0_ffn_sw1"], np.float32).T)
    sw2T = np.ascontiguousarray(np.asarray(inputs["L0_ffn_sw2"], np.float32).T)
    sw3T = np.ascontiguousarray(np.asarray(inputs["L0_ffn_sw3"], np.float32).T)
    shared = dict(wqkv=wqkv, woutT=woutT, bout=bout, gwT=gwT, w1T=w1T, w2T=w2T,
                  w3T=w3T, b1T=b1T, b2T=b2T, b3T=b3T, sw1T=sw1T, sw2T=sw2T,
                  sw3T=sw3T)
    in_maps = []
    for c in range(n_cores):
        m = dict(shared)
        m["x"] = np.ascontiguousarray(
            x[c * tokens_per_core:(c + 1) * tokens_per_core])
        in_maps.append(m)
    return in_maps


def kernel(**inputs):
    n_cores = 8
    tokens_per_core = 4096
    in_maps = make_inmaps(inputs, tokens_per_core, n_cores)
    nc = build_nc(tokens_per_core)
    res = run_bass_kernel_spmd(nc, in_maps, core_ids=list(range(n_cores)))
    out = np.concatenate([r["y"] for r in res.results], axis=0)
    return out.reshape(16, 2048, D).astype(np.float32)


# version 11
# speedup vs baseline: 1.0091x; 1.0091x over previous
import contextlib
import numpy as np

import concourse.bass as bass
import concourse.mybir as mybir
from concourse.bass import IndirectOffsetOnAxis
from concourse.bass_utils import run_bass_kernel_spmd

F32 = mybir.dt.float32
BF16 = mybir.dt.bfloat16
AX = mybir.AxisListType.X
AF = mybir.ActivationFunctionType
ALU = mybir.AluOpType

D = 1024
EXP = 8
II = 2048
SCALE = 1.0 / 32.0
NBUF = 3
CAP = 512
ENGS = ("tensor", "vector", "scalar", "sync", "gpsimd")


class K:

    def __init__(self):
        self.plan = {e: [] for e in ENGS}
        self.cnt = {}
        self.sems = {}

    def sem(self, name, stack, nc):
        self.sems[name] = stack.enter_context(nc.semaphore(name))
        self.cnt[name] = 0

    def w(self, eng, evt):
        if evt is None:
            return
        s, v = evt
        sem = self.sems[s]
        self.plan[eng].append(lambda E, sem=sem, v=v: E.wait_ge(sem, v))

    def wmax(self, eng, evts):
        by = {}
        for e in evts:
            if e is None:
                continue
            by[e[0]] = max(by.get(e[0], 0), e[1])
        for s, v in by.items():
            self.w(eng, (s, v))

    def e(self, eng, fn, inc=None, amt=1):
        if eng == "vector":
            self.w("vector", ("ev", self.cnt["ev"]))
            if inc is None:
                inc, amt = "ev", 1
        if inc is None:
            self.plan[eng].append(fn)
            return None
        self.cnt[inc] += amt
        v = self.cnt[inc]
        sem = self.sems[inc]
        self.plan[eng].append(lambda E, fn=fn, sem=sem, amt=amt: fn(E).then_inc(sem, amt))
        return (inc, v)


class Banks:

    def __init__(self, k, ap, nbanks, width):
        self.k, self.ap, self.n, self.w = k, ap, nbanks, width
        self.free = [[] for _ in range(nbanks)]
        self.rr = 0

    def acq(self, cols=None):
        i = self.rr % self.n
        self.rr += 1
        self.k.wmax("tensor", self.free[i])
        self.free[i] = []
        sl = self.ap[:, i * self.w:(i + 1) * self.w if cols is None else i * self.w + cols]
        return i, sl

    def sl(self, i, c0, c1):
        return self.ap[:, i * self.w + c0:i * self.w + c1]

    def rel(self, i, evt):
        self.free[i].append(evt)


def build_nc(tokens, debug=False, stages='ABCDEFGH', a_level=4, f_level=9, g_level=9):
    M = min(1024, tokens)
    NMT = tokens // M
    NT = M // 128
    NN = max(1, M // 512)
    NW = min(512, M)
    SM = 8 * M
    assert tokens >= 1024, "sparse MoE layout assumes M=1024 macro-tiles"

    nc = bass.Bass("TRN2", target_bir_lowering=False,
                   detect_race_conditions=False)
    k = K()

    x_d = nc.dram_tensor("x", [tokens, D], F32, kind="ExternalInput").ap()
    y_d = nc.dram_tensor("y", [tokens, D], F32, kind="ExternalOutput").ap()
    wqkv_d = nc.dram_tensor("wqkv", [D, 3 * D], F32, kind="ExternalInput").ap()
    woutT_d = nc.dram_tensor("woutT", [D, D], F32, kind="ExternalInput").ap()
    gwT_d = nc.dram_tensor("gwT", [D, EXP], F32, kind="ExternalInput").ap()
    w1T_d = nc.dram_tensor("w1T", [EXP, D, D], F32, kind="ExternalInput").ap()
    w2T_d = nc.dram_tensor("w2T", [EXP, D, D], F32, kind="ExternalInput").ap()
    w3T_d = nc.dram_tensor("w3T", [EXP, D, D], F32, kind="ExternalInput").ap()
    sw1T_d = nc.dram_tensor("sw1T", [D, II], F32, kind="ExternalInput").ap()
    sw2T_d = nc.dram_tensor("sw2T", [D, II], F32, kind="ExternalInput").ap()
    sw3T_d = nc.dram_tensor("sw3T", [II, D], F32, kind="ExternalInput").ap()
    b1T_d = nc.dram_tensor("b1T", [D, EXP], F32, kind="ExternalInput").ap()
    b2T_d = nc.dram_tensor("b2T", [D, EXP], F32, kind="ExternalInput").ap()
    b3T_d = nc.dram_tensor("b3T", [D, EXP], F32, kind="ExternalInput").ap()
    bout_d = nc.dram_tensor("bout", [D], F32, kind="ExternalInput").ap()
    xt_d = nc.dram_tensor("xt_scr", [tokens, D], BF16, kind="ExternalOutput").ap()
    yt_d = nc.dram_tensor("yt_scr", [2 * tokens, D], BF16, kind="ExternalOutput").ap()
    dbg = {}
    if debug:
        for nm, shp in [("dbg_xnf", [128, SM]), ("dbg_logits", [128, NT * 8]),
                        ("dbg_srow", [128, 8 * NT]), ("dbg_qf", [128, SM]),
                        ("dbg_aof", [128, SM]), ("dbg_acc", [128, SM]),
                        ("dbg_hsb", [128, SM]), ("dbg_zz", [128, 16 * M])]:
            dbg[nm] = nc.dram_tensor(nm, shp, F32, kind="ExternalOutput").ap()

    with contextlib.ExitStack() as st:
        def sb(name, shape, dt):
            return st.enter_context(nc.sbuf_tensor(name, shape, dt))

        def ps(name, shape, dt):
            return st.enter_context(nc.psum_tensor(name, shape, dt))

        scr = sb("scr", [128, 32 * M], BF16)
        qf = scr[:, 0:SM]
        kf = scr[:, SM:2 * SM]
        vtok = scr[:, 2 * SM:2 * SM + NT * 1024]
        aof = scr[:, 3 * SM:4 * SM]
        sxn_g = scr[:, 0:8 * CAP]
        h_g = scr[:, 8 * CAP:16 * CAP]
        p_mx = scr[:, 16 * CAP:16 * CAP + NT * CAP]
        xng_tok = scr[:, 24 * CAP:32 * CAP]
        sxng_tok = scr[:, 32 * CAP:40 * CAP]
        prodtok = scr[:, 40 * CAP:48 * CAP]
        prodtile = scr[:, 48 * CAP:49 * CAP]
        zero_zone = scr[:, 0:4096]
        h_sb = scr[:, SM:2 * SM]
        zz = scr[:, 2 * SM:4 * SM]
        xn_f = sb("xn_f", [128, SM], BF16)
        acc = sb("acc", [128, SM], F32)
        wsl = [sb(f"wsl{i}", [128, 8 * 512], BF16) for i in range(NBUF)]
        xa = [sb(f"xa{i}", [128, D], F32) for i in range(2)]
        xfin = [sb(f"xfin{i}", [128, D], F32) for i in range(2)]
        otok = [sb(f"otok{i}", [128, D], F32) for i in range(2)]
        xn32 = sb("xn32", [128, D], F32)
        xntok = [sb(f"xntok{i}", [128, D], F32) for i in range(2)]
        s_tokall = sb("s_tokall", [128, 8 * NT], F32)
        slot_tokall = sb("slot_tokall", [128, 8 * NT], F32)
        m_tokall = sb("m_tokall", [128, 8 * NT], F32)
        r_tokall = sb("r_tokall", [128, 8 * NT], F32)
        m_em8 = sb("m_em8", [8, M], BF16)
        rk_em8 = sb("rk_em8", [8, M], F32)
        iotaC = sb("iotaC", [128, CAP], F32)
        idf8 = sb("idf8", [128, 8], mybir.dt.int32)
        idlo_i = sb("idlo_i", [128, 8], mybir.dt.int32)
        c31 = sb("c31", [128, 1], mybir.dt.int32)
        idhi = sb("idhi", [128, 8], BF16)
        idlo = sb("idlo", [128, 8], BF16)
        meta = sb("meta", [128, 32], BF16)
        meta32 = sb("meta32", [4, CAP], F32)
        idx_all = sb("idx_all", [128, 8 * (CAP // 128)], mybir.dt.int32)
        ixs_all = sb("ixs_all", [128, 8 * (CAP // 128)], mybir.dt.int32)
        sg_all = sb("sg_all", [128, 8 * (CAP // 128)], F32)
        ytk_w = [sb(f"ytkw{i}", [128, 2 * D], BF16) for i in range(2)]
        idxf = sb("idxf", [128, 4], F32)
        ids1f = sb("ids1f", [128, 4], F32)
        eqf = sb("eqf", [128, 4], F32)
        mtt = sb("mtt", [128, 16], F32)
        logits = sb("logits", [128, NT * 8], F32)
        sm32 = sb("sm32", [128, 128], F32)
        p32 = sb("p32", [128, 128], F32)
        pn_sb = sb("pn_sb", [128, 128], BF16)
        pt_sb = sb("pt_sb", [128, 256], BF16)
        tmp3 = [sb(f"tmp3{i}", [128, 512], BF16) for i in range(2)]
        tmp2 = [sb(f"tmp2{i}", [128, 512], BF16) for i in range(2)]
        z1t = [sb(f"z1t{i}", [128, 512], BF16) for i in range(2)]
        gw_sb = sb("gw_sb", [128, 8 * EXP], F32)
        b1_sb = sb("b1_sb", [128, 8 * EXP], F32)
        b2_sb = sb("b2_sb", [128, 8 * EXP], F32)
        b3_sb = sb("b3_sb", [128, 8 * EXP], F32)
        bout_sb = sb("bout_sb", [128, 8], F32)
        identf = sb("identf", [128, 128], F32)
        identb = sb("identb", [128, 128], BF16)
        amask = sb("amask", [128, 128], F32)

        zerocol = sb("zerocol", [128, 1], F32)
        colp = sb("colp", [128, 20], F32)
        (CSX, CMEAN, CSXX, CEX2, CM2, CVAR, CSRT, CY0, CT1, CT2, CT3, CRSTD,
         CNRM, CRSUM, CRDEN, C32, CHALF, CT4) = range(18)

        def cp(i):
            return colp[:, i:i + 1]

        pmm = ps("pmm", [128, 4 * 512], F32)
        ptr = ps("ptr", [128, 512], F32)
        pbt = ps("pbt", [128, 2048], BF16)
        psml = ps("psml", [128, 512], F32)

        for s in ("wd0", "wd1", "wd2", "ws", "xa0", "xa1", "xf0",
                  "xf1", "od0", "od1", "dbg", "mm", "ev", "sc", "gp",
                  "xt0", "xt1", "gd", "sd", "zd", "yd0", "yd1"):
            k.sem(s, st, nc)

        mmB = Banks(k, pmm, 4, 512)
        trB = Banks(k, ptr, 1, 512)
        btB = Banks(k, pbt, 2, 1024)

        e1 = k.e("gpsimd", lambda E: E.memset(identf[:], 1.0), inc="gp")
        k.w("gpsimd", e1)
        e_idf = k.e("gpsimd", lambda E: E.affine_select(
            identf[:], identf[:], [[-1, 128]], ALU.is_equal, 0.0,
            base=0, channel_multiplier=1), inc="gp")
        e2 = k.e("gpsimd", lambda E: E.memset(amask[:], 0.0), inc="gp")
        k.w("gpsimd", e2)
        e_msk = k.e("gpsimd", lambda E: E.affine_select(
            amask[:], amask[:], [[-1, 128]], ALU.is_ge, -1.0e30,
            base=0, channel_multiplier=1), inc="gp")
        e_ioc = k.e("gpsimd", lambda E: E.iota(
            iotaC[:], [[1, CAP]], base=0, channel_multiplier=0,
            allow_small_or_imprecise_dtypes=True), inc="gp")
        e_idf8 = k.e("gpsimd", lambda E: E.iota(
            idf8[:], [[128, 8]], base=1, channel_multiplier=1), inc="gp")
        k.w("vector", e_idf)
        e_idb = k.e("vector", lambda E: E.tensor_copy(identb[:], identf[:]), inc="ev")
        k.e("vector", lambda E: E.memset(zerocol[:], 0.0))
        k.e("vector", lambda E: E.memset(cp(C32), 32.0))
        k.e("vector", lambda E: E.memset(cp(CHALF), 0.5))
        k.w("vector", e_idf8)
        k.e("vector", lambda E: E.memset(c31[:], 31))
        k.e("vector", lambda E: E.tensor_single_scalar(
            idlo_i[:], idf8[:], c31[:, 0:1], ALU.bitwise_and))
        k.e("vector", lambda E: E.tensor_copy(m_tokall[:, 0:8], idf8[:]))
        k.e("vector", lambda E: E.tensor_copy(m_tokall[:, 8:16], idlo_i[:]))
        k.e("vector", lambda E: E.tensor_copy(idlo[:], idlo_i[:]))
        k.e("vector", lambda E: E.tensor_sub(
            m_tokall[:, 0:8], m_tokall[:, 0:8], m_tokall[:, 8:16]))
        k.e("vector", lambda E: E.tensor_scalar_mul(idhi[:], m_tokall[:, 0:8], 1.0 / 32.0))
        k.e("vector", lambda E: E.tensor_copy(
            meta[:].rearrange("p (t k) -> p t k", k=4)[:, 0:8, 0:1],
            idhi[:, 0:8].rearrange("p (t u) -> p t u", u=1)))
        k.e("vector", lambda E: E.tensor_copy(
            meta[:].rearrange("p (t k) -> p t k", k=4)[:, 0:8, 1:2],
            idlo[:, 0:8].rearrange("p (t u) -> p t u", u=1)))


        def sdma(dst, src):
            return k.e("gpsimd", lambda E, d=dst, s=src: E.dma_start(
                out=d, in_=s, allow_slow_non_contiguous=True), inc="ws", amt=16)

        e_gw = sdma(gw_sb[:].rearrange("p (c e) -> p c e", c=8),
                    gwT_d.rearrange("(c p) e -> p c e", p=128))
        e_b1 = sdma(b1_sb[:].rearrange("p (t e) -> p t e", t=8),
                    b1T_d.rearrange("(t p) e -> p t e", p=128))
        e_b2 = sdma(b2_sb[:].rearrange("p (t e) -> p t e", t=8),
                    b2T_d.rearrange("(t p) e -> p t e", p=128))
        e_b3 = sdma(b3_sb[:].rearrange("p (t e) -> p t e", t=8),
                    b3T_d.rearrange("(t p) e -> p t e", p=128))
        e_bo = sdma(bout_sb[:], bout_d.rearrange("(t p) -> p t", p=128))
        e_gw = e_b1 = e_b2 = e_b3 = e_bo = ("ws", k.cnt["ws"])

        e_zz0 = k.e("vector", lambda E: E.memset(zero_zone, 0.0))
        k.w("sync", e_zz0)
        for kk in range(2 * tokens // 512):
            k.e("sync", lambda E, kk=kk: E.dma_start(
                out=yt_d.rearrange("(k g p) n -> k p g n", k=2 * tokens // 512, p=128)[kk],
                in_=zero_zone.rearrange("p (g n) -> p g n", g=4)), inc="zd", amt=16)
        for kk in range(tokens // 512):
            k.e("sync", lambda E, kk=kk: E.dma_start(
                out=xt_d.rearrange("(k g p) n -> k p g n", k=tokens // 512, p=128)[kk],
                in_=zero_zone.rearrange("p (g n) -> p g n", g=4)), inc="zd", amt=16)
        e_ztot = ("zd", k.cnt["zd"])

        bc_reg = nc.alloc_register(mybir.EngineType.Pool, "bc_reg")
        k.e("gpsimd", lambda E: nc.reg_mov(bc_reg, tokens - 1))
        bc2_reg = nc.alloc_register(mybir.EngineType.Pool, "bc2_reg")
        k.e("gpsimd", lambda E: nc.reg_mov(bc2_reg, 2 * tokens - 1))

        class Slabs:
            def __init__(self):
                self.kk = 0
                self.free_evt = [None] * NBUF

            def load(self, src3d):
                buf = self.kk % NBUF
                k.w("gpsimd", self.free_evt[buf])
                evt = k.e("gpsimd", lambda E, a=src3d, b=buf: E.dma_start(
                    out=wsl[b][:].rearrange("p (c n) -> p c n", c=8), in_=a),
                    inc=f"wd{buf}", amt=16)
                self.kk += 1
                return buf, evt

        slabs = Slabs()

        def w2d(ap2d, c0):
            return ap2d[:, c0:c0 + 512].rearrange("(c p) n -> p c n", p=128)

        def w3d(ap3d, e, c0):
            return ap3d[e, :, c0:c0 + 512].rearrange("(c p) n -> p c n", p=128)

        def wrow(ap2d, r0, c0):
            return ap2d[r0:r0 + 1024, c0:c0 + 512].rearrange("(c p) n -> p c n", p=128)

        st_ev = {
            "G_mm_end": None,
            "H_mm_end": None,
            "psml_free": None,
            "attn_mm_end": None,
            "D_mm_end": None,
            "xa_free": [None, None],
            "xfin_free": [None, None],
            "otok_store": [None, None],
            "xntok_free": [None, None],
            "sxn_rd_end": None,
            "sxn_rd_end2": [None, None],
            "tmp_free": [None, None],
            "z1t_free": [None, None],
            "gate_prev": None,
            "xt_evts": [],
            "meta_mm_prev": None,
            "routing_done": None,
            "gd_prev": None,
            "tbl_evt": {},
            "xng_free": None,
            "sxng_rd_end": None,
            "sd_prev": None,
        }

        n_store = [0]

        def stage_A(mt):
            xnf_done = None
            x32evts = []
            xt_evts = st_ev["xt_evts"]
            xt_evts.clear()
            for tt in range(NT):
                row0 = mt * M + tt * 128
                slot = tt % 2
                k.w("sync", st_ev["xa_free"][slot])
                ld = k.e("sync", lambda E, r=row0, s=slot: E.dma_start(
                    out=xa[s][:], in_=x_d[r:r + 128, :]), inc=f"xa{slot}", amt=16)
                k.w("scalar", ld)
                if st_ev["gate_prev"] is not None:
                    k.w("scalar", st_ev["gate_prev"])
                e_sq = k.e("scalar", lambda E, s=slot: E.activation(
                    xn32[:], xa[s][:], AF.Square, accum_out=cp(CSXX)), inc="sc")
                k.w("vector", ld)
                k.e("vector", lambda E, s=slot: E.reduce_sum(cp(CSX), xa[s][:], axis=AX))
                k.e("vector", lambda E: E.tensor_scalar_mul(cp(CMEAN), cp(CSX), 1.0 / D))
                k.w("vector", e_sq)
                k.e("vector", lambda E: E.tensor_scalar_mul(cp(CEX2), cp(CSXX), 1.0 / D))
                k.e("vector", lambda E: E.tensor_mul(cp(CM2), cp(CMEAN), cp(CMEAN)))
                k.e("vector", lambda E: E.tensor_sub(cp(CVAR), cp(CEX2), cp(CM2)))
                e_var = k.e("vector", lambda E: E.tensor_scalar_add(
                    cp(CT1), cp(CVAR), 1e-5), inc="ev")
                k.w("scalar", e_var)
                e_srt = k.e("scalar", lambda E: E.activation(
                    cp(CSRT), cp(CT1), AF.Sqrt), inc="sc")
                k.w("vector", e_srt)
                e_rc = k.e("vector", lambda E: E.reciprocal(cp(CY0), cp(CSRT)),
                           inc="ev")
                k.w("vector", e_rc)
                k.e("vector", lambda E: E.tensor_mul(cp(CT2), cp(CY0), cp(CY0)))
                k.e("vector", lambda E: E.tensor_mul(cp(CT2), cp(CT2), cp(CT1)))
                k.e("vector", lambda E: E.tensor_scalar(cp(CT3), cp(CT2), -0.5, 1.5, ALU.mult, ALU.add))
                k.e("vector", lambda E: E.tensor_mul(cp(CRSTD), cp(CY0), cp(CT3)))
                k.wmax("vector", st_ev["xntok_free"][slot] or [])
                e_xn = k.e("vector", lambda E, s=slot: E.tensor_scalar(
                    xntok[s][:], xa[s][:], cp(CMEAN), cp(CRSTD),
                    ALU.subtract, ALU.mult), inc="ev")
                st_ev["xa_free"][slot] = e_xn
                k.w("gpsimd", e_xn)
                if mt == 0 and tt == 0:
                    k.w("gpsimd", e_ztot)
                e_xt = k.e("gpsimd", lambda E, r=row0, s=slot: E.dma_start(
                    out=xt_d[r:r + 128, :], in_=xntok[s][:]),
                    inc=f"xt{slot}", amt=16)
                xt_evts.append(e_xt)
                if a_level < 2:
                    continue
                k.w("tensor", e_xn)
                if mt == 0 and tt == 0:
                    k.w("tensor", e_idf)
                ev_l = None
                for half in range(2):
                    bi, _ = trB.acq()
                    for cc in range(4):
                        c = half * 4 + cc
                        fn = (lambda E, s=slot, c=c, bi=bi, cc=cc: E.transpose(
                            trB.sl(bi, cc * 128, cc * 128 + 128),
                            xntok[s][:, c * 128:(c + 1) * 128], identf[:]))
                        ev = k.e("tensor", fn, inc="mm" if cc == 3 else None)
                    ev_l = ev
                    if a_level < 3:
                        trB.rel(bi, ev)
                        continue
                    k.w("vector", ev)
                    if half == 0 and st_ev["gate_prev"] is not None:
                        k.w("vector", st_ev["gate_prev"])
                        k.w("vector", xnf_done)
                    e_v = k.e("vector", lambda E, bi=bi, h=half: E.tensor_copy(
                        xn32[:, h * 512:(h + 1) * 512], trB.sl(bi, 0, 512)), inc="ev")
                    k.w("scalar", e_v)
                    if tt == 0 and half == 0 and st_ev["G_mm_end"] is not None:
                        k.w("scalar", st_ev["G_mm_end"])
                    e_s = k.e("scalar", lambda E, h=half, tt=tt: E.activation(
                        xn_f[:, 0:SM].rearrange("p (c t) -> p c t", c=8)
                        [:, h * 4:(h + 1) * 4, tt * 128:(tt + 1) * 128],
                        xn32[:, h * 512:(h + 1) * 512].rearrange("p (c t) -> p c t", c=4),
                        AF.Copy), inc="sc")
                    trB.rel(bi, e_v)
                    x32evts.append(e_v)
                    xnf_done = e_s
                st_ev["xntok_free"][slot] = [ev_l, xt_evts[-1]]
                if a_level < 3:
                    continue
                if a_level < 4:
                    continue
                k.wmax("tensor", x32evts[-2:])
                if mt == 0 and tt == 0:
                    k.w("tensor", e_gw)
                if st_ev["psml_free"] is not None:
                    k.w("tensor", st_ev["psml_free"])
                for c in range(8):
                    ev = k.e("tensor", lambda E, c=c: E.matmul(
                        psml[:, 0:8], xn32[:, c * 128:(c + 1) * 128],
                        gw_sb[:, c * 8:(c + 1) * 8],
                        start=(c == 0), stop=(c == 7)),
                        inc="mm" if c == 7 else None)
                k.w("vector", ev)
                e_lg = k.e("vector", lambda E, tt=tt: E.tensor_copy(
                    logits[:, tt * 8:(tt + 1) * 8], psml[:, 0:8]), inc="ev")
                st_ev["psml_free"] = e_lg
                st_ev["gate_prev"] = ev
            return xnf_done

        def preload_B(mt):
            return [slabs.load(w2d(wqkv_d, j * 512)) for j in range(3)]

        def stage_B(mt, xnf_done, pre):
            qk_done = None
            v_done = None
            for j in range(6):
                buf, rdy = pre[j] if j < len(pre) else slabs.load(w2d(wqkv_d, j * 512))
                k.w("tensor", rdy)
                if j == 0:
                    k.w("tensor", xnf_done)
                last_mm = None
                if j < 4:
                    dest = qf if j < 2 else kf
                    base_oc = (j % 2) * 4
                    for o in range(4):
                        for n in range(NN):
                            bi, bsl = mmB.acq(cols=NW)
                            for c in range(8):
                                ev = k.e("tensor", lambda E, b=buf, c=c, o=o, n=n, bsl=bsl: E.matmul(
                                    bsl, wsl[b][:, c * 512 + o * 128:c * 512 + (o + 1) * 128],
                                    xn_f[:, c * M + n * 512:c * M + n * 512 + NW],
                                    start=(c == 0), stop=(c == 7)),
                                    inc="mm" if c == 7 else None)
                            last_mm = ev
                            k.w("vector", ev)
                            if mt == 0 and j == 0 and o == 0 and n == 0:
                                k.w("vector", e_ztot)
                            oc = base_oc + o
                            e_v = k.e("vector", lambda E, bi=bi, oc=oc, n=n, d=dest: E.tensor_copy(
                                d[:, oc * M + n * 512:oc * M + n * 512 + NW],
                                mmB.sl(bi, 0, NW)), inc="ev")
                            mmB.rel(bi, e_v)
                            qk_done = e_v
                else:
                    for tt in range(NT):
                        bi, bsl = mmB.acq()
                        for c in range(8):
                            ev = k.e("tensor", lambda E, b=buf, c=c, tt=tt, bsl=bsl: E.matmul(
                                bsl, xn_f[:, c * M + tt * 128:c * M + (tt + 1) * 128],
                                wsl[b][:, c * 512:(c + 1) * 512],
                                start=(c == 0), stop=(c == 7)),
                                inc="mm" if c == 7 else None)
                        last_mm = ev
                        k.w("vector", ev)
                        if j == 4 and tt == 0 and st_ev["sd_prev"] is not None:
                            k.w("vector", st_ev["sd_prev"])
                        e_v = k.e("vector", lambda E, bi=bi, tt=tt, j=j: E.tensor_copy(
                            vtok[:, tt * 1024 + (j - 4) * 512:tt * 1024 + (j - 4) * 512 + 512],
                            mmB.sl(bi, 0, 512)), inc="ev")
                        mmB.rel(bi, e_v)
                        v_done = e_v
                slabs.free_evt[buf] = last_mm
            return qk_done, v_done

        def stage_C(mt, qk_done, v_done):
            aof_done = None
            attn_end = None
            for b in range(NT):
                bi, _ = mmB.acq(cols=128)
                if b == 0:
                    k.w("tensor", qk_done)
                for c in range(8):
                    ev = k.e("tensor", lambda E, c=c, b=b, bi=bi: E.matmul(
                        mmB.sl(bi, 0, 128),
                        qf[:, c * M + b * 128:c * M + (b + 1) * 128],
                        kf[:, c * M + b * 128:c * M + (b + 1) * 128],
                        start=(c == 0), stop=(c == 7)),
                        inc="mm" if c == 7 else None)
                attn_end = ev
                k.w("vector", ev)
                if b == 0 and mt == 0:
                    k.w("vector", (("gp", k.cnt["gp"])))
                k.e("vector", lambda E, bi=bi: E.scalar_tensor_tensor(
                    sm32[:], mmB.sl(bi, 0, 128), SCALE, amask[:],
                    ALU.mult, ALU.add))
                e_nm = k.e("vector", lambda E: E.reduce_max(
                    cp(CNRM), sm32[:], axis=AX, negate=True), inc="ev")
                mmB.rel(bi, e_nm)
                k.w("scalar", e_nm)
                e_p = k.e("scalar", lambda E: E.activation(
                    p32[:], sm32[:], AF.Exp, bias=cp(CNRM),
                    accum_out=cp(CRSUM)), inc="sc")
                k.w("vector", e_p)
                k.e("vector", lambda E: E.tensor_scalar_add(cp(CRDEN), cp(CRSUM), 1e-6))
                e_rc = k.e("vector", lambda E: E.reciprocal(cp(CRDEN), cp(CRDEN)),
                           inc="ev")
                k.w("vector", e_rc)
                e_pn = k.e("vector", lambda E: E.tensor_scalar_mul(
                    pn_sb[:], p32[:], cp(CRDEN)), inc="ev")
                sloti = b % 2
                k.w("tensor", e_pn)
                if mt == 0 and b < 2:
                    k.w("tensor", e_idb)
                k.wmax("tensor", btB.free[sloti])
                btB.free[sloti] = []
                e_tp = k.e("tensor", lambda E, s=sloti: E.transpose(
                    pbt[:, s * 1024:s * 1024 + 128], pn_sb[:], identb[:]), inc="mm")
                k.w("vector", e_tp)
                e_pt = k.e("vector", lambda E, s=sloti: E.tensor_copy(
                    pt_sb[:, s * 128:(s + 1) * 128], pbt[:, s * 1024:s * 1024 + 128]),
                    inc="ev")
                btB.free[sloti].append(e_pt)
                k.w("tensor", e_pt)
                if b == 0:
                    k.w("tensor", v_done)
                for half in range(2):
                    bi2, _ = mmB.acq()
                    for dt in range(4):
                        ev = k.e("tensor", lambda E, bi2=bi2, dt=dt, half=half, b=b, s=sloti: E.matmul(
                            mmB.sl(bi2, dt * 128, (dt + 1) * 128),
                            vtok[:, b * 1024 + (half * 4 + dt) * 128:b * 1024 + (half * 4 + dt + 1) * 128],
                            pt_sb[:, s * 128:(s + 1) * 128],
                            start=True, stop=True),
                            inc="mm" if dt == 3 else None)
                    attn_end = ev
                    k.w("scalar", ev)
                    e_a = k.e("scalar", lambda E, bi2=bi2, half=half, b=b: E.activation(
                        aof[:, 0:SM].rearrange("p (c t) -> p c t", c=8)
                        [:, half * 4:(half + 1) * 4, b * 128:(b + 1) * 128],
                        mmB.sl(bi2, 0, 512).rearrange("p (c t) -> p c t", c=4),
                        AF.Copy), inc="sc")
                    mmB.rel(bi2, e_a)
                    aof_done = e_a
            st_ev["attn_mm_end"] = attn_end
            return aof_done

        def table_build(mt, e):
            NG = CAP // 128
            if e == 0:
                k.w("vector", st_ev["routing_done"])
                k.w("vector", st_ev["attn_mm_end"])
                if st_ev["sd_prev"] is not None:
                    k.w("vector", st_ev["sd_prev"])
                if st_ev["gd_prev"] is not None:
                    k.w("vector", st_ev["gd_prev"])
            else:
                k.w("vector", st_ev["meta_mm_prev"])
            for tc in range(NT):
                e_pm = k.e("vector", lambda E, tc=tc, e=e: E.tensor_scalar(
                    p_mx[:, tc * CAP:(tc + 1) * CAP], iotaC[:, 0:CAP],
                    r_tokall[:, tc * 8 + e:tc * 8 + e + 1],
                    m_tokall[:, tc * 8 + e:tc * 8 + e + 1],
                    ALU.is_equal, ALU.mult))
            k.e("vector", lambda E, e=e: E.tensor_copy(
                meta[:].rearrange("p (t k) -> p t k", k=4)[:, 0:NT, 2:3],
                s_tokall[:].rearrange("p (t q) -> p t q", q=8)[:, 0:NT, e:e + 1]))
            e_meta = k.e("vector", lambda E, e=e: E.tensor_copy(
                meta[:].rearrange("p (t k) -> p t k", k=4)[:, 0:NT, 3:4],
                slot_tokall[:].rearrange("p (t q) -> p t q", q=8)[:, 0:NT, e:e + 1]))
            k.w("tensor", e_pm)
            k.w("tensor", e_meta)
            if st_ev["psml_free"] is not None:
                k.w("tensor", st_ev["psml_free"])
            for tc in range(NT):
                mev = k.e("tensor", lambda E, tc=tc: E.matmul(
                    psml[0:4, 0:CAP], meta[:, tc * 4:(tc + 1) * 4],
                    p_mx[:, tc * CAP:(tc + 1) * CAP],
                    start=(tc == 0), stop=(tc == NT - 1)),
                    inc="mm" if tc == NT - 1 else None)
            st_ev["meta_mm_prev"] = mev
            k.w("vector", mev)
            e_m32 = k.e("vector", lambda E: E.tensor_copy(
                meta32[0:4, 0:CAP], psml[0:4, 0:CAP]))
            k.w("tensor", e_m32)
            for g in range(NG):
                tev = k.e("tensor", lambda E, g=g: E.transpose(
                    psml[0:128, g * 4:(g + 1) * 4],
                    meta32[0:4, g * 128:(g + 1) * 128],
                    identf[0:4, 0:4]), inc="mm" if g == NG - 1 else None)
            k.w("vector", tev)
            k.e("vector", lambda E: E.tensor_copy(
                mtt[:, 0:4 * NG], psml[:, 0:4 * NG]))
            k.e("vector", lambda E: E.scalar_tensor_tensor(
                ids1f[:, 0:NG], mtt[:, 0:4 * NG:4], 32.0,
                mtt[:, 1:4 * NG:4], ALU.mult, ALU.add))
            k.e("vector", lambda E: E.tensor_single_scalar(
                eqf[:, 0:NG], ids1f[:, 0:NG], cp(CHALF), ALU.is_lt))
            k.e("vector", lambda E, mt=mt: E.tensor_scalar_add(
                idxf[:, 0:NG], ids1f[:, 0:NG], float(mt * M - 1)))
            k.e("vector", lambda E: E.scalar_tensor_tensor(
                idxf[:, 0:NG], eqf[:, 0:NG], float(tokens + 1), idxf[:, 0:NG],
                ALU.mult, ALU.add))
            e_idx = k.e("vector", lambda E, e=e: E.tensor_copy(
                idx_all[:, e * NG:(e + 1) * NG], idxf[:, 0:NG]))
            k.e("vector", lambda E: E.scalar_tensor_tensor(
                idxf[:, 0:NG], ids1f[:, 0:NG], 2.0, mtt[:, 3:4 * NG:4],
                ALU.mult, ALU.add))
            k.e("vector", lambda E, mt=mt: E.tensor_scalar_add(
                idxf[:, 0:NG], idxf[:, 0:NG], float(2 * mt * M - 2)))
            k.e("vector", lambda E: E.scalar_tensor_tensor(
                idxf[:, 0:NG], eqf[:, 0:NG], float(2 * tokens + 4), idxf[:, 0:NG],
                ALU.mult, ALU.add))
            k.e("vector", lambda E, e=e: E.tensor_copy(
                ixs_all[:, e * NG:(e + 1) * NG], idxf[:, 0:NG]))
            e_sg = k.e("vector", lambda E, e=e: E.tensor_copy(
                sg_all[:, e * NG:(e + 1) * NG], mtt[:, 2:4 * NG:4]))
            st_ev["psml_free"] = e_sg
            st_ev["tbl_evt"][e] = e_sg

        def stage_D(mt, aof_done):
            D_end = None
            acc_evts = []
            it = [0]
            for j in range(2):
                buf, rdy = slabs.load(w2d(woutT_d, j * 512))
                k.w("tensor", rdy)
                if j == 0:
                    k.w("tensor", aof_done)
                last_mm = None
                for o in range(4):
                    for n in range(NN):
                        bi, bsl = mmB.acq(cols=NW)
                        for c in range(8):
                            ev = k.e("tensor", lambda E, b=buf, c=c, o=o, n=n, bsl=bsl: E.matmul(
                                bsl, wsl[b][:, c * 512 + o * 128:c * 512 + (o + 1) * 128],
                                aof[:, c * M + n * 512:c * M + n * 512 + NW],
                                start=(c == 0), stop=(c == 7)),
                                inc="mm" if c == 7 else None)
                        last_mm = ev
                        D_end = ev
                        oc = j * 4 + o
                        k.w("scalar", ev)
                        if j == 0 and o == 0 and n == 0:
                            if st_ev["H_mm_end"] is not None:
                                k.w("scalar", st_ev["H_mm_end"])
                            if mt == 0:
                                k.w("scalar", e_bo)
                        e_s = k.e("scalar", lambda E, bi=bi, oc=oc, n=n: E.activation(
                            acc[:, oc * M + n * 512:oc * M + n * 512 + NW],
                            mmB.sl(bi, 0, NW), AF.Identity,
                            bias=bout_sb[:, oc:oc + 1]), inc="sc")
                        mmB.rel(bi, e_s)
                        acc_evts.append(e_s)
                        if it[0] % 2 == 0 and it[0] // 2 < EXP:
                            table_build(mt, it[0] // 2)
                        it[0] += 1
                slabs.free_evt[buf] = last_mm
            st_ev["D_mm_end"] = D_end
            return acc_evts[-1]

        def stage_E(mt):
            srow_done = None
            for tt in range(NT):
                slot = tt % 2
                e_nm = k.e("vector", lambda E, tt=tt: E.reduce_max(
                    cp(CNRM), logits[:, tt * 8:(tt + 1) * 8], axis=AX, negate=True),
                    inc="ev")
                k.w("scalar", e_nm)
                e_p = k.e("scalar", lambda E, tt=tt: E.activation(
                    p32[:, 0:8],
                    logits[:, tt * 8:(tt + 1) * 8], AF.Exp, bias=cp(CNRM),
                    accum_out=cp(CRSUM)), inc="sc")
                k.w("vector", e_p)
                e_rc = k.e("vector", lambda E: E.reciprocal(cp(CRDEN), cp(CRSUM)),
                           inc="ev")
                k.w("vector", e_rc)
                k.e("vector", lambda E: E.tensor_scalar_mul(
                    sm32[:, 0:8], p32[:, 0:8], cp(CRDEN)))
                k.e("vector", lambda E: E.reduce_max(cp(CT1), sm32[:, 0:8], axis=AX))
                k.e("vector", lambda E: E.tensor_single_scalar(
                    sm32[:, 8:16], sm32[:, 0:8], cp(CT1), ALU.is_ge))
                k.e("vector", lambda E: E.scalar_tensor_tensor(
                    sm32[:, 16:24], sm32[:, 8:16], -1.0e30, sm32[:, 0:8],
                    ALU.mult, ALU.add))
                k.e("vector", lambda E: E.reduce_max(cp(CT2), sm32[:, 16:24], axis=AX))
                k.e("vector", lambda E: E.tensor_single_scalar(
                    sm32[:, 8:16], sm32[:, 0:8], cp(CT2), ALU.is_ge))
                e_s = k.e("vector", lambda E, tt=tt: E.tensor_mul(
                    s_tokall[:, tt * 8:tt * 8 + 8], sm32[:, 0:8], sm32[:, 8:16]))
                e_m = k.e("vector", lambda E, tt=tt: E.tensor_single_scalar(
                    m_tokall[:, tt * 8:tt * 8 + 8],
                    s_tokall[:, tt * 8:tt * 8 + 8],
                    zerocol[:, 0:1], ALU.is_gt))
                k.e("vector", lambda E, tt=tt: E.tensor_tensor_scan(
                    slot_tokall[:, tt * 8:tt * 8 + 8],
                    m_tokall[:, tt * 8:tt * 8 + 8],
                    m_tokall[:, tt * 8:tt * 8 + 8], 0.0, ALU.add, ALU.bypass))
                k.e("vector", lambda E, tt=tt: E.tensor_sub(
                    slot_tokall[:, tt * 8:tt * 8 + 8],
                    slot_tokall[:, tt * 8:tt * 8 + 8],
                    m_tokall[:, tt * 8:tt * 8 + 8]))
                k.w("tensor", e_m)
                if st_ev["psml_free"] is not None:
                    k.w("tensor", st_ev["psml_free"])
                ev = k.e("tensor", lambda E, tt=tt: E.transpose(
                    psml[0:8, 0:128], m_tokall[:, tt * 8:tt * 8 + 8],
                    identf[:]), inc="mm")
                k.w("vector", ev)
                e_r = k.e("vector", lambda E, tt=tt: E.tensor_copy(
                    m_em8[0:8, tt * 128:(tt + 1) * 128], psml[0:8, 0:128]))
                st_ev["psml_free"] = e_r
            k.e("vector", lambda E: E.tensor_tensor_scan(
                rk_em8[0:8, :], m_em8[0:8, :], m_em8[0:8, :], 0.0,
                ALU.add, ALU.bypass))
            e_r0 = k.e("vector", lambda E: E.tensor_sub(
                rk_em8[0:8, :], rk_em8[0:8, :], m_em8[0:8, :]))
            rt_done = None
            for tt in range(NT):
                k.w("tensor", e_r0)
                k.w("tensor", st_ev["psml_free"])
                ev = k.e("tensor", lambda E, tt=tt: E.transpose(
                    psml[0:128, 0:8], rk_em8[0:8, tt * 128:(tt + 1) * 128],
                    identf[0:8, 0:8]), inc="mm")
                k.w("vector", ev)
                rt_done = k.e("vector", lambda E, tt=tt: E.tensor_copy(
                    r_tokall[:, tt * 8:(tt + 1) * 8], psml[0:128, 0:8]))
                st_ev["psml_free"] = rt_done
            return rt_done

        def stage_F(mt, routing_done, xnf_done):
            NG = CAP // 128
            sd_after = {}
            sxn_gbuf = [sxn_g, scr[:, 49 * CAP:57 * CAP]]

            def prologue1(e):
                k.w("gpsimd", st_ev["tbl_evt"][e])
                k.wmax("gpsimd", st_ev["xt_evts"])
                if st_ev["xng_free"] is not None:
                    k.w("gpsimd", st_ev["xng_free"])
                if e == 0:
                    k.w("gpsimd", st_ev["attn_mm_end"])
                for g in range(NG):
                    k.e("gpsimd", lambda E, g=g, e=e: E.indirect_dma_start(
                        out=xng_tok[:, g * 1024:(g + 1) * 1024], out_offset=None,
                        in_=xt_d,
                        in_offset=IndirectOffsetOnAxis(
                            ap=idx_all[:, e * NG + g:e * NG + g + 1], axis=0),
                        bounds_check=bc_reg, oob_is_err=False), inc="gd", amt=16)
                return ("gd", k.cnt["gd"])

            def prologue2(e, gd_tot):
                slot = e % 2
                sxn = sxn_gbuf[slot]
                k.w("vector", gd_tot)
                if st_ev["sxng_rd_end"] is not None:
                    k.w("vector", st_ev["sxng_rd_end"])
                if e == 0:
                    k.w("vector", st_ev["G_mm_end"])
                e_sc = None
                for g in range(NG):
                    e_sc = k.e("vector", lambda E, g=g, e=e: E.tensor_scalar_mul(
                        sxng_tok[:, g * 1024:(g + 1) * 1024],
                        xng_tok[:, g * 1024:(g + 1) * 1024],
                        sg_all[:, e * NG + g:e * NG + g + 1]))
                st_ev["xng_free"] = e_sc
                k.w("tensor", e_sc)
                if e == 0 and mt == 0:
                    k.w("tensor", e_idb)
                last_sxn_wr = None
                for dc in range(8):
                    bi, _ = btB.acq()
                    for g in range(NG):
                        tev = k.e("tensor", lambda E, dc=dc, g=g, bi=bi: E.transpose(
                            pbt[:, bi * 1024 + g * 128:bi * 1024 + (g + 1) * 128],
                            sxng_tok[:, g * 1024 + dc * 128:g * 1024 + (dc + 1) * 128],
                            identb[:]), inc="mm" if g == NG - 1 else None)
                    k.w("vector", tev)
                    if dc == 0:
                        if e == 0:
                            k.w("vector", st_ev["attn_mm_end"])
                        k.w("vector", st_ev["sxn_rd_end2"][slot])
                    last_sxn_wr = k.e("vector", lambda E, dc=dc, bi=bi, sxn=sxn: E.tensor_copy(
                        sxn[:, dc * CAP:(dc + 1) * CAP],
                        pbt[:, bi * 1024:bi * 1024 + CAP]))
                    btB.rel(bi, last_sxn_wr)
                st_ev["sxng_rd_end"] = tev
                return last_sxn_wr

            prep = {}
            gd0 = prologue1(0)
            prep[0] = prologue2(0, gd0)
            for e in range(EXP):
                slot = e % 2
                sxn = sxn_gbuf[slot]
                last_sxn_wr = prep.pop(e)
                h_done = None
                for j in range(2):
                    buf, rdy = slabs.load(w3d(w1T_d, e, j * 512))
                    k.w("tensor", rdy)
                    if j == 0:
                        k.w("tensor", last_sxn_wr)
                    for o in range(4):
                        oc = j * 4 + o
                        bi, bsl = mmB.acq(cols=CAP)
                        for c in range(8):
                            ev = k.e("tensor", lambda E, b=buf, c=c, o=o, bsl=bsl, sxn=sxn: E.matmul(
                                bsl, wsl[b][:, c * 512 + o * 128:c * 512 + (o + 1) * 128],
                                sxn[:, c * CAP:c * CAP + CAP],
                                start=(c == 0), stop=(c == 7)),
                                inc="mm" if c == 7 else None)
                        k.w("scalar", ev)
                        if mt == 0 and e == 0 and oc == 0:
                            k.w("scalar", e_b1)
                            k.w("scalar", e_b2)
                            k.w("scalar", e_b3)
                            k.w("scalar", st_ev["attn_mm_end"])
                        e_h = k.e("scalar", lambda E, bi=bi, oc=oc, e=e: E.activation(
                            h_g[:, oc * CAP:oc * CAP + CAP],
                            mmB.sl(bi, 0, CAP), AF.Silu,
                            bias=b1_sb[:, oc * 8 + e:oc * 8 + e + 1]), inc="sc")
                        mmB.rel(bi, e_h)
                        h_done = e_h
                    slabs.free_evt[buf] = ev
                w3bufs = {}
                w2bufs = {}
                prod_done = None
                gd_next = None
                for oc in range(8):
                    j = oc // 4
                    if j not in w3bufs:
                        w3bufs[j] = slabs.load(w3d(w3T_d, e, j * 512))
                        w2bufs[j] = slabs.load(w3d(w2T_d, e, j * 512))
                    b3buf, rdy3 = w3bufs[j]
                    b2buf, rdy2 = w2bufs[j]
                    bi3, bsl3 = mmB.acq(cols=CAP)
                    if oc % 4 == 0:
                        k.w("tensor", rdy3)
                    for c in range(8):
                        ev3 = k.e("tensor", lambda E, b=b3buf, c=c, oc=oc, bsl=bsl3, sxn=sxn: E.matmul(
                            bsl, wsl[b][:, c * 512 + (oc % 4) * 128:c * 512 + (oc % 4 + 1) * 128],
                            sxn[:, c * CAP:c * CAP + CAP],
                            start=(c == 0), stop=(c == 7)),
                            inc="mm" if c == 7 else None)
                    bi2, bsl2 = mmB.acq(cols=CAP)
                    if oc % 4 == 0:
                        k.w("tensor", rdy2)
                    if oc == 0:
                        k.w("tensor", h_done)
                    for c in range(8):
                        ev2 = k.e("tensor", lambda E, b=b2buf, c=c, oc=oc, bsl=bsl2: E.matmul(
                            bsl, wsl[b][:, c * 512 + (oc % 4) * 128:c * 512 + (oc % 4 + 1) * 128],
                            h_g[:, c * CAP:c * CAP + CAP],
                            start=(c == 0), stop=(c == 7)),
                            inc="mm" if c == 7 else None)
                    i2 = oc % 2
                    k.w("scalar", ev3)
                    if st_ev["tmp_free"][i2] is not None:
                        k.w("scalar", st_ev["tmp_free"][i2])
                    e_x3 = k.e("scalar", lambda E, bi3=bi3, oc=oc, e=e, i2=i2: E.activation(
                        tmp3[i2][:, 0:CAP], mmB.sl(bi3, 0, CAP), AF.Identity,
                        bias=b3_sb[:, oc * 8 + e:oc * 8 + e + 1]), inc="sc")
                    mmB.rel(bi3, e_x3)
                    k.w("scalar", ev2)
                    e_x2 = k.e("scalar", lambda E, bi2=bi2, oc=oc, e=e, i2=i2: E.activation(
                        tmp2[i2][:, 0:CAP], mmB.sl(bi2, 0, CAP), AF.Identity,
                        bias=b2_sb[:, oc * 8 + e:oc * 8 + e + 1]), inc="sc")
                    mmB.rel(bi2, e_x2)
                    k.w("vector", e_x3)
                    k.w("vector", e_x2)
                    if oc == 0 and e == 0:
                        k.w("vector", st_ev["D_mm_end"])
                    e_pr = k.e("vector", lambda E, i2=i2: E.tensor_mul(
                        prodtile[:, 0:CAP], tmp3[i2][:, 0:CAP], tmp2[i2][:, 0:CAP]))
                    st_ev["tmp_free"][i2] = e_pr
                    k.w("tensor", e_pr)
                    bi, _ = btB.acq()
                    for g in range(NG):
                        tev = k.e("tensor", lambda E, g=g, bi=bi: E.transpose(
                            pbt[:, bi * 1024 + g * 128:bi * 1024 + (g + 1) * 128],
                            prodtile[:, g * 128:(g + 1) * 128],
                            identb[:]), inc="mm" if g == NG - 1 else None)
                    k.w("vector", tev)
                    if oc == 0:
                        if e == 0:
                            if st_ev["sd_prev"] is not None:
                                k.w("vector", st_ev["sd_prev"])
                        elif sd_after.get(e - 1) is not None:
                            k.w("vector", sd_after[e - 1])
                    prod_done = k.e("vector", lambda E, oc=oc, bi=bi: E.tensor_copy(
                        prodtok[:].rearrange("p (g n) -> p g n", g=NG)
                        [:, :, oc * 128:(oc + 1) * 128],
                        pbt[:, bi * 1024:bi * 1024 + CAP]
                        .rearrange("p (g n) -> p g n", g=NG)))
                    btB.rel(bi, prod_done)
                    if oc % 4 == 3:
                        slabs.free_evt[b3buf] = ev3
                        slabs.free_evt[b2buf] = ev2
                    if e + 1 < EXP:
                        if oc == 1:
                            gd_next = prologue1(e + 1)
                        elif oc == 4:
                            prep[e + 1] = prologue2(e + 1, gd_next)
                st_ev["sxn_rd_end"] = ev3
                st_ev["sxn_rd_end2"][slot] = ev3
                k.w("gpsimd", prod_done)
                if mt == 0 and e == 0:
                    k.w("gpsimd", e_ztot)
                for g in range(NG):
                    k.e("gpsimd", lambda E, g=g, e=e: E.indirect_dma_start(
                        out=yt_d,
                        out_offset=IndirectOffsetOnAxis(
                            ap=ixs_all[:, e * NG + g:e * NG + g + 1], axis=0),
                        in_=prodtok[:, g * 1024:(g + 1) * 1024], in_offset=None,
                        bounds_check=bc2_reg, oob_is_err=False), inc="sd", amt=16)
                sd_after[e] = ("sd", k.cnt["sd"])
            st_ev["sd_prev"] = ("sd", k.cnt["sd"])
            st_ev["gd_prev"] = ("gd", k.cnt["gd"])
            return ("sd", k.cnt["sd"])

        def stage_G(mt, sd_tot):
            zi = [0]
            zz_done = None
            for j in range(4):
                b1buf, rdy1 = slabs.load(w2d(sw1T_d, j * 512))
                b2buf, rdy2 = slabs.load(w2d(sw2T_d, j * 512))
                for it_in in range(4):
                    it = j * 4 + it_in
                    for n in range(NN):
                        bia, bsla = mmB.acq(cols=NW)
                        if it_in == 0 and n == 0:
                            k.w("tensor", rdy1)
                        for c in range(8):
                            ev1 = k.e("tensor", lambda E, b=b1buf, c=c, it_in=it_in, n=n, bsl=bsla: E.matmul(
                                bsl, wsl[b][:, c * 512 + it_in * 128:c * 512 + (it_in + 1) * 128],
                                xn_f[:, c * M + n * 512:c * M + n * 512 + NW],
                                start=(c == 0), stop=(c == 7)),
                                inc="mm" if c == 7 else None)
                        bib, bslb = mmB.acq(cols=NW)
                        if it_in == 0 and n == 0:
                            k.w("tensor", rdy2)
                        for c in range(8):
                            ev2 = k.e("tensor", lambda E, b=b2buf, c=c, it_in=it_in, n=n, bsl=bslb: E.matmul(
                                bsl, wsl[b][:, c * 512 + it_in * 128:c * 512 + (it_in + 1) * 128],
                                xn_f[:, c * M + n * 512:c * M + n * 512 + NW],
                                start=(c == 0), stop=(c == 7)),
                                inc="mm" if c == 7 else None)
                        i2 = zi[0] % 2
                        k.w("scalar", ev1)
                        if st_ev["z1t_free"][i2] is not None:
                            k.w("scalar", st_ev["z1t_free"][i2])
                        e_z1 = k.e("scalar", lambda E, bia=bia, i2=i2: E.activation(
                            z1t[i2][:, 0:NW], mmB.sl(bia, 0, NW), AF.Silu), inc="sc")
                        mmB.rel(bia, e_z1)
                        k.w("vector", ev2)
                        k.w("vector", e_z1)
                        if it == 0 and n == 0:
                            k.w("vector", st_ev["D_mm_end"])
                            k.w("vector", sd_tot)
                        e_zz = k.e("vector", lambda E, bib=bib, it=it, n=n, i2=i2: E.tensor_mul(
                            zz[:, it * M + n * 512:it * M + n * 512 + NW],
                            z1t[i2][:, 0:NW], mmB.sl(bib, 0, NW)), inc="ev")
                        mmB.rel(bib, e_zz)
                        st_ev["z1t_free"][i2] = e_zz
                        zz_done = e_zz
                        zi[0] += 1
                slabs.free_evt[b1buf] = ev1
                slabs.free_evt[b2buf] = ev2
            if g_level < 2:
                st_ev["G_mm_end"] = ev2
                return zz_done
            G_end = None
            acc_done = None
            for j in range(2):
                br0, rdy0 = slabs.load(wrow(sw3T_d, 0, j * 512))
                br1, rdy1 = slabs.load(wrow(sw3T_d, 1024, j * 512))
                for o in range(4):
                    oc = j * 4 + o
                    for n in range(NN):
                        bi, bsl = mmB.acq(cols=NW)
                        if o == 0 and n == 0:
                            k.w("tensor", rdy0)
                            k.w("tensor", rdy1)
                            k.w("tensor", zz_done)
                        for half, bb in ((0, br0), (1, br1)):
                            for c in range(8):
                                i = half * 8 + c
                                ev = k.e("tensor", lambda E, b=bb, c=c, o=o, n=n, i=i, bsl=bsl: E.matmul(
                                    bsl, wsl[b][:, c * 512 + o * 128:c * 512 + (o + 1) * 128],
                                    zz[:, i * M + n * 512:i * M + n * 512 + NW],
                                    start=(i == 0), stop=(i == 15)),
                                    inc="mm" if i == 15 else None)
                        G_end = ev
                        k.w("vector", ev)
                        e_a = k.e("vector", lambda E, bi=bi, oc=oc, n=n: E.tensor_add(
                            acc[:, oc * M + n * 512:oc * M + n * 512 + NW],
                            acc[:, oc * M + n * 512:oc * M + n * 512 + NW],
                            mmB.sl(bi, 0, NW)), inc="ev")
                        mmB.rel(bi, e_a)
                        acc_done = e_a
                slabs.free_evt[br0] = G_end
                slabs.free_evt[br1] = G_end
            st_ev["G_mm_end"] = G_end
            return acc_done

        def stage_H(mt, acc_done, sd_tot):
            H_end = None
            for tt in range(NT):
                row0 = mt * M + tt * 128
                slot = tt % 2
                k.w("sync", st_ev["xfin_free"][slot])
                ld = k.e("sync", lambda E, r=row0, s=slot: E.dma_start(
                    out=xfin[s][:], in_=x_d[r:r + 128, :]), inc=f"xf{slot}", amt=16)
                if tt == 0:
                    k.w("sync", sd_tot)
                if st_ev.get("ytk_free", [None, None])[slot] is not None:
                    k.w("sync", st_ev["ytk_free"][slot])
                ld_y = k.e("sync", lambda E, r=row0, s=slot: E.dma_start(
                    out=ytk_w[s][:].rearrange("p (two n) -> p two n", two=2),
                    in_=yt_d.rearrange("(t two) n -> t two n", two=2)[r:r + 128]),
                    inc=f"yd{slot}", amt=16)
                k.w("tensor", acc_done)
                k.w("vector", ld)
                k.w("vector", ld_y)
                if st_ev["otok_store"][slot] is not None:
                    k.w("vector", st_ev["otok_store"][slot])
                for half in range(2):
                    bi, _ = trB.acq()
                    for cc in range(4):
                        c = half * 4 + cc
                        ev = k.e("tensor", lambda E, c=c, tt=tt, bi=bi, cc=cc: E.transpose(
                            trB.sl(bi, cc * 128, cc * 128 + 128),
                            acc[:, c * M + tt * 128:c * M + (tt + 1) * 128],
                            identf[:]), inc="mm" if cc == 3 else None)
                    H_end = ev
                    k.w("vector", ev)
                    k.e("vector", lambda E, bi=bi, h=half, s=slot: E.tensor_add(
                        otok[s][:, h * 512:(h + 1) * 512], trB.sl(bi, 0, 512),
                        xfin[s][:, h * 512:(h + 1) * 512]))
                    k.e("vector", lambda E, h=half, s=slot: E.tensor_add(
                        otok[s][:, h * 512:(h + 1) * 512],
                        otok[s][:, h * 512:(h + 1) * 512],
                        ytk_w[s][:, h * 512:(h + 1) * 512]), inc="ev")
                    e_o = k.e("vector", lambda E, h=half, s=slot: E.tensor_add(
                        otok[s][:, h * 512:(h + 1) * 512],
                        otok[s][:, h * 512:(h + 1) * 512],
                        ytk_w[s][:, 1024 + h * 512:1024 + (h + 1) * 512]), inc="ev")
                    trB.rel(bi, e_o)
                st_ev["xfin_free"][slot] = e_o
                st_ev.setdefault("ytk_free", [None, None])[slot] = e_o
                k.w("sync", e_o)
                st_out = k.e("sync", lambda E, r=row0, s=slot: E.dma_start(
                    out=y_d[r:r + 128, :], in_=otok[s][:]), inc=f"od{slot}", amt=16)
                st_ev["otok_store"][slot] = st_out
                n_store[0] += 1
            st_ev["H_mm_end"] = H_end

        for mt in range(NMT):
            xnf_done = qk_done = v_done = aof_done = rt_done = acc_done = None
            sd_tot = None
            pre = []
            if "B" in stages:
                pre = preload_B(mt)
            if "A" in stages:
                xnf_done = stage_A(mt)
            if "B" in stages:
                qk_done, v_done = stage_B(mt, xnf_done, pre)
            if "C" in stages:
                aof_done = stage_C(mt, qk_done, v_done)
            if "E" in stages:
                rt_done = stage_E(mt)
                st_ev["routing_done"] = rt_done
            if "D" in stages:
                stage_D(mt, aof_done)
            if "F" in stages:
                sd_tot = stage_F(mt, rt_done, xnf_done)
            if "G" in stages:
                acc_done = stage_G(mt, sd_tot)
            if "H" in stages:
                stage_H(mt, acc_done, sd_tot)

        if debug:
            k.w("gpsimd", ("od0", k.cnt["od0"]))
            k.w("gpsimd", ("od1", k.cnt["od1"]))
            k.e("gpsimd", lambda E: E.dma_start(out=dbg["dbg_xnf"], in_=xn_f[:]),
                inc="dbg", amt=16)
            k.e("gpsimd", lambda E: E.dma_start(out=dbg["dbg_logits"], in_=logits[:]),
                inc="dbg", amt=16)
            k.e("gpsimd", lambda E: E.dma_start(out=dbg["dbg_srow"], in_=s_tokall[:]),
                inc="dbg", amt=16)
            k.e("gpsimd", lambda E: E.dma_start(out=dbg["dbg_qf"], in_=scr[:, 0:SM]),
                inc="dbg", amt=16)
            k.e("gpsimd", lambda E: E.dma_start(out=dbg["dbg_aof"], in_=scr[:, 3 * SM:4 * SM]),
                inc="dbg", amt=16)
            k.e("gpsimd", lambda E: E.dma_start(out=dbg["dbg_acc"], in_=acc[:]),
                inc="dbg", amt=16)
            k.e("gpsimd", lambda E: E.dma_start(out=dbg["dbg_hsb"], in_=scr[:, SM:2 * SM]),
                inc="dbg", amt=16)
            k.e("gpsimd", lambda E: E.dma_start(out=dbg["dbg_zz"], in_=scr[:, 2 * SM:4 * SM]),
                inc="dbg", amt=16)
            k.w("gpsimd", ("dbg", k.cnt["dbg"]))

        k.w("sync", ("od0", k.cnt["od0"]))
        k.w("sync", ("od1", k.cnt["od1"]))

        with nc.Block() as block:
            @block.tensor
            def _(E):
                for f in k.plan["tensor"]:
                    f(E)

            @block.vector
            def _(E):
                for f in k.plan["vector"]:
                    f(E)

            @block.scalar
            def _(E):
                for f in k.plan["scalar"]:
                    f(E)

            @block.gpsimd
            def _(E):
                for f in k.plan["gpsimd"]:
                    f(E)

            @block.sync
            def _(E):
                for f in k.plan["sync"]:
                    f(E)

    return nc


def make_inmaps(inputs, tokens_per_core, n_cores):
    x = np.ascontiguousarray(np.asarray(inputs["L0_attn_x"], dtype=np.float32)
                             .reshape(-1, D))
    wqkv = np.ascontiguousarray(np.asarray(inputs["L0_attn_W_qkv"], np.float32))
    woutT = np.ascontiguousarray(np.asarray(inputs["L0_attn_W_out"], np.float32).T)
    bout = np.ascontiguousarray(np.asarray(inputs["L0_attn_b_out"], np.float32))
    gwT = np.ascontiguousarray(np.asarray(inputs["L0_ffn_gate_w"], np.float32).T)
    w1T = np.ascontiguousarray(np.asarray(inputs["L0_ffn_w1"], np.float32).transpose(0, 2, 1))
    w2T = np.ascontiguousarray(np.asarray(inputs["L0_ffn_w2"], np.float32).transpose(0, 2, 1))
    w3T = np.ascontiguousarray(np.asarray(inputs["L0_ffn_w3"], np.float32).transpose(0, 2, 1))
    b1T = np.ascontiguousarray(np.asarray(inputs["L0_ffn_b1"], np.float32).T)
    b2T = np.ascontiguousarray(np.asarray(inputs["L0_ffn_b2"], np.float32).T)
    b3T = np.ascontiguousarray(np.asarray(inputs["L0_ffn_b3"], np.float32).T)
    sw1T = np.ascontiguousarray(np.asarray(inputs["L0_ffn_sw1"], np.float32).T)
    sw2T = np.ascontiguousarray(np.asarray(inputs["L0_ffn_sw2"], np.float32).T)
    sw3T = np.ascontiguousarray(np.asarray(inputs["L0_ffn_sw3"], np.float32).T)
    shared = dict(wqkv=wqkv, woutT=woutT, bout=bout, gwT=gwT, w1T=w1T, w2T=w2T,
                  w3T=w3T, b1T=b1T, b2T=b2T, b3T=b3T, sw1T=sw1T, sw2T=sw2T,
                  sw3T=sw3T)
    in_maps = []
    for c in range(n_cores):
        m = dict(shared)
        m["x"] = np.ascontiguousarray(
            x[c * tokens_per_core:(c + 1) * tokens_per_core])
        in_maps.append(m)
    return in_maps


def kernel(**inputs):
    n_cores = 8
    tokens_per_core = 4096
    in_maps = make_inmaps(inputs, tokens_per_core, n_cores)
    nc = build_nc(tokens_per_core)
    res = run_bass_kernel_spmd(nc, in_maps, core_ids=list(range(n_cores)))
    out = np.concatenate([r["y"] for r in res.results], axis=0)
    return out.reshape(16, 2048, D).astype(np.float32)


# version 12
# speedup vs baseline: 1.0147x; 1.0055x over previous
import contextlib
import numpy as np

import concourse.bass as bass
import concourse.mybir as mybir
from concourse.bass import IndirectOffsetOnAxis
from concourse.bass_utils import run_bass_kernel_spmd

F32 = mybir.dt.float32
BF16 = mybir.dt.bfloat16
AX = mybir.AxisListType.X
AF = mybir.ActivationFunctionType
ALU = mybir.AluOpType

D = 1024
EXP = 8
II = 2048
SCALE = 1.0 / 32.0
NBUF = 3
CAP = 512
ENGS = ("tensor", "vector", "scalar", "sync", "gpsimd")


class K:

    def __init__(self):
        self.plan = {e: [] for e in ENGS}
        self.cnt = {}
        self.sems = {}

    def sem(self, name, stack, nc):
        self.sems[name] = stack.enter_context(nc.semaphore(name))
        self.cnt[name] = 0

    def w(self, eng, evt):
        if evt is None:
            return
        s, v = evt
        sem = self.sems[s]
        self.plan[eng].append(lambda E, sem=sem, v=v: E.wait_ge(sem, v))

    def wmax(self, eng, evts):
        by = {}
        for e in evts:
            if e is None:
                continue
            by[e[0]] = max(by.get(e[0], 0), e[1])
        for s, v in by.items():
            self.w(eng, (s, v))

    def e(self, eng, fn, inc=None, amt=1):
        if eng == "vector":
            self.w("vector", ("ev", self.cnt["ev"]))
            if inc is None:
                inc, amt = "ev", 1
        if inc is None:
            self.plan[eng].append(fn)
            return None
        self.cnt[inc] += amt
        v = self.cnt[inc]
        sem = self.sems[inc]
        self.plan[eng].append(lambda E, fn=fn, sem=sem, amt=amt: fn(E).then_inc(sem, amt))
        return (inc, v)


class Banks:

    def __init__(self, k, ap, nbanks, width):
        self.k, self.ap, self.n, self.w = k, ap, nbanks, width
        self.free = [[] for _ in range(nbanks)]
        self.rr = 0

    def acq(self, cols=None):
        i = self.rr % self.n
        self.rr += 1
        self.k.wmax("tensor", self.free[i])
        self.free[i] = []
        sl = self.ap[:, i * self.w:(i + 1) * self.w if cols is None else i * self.w + cols]
        return i, sl

    def sl(self, i, c0, c1):
        return self.ap[:, i * self.w + c0:i * self.w + c1]

    def rel(self, i, evt):
        self.free[i].append(evt)


def build_nc(tokens, debug=False, stages='ABCDEFGH', a_level=4, f_level=9, g_level=9):
    M = min(1024, tokens)
    NMT = tokens // M
    NT = M // 128
    NN = max(1, M // 512)
    NW = min(512, M)
    SM = 8 * M
    assert tokens >= 1024, "sparse MoE layout assumes M=1024 macro-tiles"

    nc = bass.Bass("TRN2", target_bir_lowering=False,
                   detect_race_conditions=False)
    k = K()

    x_d = nc.dram_tensor("x", [tokens, D], F32, kind="ExternalInput").ap()
    y_d = nc.dram_tensor("y", [tokens, D], F32, kind="ExternalOutput").ap()
    wqkv_d = nc.dram_tensor("wqkv", [D, 3 * D], F32, kind="ExternalInput").ap()
    woutT_d = nc.dram_tensor("woutT", [D, D], F32, kind="ExternalInput").ap()
    gwT_d = nc.dram_tensor("gwT", [D, EXP], F32, kind="ExternalInput").ap()
    w1T_d = nc.dram_tensor("w1T", [EXP, D, D], F32, kind="ExternalInput").ap()
    w2T_d = nc.dram_tensor("w2T", [EXP, D, D], F32, kind="ExternalInput").ap()
    w3T_d = nc.dram_tensor("w3T", [EXP, D, D], F32, kind="ExternalInput").ap()
    sw1T_d = nc.dram_tensor("sw1T", [D, II], F32, kind="ExternalInput").ap()
    sw2T_d = nc.dram_tensor("sw2T", [D, II], F32, kind="ExternalInput").ap()
    sw3T_d = nc.dram_tensor("sw3T", [II, D], F32, kind="ExternalInput").ap()
    b1T_d = nc.dram_tensor("b1T", [D, EXP], F32, kind="ExternalInput").ap()
    b2T_d = nc.dram_tensor("b2T", [D, EXP], F32, kind="ExternalInput").ap()
    b3T_d = nc.dram_tensor("b3T", [D, EXP], F32, kind="ExternalInput").ap()
    bout_d = nc.dram_tensor("bout", [D], F32, kind="ExternalInput").ap()
    xt_d = nc.dram_tensor("xt_scr", [tokens, D], BF16, kind="ExternalOutput").ap()
    yt_d = nc.dram_tensor("yt_scr", [2 * tokens, D], BF16, kind="ExternalOutput").ap()
    dbg = {}
    if debug:
        for nm, shp in [("dbg_xnf", [128, SM]), ("dbg_logits", [128, NT * 8]),
                        ("dbg_srow", [128, 8 * NT]), ("dbg_qf", [128, SM]),
                        ("dbg_aof", [128, SM]), ("dbg_acc", [128, SM]),
                        ("dbg_hsb", [128, SM]), ("dbg_zz", [128, 16 * M])]:
            dbg[nm] = nc.dram_tensor(nm, shp, F32, kind="ExternalOutput").ap()

    with contextlib.ExitStack() as st:
        def sb(name, shape, dt):
            return st.enter_context(nc.sbuf_tensor(name, shape, dt))

        def ps(name, shape, dt):
            return st.enter_context(nc.psum_tensor(name, shape, dt))

        scr = sb("scr", [128, 32 * M], BF16)
        qf = scr[:, 0:SM]
        kf = scr[:, SM:2 * SM]
        vtok = scr[:, 2 * SM:2 * SM + NT * 1024]
        aof = scr[:, 3 * SM:4 * SM]
        sxn_g = scr[:, 0:8 * CAP]
        h_g = scr[:, 8 * CAP:16 * CAP]
        p_mx = scr[:, 16 * CAP:16 * CAP + NT * CAP]
        xng_tok = scr[:, 24 * CAP:32 * CAP]
        sxng_tok = scr[:, 32 * CAP:40 * CAP]
        prodtok = scr[:, 40 * CAP:48 * CAP]
        prodtile = scr[:, 48 * CAP:49 * CAP]
        zero_zone = scr[:, 0:4096]
        h_sb = scr[:, SM:2 * SM]
        zz = scr[:, 2 * SM:4 * SM]
        xn_f = sb("xn_f", [128, SM], BF16)
        acc = sb("acc", [128, SM], F32)
        wsl = [sb(f"wsl{i}", [128, 8 * 512], BF16) for i in range(NBUF)]
        xa = [sb(f"xa{i}", [128, D], F32) for i in range(2)]
        xfin = [sb(f"xfin{i}", [128, D], F32) for i in range(2)]
        otok = [sb(f"otok{i}", [128, D], F32) for i in range(2)]
        xn32 = sb("xn32", [128, D], F32)
        xntok = [sb(f"xntok{i}", [128, D], F32) for i in range(2)]
        s_tokall = sb("s_tokall", [128, 8 * NT], F32)
        slot_tokall = sb("slot_tokall", [128, 8 * NT], F32)
        m_tokall = sb("m_tokall", [128, 8 * NT], F32)
        r_tokall = sb("r_tokall", [128, 8 * NT], F32)
        m_em8 = sb("m_em8", [8, M], BF16)
        rk_em8 = sb("rk_em8", [8, M], F32)
        iotaC = sb("iotaC", [128, CAP], F32)
        idf8 = sb("idf8", [128, 8], mybir.dt.int32)
        idlo_i = sb("idlo_i", [128, 8], mybir.dt.int32)
        c31 = sb("c31", [128, 1], mybir.dt.int32)
        idhi = sb("idhi", [128, 8], BF16)
        idlo = sb("idlo", [128, 8], BF16)
        meta = sb("meta", [128, 32], BF16)
        meta32 = sb("meta32", [4, CAP], F32)
        idx_all = sb("idx_all", [128, 8 * (CAP // 128)], mybir.dt.int32)
        ixs_all = sb("ixs_all", [128, 8 * (CAP // 128)], mybir.dt.int32)
        sg_all = sb("sg_all", [128, 8 * (CAP // 128)], F32)
        ytk_w = [sb(f"ytkw{i}", [128, 2 * D], BF16) for i in range(2)]
        idxf = sb("idxf", [128, 4], F32)
        ids1f = sb("ids1f", [128, 4], F32)
        eqf = sb("eqf", [128, 4], F32)
        mtt = sb("mtt", [128, 16], F32)
        logits = sb("logits", [128, NT * 8], F32)
        sm32 = sb("sm32", [128, 128], F32)
        p32 = sb("p32", [128, 128], F32)
        pn_sb = sb("pn_sb", [128, 128], BF16)
        pt_sb = sb("pt_sb", [128, 256], BF16)
        tmp3 = [sb(f"tmp3{i}", [128, 512], BF16) for i in range(2)]
        tmp2 = [sb(f"tmp2{i}", [128, 512], BF16) for i in range(2)]
        z1t = [sb(f"z1t{i}", [128, 512], BF16) for i in range(2)]
        gw_sb = sb("gw_sb", [128, 8 * EXP], F32)
        b1_sb = sb("b1_sb", [128, 8 * EXP], F32)
        b2_sb = sb("b2_sb", [128, 8 * EXP], F32)
        b3_sb = sb("b3_sb", [128, 8 * EXP], F32)
        bout_sb = sb("bout_sb", [128, 8], F32)
        identf = sb("identf", [128, 128], F32)
        identb = sb("identb", [128, 128], BF16)
        amask = sb("amask", [128, 128], F32)

        zerocol = sb("zerocol", [128, 1], F32)
        colp = sb("colp", [128, 20], F32)
        (CSX, CMEAN, CSXX, CEX2, CM2, CVAR, CSRT, CY0, CT1, CT2, CT3, CRSTD,
         CNRM, CRSUM, CRDEN, C32, CHALF, CT4) = range(18)

        def cp(i):
            return colp[:, i:i + 1]

        pmm = ps("pmm", [128, 4 * 512], F32)
        ptr = ps("ptr", [128, 512], F32)
        pbt = ps("pbt", [128, 2048], BF16)
        psml = ps("psml", [128, 512], F32)

        for s in ("wd0", "wd1", "wd2", "ws", "xa0", "xa1", "xf0",
                  "xf1", "od0", "od1", "dbg", "mm", "ev", "sc", "gp",
                  "xt0", "xt1", "gd", "sd", "zd", "yd0", "yd1"):
            k.sem(s, st, nc)

        mmB = Banks(k, pmm, 4, 512)
        trB = Banks(k, ptr, 1, 512)
        btB = Banks(k, pbt, 2, 1024)

        e1 = k.e("gpsimd", lambda E: E.memset(identf[:], 1.0), inc="gp")
        k.w("gpsimd", e1)
        e_idf = k.e("gpsimd", lambda E: E.affine_select(
            identf[:], identf[:], [[-1, 128]], ALU.is_equal, 0.0,
            base=0, channel_multiplier=1), inc="gp")
        e2 = k.e("gpsimd", lambda E: E.memset(amask[:], 0.0), inc="gp")
        k.w("gpsimd", e2)
        e_msk = k.e("gpsimd", lambda E: E.affine_select(
            amask[:], amask[:], [[-1, 128]], ALU.is_ge, -1.0e30,
            base=0, channel_multiplier=1), inc="gp")
        e_ioc = k.e("gpsimd", lambda E: E.iota(
            iotaC[:], [[1, CAP]], base=0, channel_multiplier=0,
            allow_small_or_imprecise_dtypes=True), inc="gp")
        e_idf8 = k.e("gpsimd", lambda E: E.iota(
            idf8[:], [[128, 8]], base=1, channel_multiplier=1), inc="gp")
        k.w("vector", e_idf)
        e_idb = k.e("vector", lambda E: E.tensor_copy(identb[:], identf[:]), inc="ev")
        k.e("vector", lambda E: E.memset(zerocol[:], 0.0))
        k.e("vector", lambda E: E.memset(cp(C32), 32.0))
        k.e("vector", lambda E: E.memset(cp(CHALF), 0.5))
        k.w("vector", e_idf8)
        k.e("vector", lambda E: E.memset(c31[:], 31))
        k.e("vector", lambda E: E.tensor_single_scalar(
            idlo_i[:], idf8[:], c31[:, 0:1], ALU.bitwise_and))
        k.e("vector", lambda E: E.tensor_copy(m_tokall[:, 0:8], idf8[:]))
        k.e("vector", lambda E: E.tensor_copy(m_tokall[:, 8:16], idlo_i[:]))
        k.e("vector", lambda E: E.tensor_copy(idlo[:], idlo_i[:]))
        k.e("vector", lambda E: E.tensor_sub(
            m_tokall[:, 0:8], m_tokall[:, 0:8], m_tokall[:, 8:16]))
        k.e("vector", lambda E: E.tensor_scalar_mul(idhi[:], m_tokall[:, 0:8], 1.0 / 32.0))
        k.e("vector", lambda E: E.tensor_copy(
            meta[:].rearrange("p (t k) -> p t k", k=4)[:, 0:8, 0:1],
            idhi[:, 0:8].rearrange("p (t u) -> p t u", u=1)))
        k.e("vector", lambda E: E.tensor_copy(
            meta[:].rearrange("p (t k) -> p t k", k=4)[:, 0:8, 1:2],
            idlo[:, 0:8].rearrange("p (t u) -> p t u", u=1)))


        def sdma(dst, src):
            return k.e("gpsimd", lambda E, d=dst, s=src: E.dma_start(
                out=d, in_=s, allow_slow_non_contiguous=True), inc="ws", amt=16)

        e_gw = sdma(gw_sb[:].rearrange("p (c e) -> p c e", c=8),
                    gwT_d.rearrange("(c p) e -> p c e", p=128))
        e_b1 = sdma(b1_sb[:].rearrange("p (t e) -> p t e", t=8),
                    b1T_d.rearrange("(t p) e -> p t e", p=128))
        e_b2 = sdma(b2_sb[:].rearrange("p (t e) -> p t e", t=8),
                    b2T_d.rearrange("(t p) e -> p t e", p=128))
        e_b3 = sdma(b3_sb[:].rearrange("p (t e) -> p t e", t=8),
                    b3T_d.rearrange("(t p) e -> p t e", p=128))
        e_bo = sdma(bout_sb[:], bout_d.rearrange("(t p) -> p t", p=128))
        e_gw = e_b1 = e_b2 = e_b3 = e_bo = ("ws", k.cnt["ws"])

        e_zz0 = k.e("vector", lambda E: E.memset(zero_zone, 0.0))
        k.w("sync", e_zz0)
        for kk in range(2 * tokens // 512):
            k.e("sync", lambda E, kk=kk: E.dma_start(
                out=yt_d.rearrange("(k g p) n -> k p g n", k=2 * tokens // 512, p=128)[kk],
                in_=zero_zone.rearrange("p (g n) -> p g n", g=4)), inc="zd", amt=16)
        for kk in range(tokens // 512):
            k.e("sync", lambda E, kk=kk: E.dma_start(
                out=xt_d.rearrange("(k g p) n -> k p g n", k=tokens // 512, p=128)[kk],
                in_=zero_zone.rearrange("p (g n) -> p g n", g=4)), inc="zd", amt=16)
        e_ztot = ("zd", k.cnt["zd"])

        bc_reg = nc.alloc_register(mybir.EngineType.Pool, "bc_reg")
        k.e("gpsimd", lambda E: nc.reg_mov(bc_reg, tokens - 1))
        bc2_reg = nc.alloc_register(mybir.EngineType.Pool, "bc2_reg")
        k.e("gpsimd", lambda E: nc.reg_mov(bc2_reg, 2 * tokens - 1))

        class Slabs:
            def __init__(self):
                self.kk = 0
                self.free_evt = [None] * NBUF

            def load(self, src3d):
                buf = self.kk % NBUF
                k.w("gpsimd", self.free_evt[buf])
                evt = k.e("gpsimd", lambda E, a=src3d, b=buf: E.dma_start(
                    out=wsl[b][:].rearrange("p (c n) -> p c n", c=8), in_=a),
                    inc=f"wd{buf}", amt=16)
                self.kk += 1
                return buf, evt

        slabs = Slabs()

        def w2d(ap2d, c0):
            return ap2d[:, c0:c0 + 512].rearrange("(c p) n -> p c n", p=128)

        def w3d(ap3d, e, c0):
            return ap3d[e, :, c0:c0 + 512].rearrange("(c p) n -> p c n", p=128)

        def wrow(ap2d, r0, c0):
            return ap2d[r0:r0 + 1024, c0:c0 + 512].rearrange("(c p) n -> p c n", p=128)

        st_ev = {
            "G_mm_end": None,
            "H_mm_end": None,
            "psml_free": None,
            "attn_mm_end": None,
            "D_mm_end": None,
            "xa_free": [None, None],
            "xfin_free": [None, None],
            "otok_store": [None, None],
            "xntok_free": [None, None],
            "sxn_rd_end": None,
            "sxn_rd_end2": [None, None],
            "tmp_free": [None, None],
            "z1t_free": [None, None],
            "gate_prev": None,
            "xt_evts": [],
            "meta_mm_prev": None,
            "routing_done": None,
            "gd_prev": None,
            "tbl_evt": {},
            "xng_free": None,
            "sxng_rd_end": None,
            "sd_prev": None,
        }

        n_store = [0]

        def stage_A(mt):
            xnf_done = None
            x32evts = []
            xt_evts = st_ev["xt_evts"]
            xt_evts.clear()
            for tt in range(NT):
                row0 = mt * M + tt * 128
                slot = tt % 2
                k.w("sync", st_ev["xa_free"][slot])
                ld = k.e("sync", lambda E, r=row0, s=slot: E.dma_start(
                    out=xa[s][:], in_=x_d[r:r + 128, :]), inc=f"xa{slot}", amt=16)
                k.w("scalar", ld)
                if st_ev["gate_prev"] is not None:
                    k.w("scalar", st_ev["gate_prev"])
                e_sq = k.e("scalar", lambda E, s=slot: E.activation(
                    xn32[:], xa[s][:], AF.Square, accum_out=cp(CSXX)), inc="sc")
                k.w("vector", ld)
                k.e("vector", lambda E, s=slot: E.reduce_sum(cp(CSX), xa[s][:], axis=AX))
                k.e("vector", lambda E: E.tensor_scalar_mul(cp(CMEAN), cp(CSX), 1.0 / D))
                k.w("vector", e_sq)
                k.e("vector", lambda E: E.tensor_scalar_mul(cp(CEX2), cp(CSXX), 1.0 / D))
                k.e("vector", lambda E: E.tensor_mul(cp(CM2), cp(CMEAN), cp(CMEAN)))
                k.e("vector", lambda E: E.tensor_sub(cp(CVAR), cp(CEX2), cp(CM2)))
                e_var = k.e("vector", lambda E: E.tensor_scalar_add(
                    cp(CT1), cp(CVAR), 1e-5), inc="ev")
                k.w("scalar", e_var)
                e_srt = k.e("scalar", lambda E: E.activation(
                    cp(CSRT), cp(CT1), AF.Sqrt), inc="sc")
                k.w("vector", e_srt)
                e_rc = k.e("vector", lambda E: E.reciprocal(cp(CY0), cp(CSRT)),
                           inc="ev")
                k.w("vector", e_rc)
                k.e("vector", lambda E: E.tensor_mul(cp(CT2), cp(CY0), cp(CY0)))
                k.e("vector", lambda E: E.tensor_mul(cp(CT2), cp(CT2), cp(CT1)))
                k.e("vector", lambda E: E.tensor_scalar(cp(CT3), cp(CT2), -0.5, 1.5, ALU.mult, ALU.add))
                k.e("vector", lambda E: E.tensor_mul(cp(CRSTD), cp(CY0), cp(CT3)))
                k.wmax("vector", st_ev["xntok_free"][slot] or [])
                e_xn = k.e("vector", lambda E, s=slot: E.tensor_scalar(
                    xntok[s][:], xa[s][:], cp(CMEAN), cp(CRSTD),
                    ALU.subtract, ALU.mult), inc="ev")
                st_ev["xa_free"][slot] = e_xn
                k.w("gpsimd", e_xn)
                if mt == 0 and tt == 0:
                    k.w("gpsimd", e_ztot)
                e_xt = k.e("gpsimd", lambda E, r=row0, s=slot: E.dma_start(
                    out=xt_d[r:r + 128, :], in_=xntok[s][:]),
                    inc=f"xt{slot}", amt=16)
                xt_evts.append(e_xt)
                if a_level < 2:
                    continue
                k.w("tensor", e_xn)
                if mt == 0 and tt == 0:
                    k.w("tensor", e_idf)
                ev_l = None
                for half in range(2):
                    bi, _ = trB.acq()
                    for cc in range(4):
                        c = half * 4 + cc
                        fn = (lambda E, s=slot, c=c, bi=bi, cc=cc: E.transpose(
                            trB.sl(bi, cc * 128, cc * 128 + 128),
                            xntok[s][:, c * 128:(c + 1) * 128], identf[:]))
                        ev = k.e("tensor", fn, inc="mm" if cc == 3 else None)
                    ev_l = ev
                    if a_level < 3:
                        trB.rel(bi, ev)
                        continue
                    k.w("vector", ev)
                    if half == 0 and st_ev["gate_prev"] is not None:
                        k.w("vector", st_ev["gate_prev"])
                        k.w("vector", xnf_done)
                    e_v = k.e("vector", lambda E, bi=bi, h=half: E.tensor_copy(
                        xn32[:, h * 512:(h + 1) * 512], trB.sl(bi, 0, 512)), inc="ev")
                    k.w("scalar", e_v)
                    if tt == 0 and half == 0 and st_ev["G_mm_end"] is not None:
                        k.w("scalar", st_ev["G_mm_end"])
                    e_s = k.e("scalar", lambda E, h=half, tt=tt: E.activation(
                        xn_f[:, 0:SM].rearrange("p (c t) -> p c t", c=8)
                        [:, h * 4:(h + 1) * 4, tt * 128:(tt + 1) * 128],
                        xn32[:, h * 512:(h + 1) * 512].rearrange("p (c t) -> p c t", c=4),
                        AF.Copy), inc="sc")
                    trB.rel(bi, e_v)
                    x32evts.append(e_v)
                    xnf_done = e_s
                st_ev["xntok_free"][slot] = [ev_l, xt_evts[-1]]
                if a_level < 3:
                    continue
                if a_level < 4:
                    continue
                k.wmax("tensor", x32evts[-2:])
                if mt == 0 and tt == 0:
                    k.w("tensor", e_gw)
                if st_ev["psml_free"] is not None:
                    k.w("tensor", st_ev["psml_free"])
                for c in range(8):
                    ev = k.e("tensor", lambda E, c=c: E.matmul(
                        psml[:, 0:8], xn32[:, c * 128:(c + 1) * 128],
                        gw_sb[:, c * 8:(c + 1) * 8],
                        start=(c == 0), stop=(c == 7)),
                        inc="mm" if c == 7 else None)
                k.w("vector", ev)
                e_lg = k.e("vector", lambda E, tt=tt: E.tensor_copy(
                    logits[:, tt * 8:(tt + 1) * 8], psml[:, 0:8]), inc="ev")
                st_ev["psml_free"] = e_lg
                st_ev["gate_prev"] = ev
            return xnf_done

        def preload_B(mt):
            return [slabs.load(w2d(wqkv_d, j * 512)) for j in range(3)]

        def stage_B(mt, xnf_done, pre):
            qk_done = None
            v_done = None
            for j in range(6):
                buf, rdy = pre[j] if j < len(pre) else slabs.load(w2d(wqkv_d, j * 512))
                k.w("tensor", rdy)
                if j == 0:
                    k.w("tensor", xnf_done)
                last_mm = None
                if j < 4:
                    dest = qf if j < 2 else kf
                    base_oc = (j % 2) * 4
                    for o in range(4):
                        for n in range(NN):
                            bi, bsl = mmB.acq(cols=NW)
                            for c in range(8):
                                ev = k.e("tensor", lambda E, b=buf, c=c, o=o, n=n, bsl=bsl: E.matmul(
                                    bsl, wsl[b][:, c * 512 + o * 128:c * 512 + (o + 1) * 128],
                                    xn_f[:, c * M + n * 512:c * M + n * 512 + NW],
                                    start=(c == 0), stop=(c == 7)),
                                    inc="mm" if c == 7 else None)
                            last_mm = ev
                            k.w("vector", ev)
                            if mt == 0 and j == 0 and o == 0 and n == 0:
                                k.w("vector", e_ztot)
                            oc = base_oc + o
                            e_v = k.e("vector", lambda E, bi=bi, oc=oc, n=n, d=dest: E.tensor_copy(
                                d[:, oc * M + n * 512:oc * M + n * 512 + NW],
                                mmB.sl(bi, 0, NW)), inc="ev")
                            mmB.rel(bi, e_v)
                            qk_done = e_v
                else:
                    for tt in range(NT):
                        bi, bsl = mmB.acq()
                        for c in range(8):
                            ev = k.e("tensor", lambda E, b=buf, c=c, tt=tt, bsl=bsl: E.matmul(
                                bsl, xn_f[:, c * M + tt * 128:c * M + (tt + 1) * 128],
                                wsl[b][:, c * 512:(c + 1) * 512],
                                start=(c == 0), stop=(c == 7)),
                                inc="mm" if c == 7 else None)
                        last_mm = ev
                        k.w("vector", ev)
                        if j == 4 and tt == 0 and st_ev["sd_prev"] is not None:
                            k.w("vector", st_ev["sd_prev"])
                        e_v = k.e("vector", lambda E, bi=bi, tt=tt, j=j: E.tensor_copy(
                            vtok[:, tt * 1024 + (j - 4) * 512:tt * 1024 + (j - 4) * 512 + 512],
                            mmB.sl(bi, 0, 512)), inc="ev")
                        mmB.rel(bi, e_v)
                        v_done = e_v
                slabs.free_evt[buf] = last_mm
            return qk_done, v_done

        def stage_C(mt, qk_done, v_done):
            aof_done = None
            attn_end = None
            for b in range(NT):
                bi, _ = mmB.acq(cols=128)
                if b == 0:
                    k.w("tensor", qk_done)
                for c in range(8):
                    ev = k.e("tensor", lambda E, c=c, b=b, bi=bi: E.matmul(
                        mmB.sl(bi, 0, 128),
                        qf[:, c * M + b * 128:c * M + (b + 1) * 128],
                        kf[:, c * M + b * 128:c * M + (b + 1) * 128],
                        start=(c == 0), stop=(c == 7)),
                        inc="mm" if c == 7 else None)
                attn_end = ev
                k.w("vector", ev)
                if b == 0 and mt == 0:
                    k.w("vector", (("gp", k.cnt["gp"])))
                k.e("vector", lambda E, bi=bi: E.scalar_tensor_tensor(
                    sm32[:], mmB.sl(bi, 0, 128), SCALE, amask[:],
                    ALU.mult, ALU.add))
                e_nm = k.e("vector", lambda E: E.reduce_max(
                    cp(CNRM), sm32[:], axis=AX, negate=True), inc="ev")
                mmB.rel(bi, e_nm)
                k.w("scalar", e_nm)
                e_p = k.e("scalar", lambda E: E.activation(
                    p32[:], sm32[:], AF.Exp, bias=cp(CNRM),
                    accum_out=cp(CRSUM)), inc="sc")
                k.w("vector", e_p)
                k.e("vector", lambda E: E.tensor_scalar_add(cp(CRDEN), cp(CRSUM), 1e-6))
                e_rc = k.e("vector", lambda E: E.reciprocal(cp(CRDEN), cp(CRDEN)),
                           inc="ev")
                k.w("vector", e_rc)
                e_pn = k.e("vector", lambda E: E.tensor_scalar_mul(
                    pn_sb[:], p32[:], cp(CRDEN)), inc="ev")
                sloti = b % 2
                k.w("tensor", e_pn)
                if mt == 0 and b < 2:
                    k.w("tensor", e_idb)
                k.wmax("tensor", btB.free[sloti])
                btB.free[sloti] = []
                e_tp = k.e("tensor", lambda E, s=sloti: E.transpose(
                    pbt[:, s * 1024:s * 1024 + 128], pn_sb[:], identb[:]), inc="mm")
                k.w("vector", e_tp)
                e_pt = k.e("vector", lambda E, s=sloti: E.tensor_copy(
                    pt_sb[:, s * 128:(s + 1) * 128], pbt[:, s * 1024:s * 1024 + 128]),
                    inc="ev")
                btB.free[sloti].append(e_pt)
                k.w("tensor", e_pt)
                if b == 0:
                    k.w("tensor", v_done)
                for half in range(2):
                    bi2, _ = mmB.acq()
                    for dt in range(4):
                        ev = k.e("tensor", lambda E, bi2=bi2, dt=dt, half=half, b=b, s=sloti: E.matmul(
                            mmB.sl(bi2, dt * 128, (dt + 1) * 128),
                            vtok[:, b * 1024 + (half * 4 + dt) * 128:b * 1024 + (half * 4 + dt + 1) * 128],
                            pt_sb[:, s * 128:(s + 1) * 128],
                            start=True, stop=True),
                            inc="mm" if dt == 3 else None)
                    attn_end = ev
                    k.w("scalar", ev)
                    e_a = k.e("scalar", lambda E, bi2=bi2, half=half, b=b: E.activation(
                        aof[:, 0:SM].rearrange("p (c t) -> p c t", c=8)
                        [:, half * 4:(half + 1) * 4, b * 128:(b + 1) * 128],
                        mmB.sl(bi2, 0, 512).rearrange("p (c t) -> p c t", c=4),
                        AF.Copy), inc="sc")
                    mmB.rel(bi2, e_a)
                    aof_done = e_a
            st_ev["attn_mm_end"] = attn_end
            return aof_done

        def table_build(mt, e):
            NG = CAP // 128
            if e == 0:
                k.w("vector", st_ev["routing_done"])
                k.w("vector", st_ev["attn_mm_end"])
                if st_ev["sd_prev"] is not None:
                    k.w("vector", st_ev["sd_prev"])
                if st_ev["gd_prev"] is not None:
                    k.w("vector", st_ev["gd_prev"])
            else:
                k.w("vector", st_ev["meta_mm_prev"])
            for tc in range(NT):
                e_pm = k.e("vector", lambda E, tc=tc, e=e: E.tensor_scalar(
                    p_mx[:, tc * CAP:(tc + 1) * CAP], iotaC[:, 0:CAP],
                    r_tokall[:, tc * 8 + e:tc * 8 + e + 1],
                    m_tokall[:, tc * 8 + e:tc * 8 + e + 1],
                    ALU.is_equal, ALU.mult))
            k.e("vector", lambda E, e=e: E.tensor_copy(
                meta[:].rearrange("p (t k) -> p t k", k=4)[:, 0:NT, 2:3],
                s_tokall[:].rearrange("p (t q) -> p t q", q=8)[:, 0:NT, e:e + 1]))
            e_meta = k.e("vector", lambda E, e=e: E.tensor_copy(
                meta[:].rearrange("p (t k) -> p t k", k=4)[:, 0:NT, 3:4],
                slot_tokall[:].rearrange("p (t q) -> p t q", q=8)[:, 0:NT, e:e + 1]))
            k.w("tensor", e_pm)
            k.w("tensor", e_meta)
            if st_ev["psml_free"] is not None:
                k.w("tensor", st_ev["psml_free"])
            for tc in range(NT):
                mev = k.e("tensor", lambda E, tc=tc: E.matmul(
                    psml[0:4, 0:CAP], meta[:, tc * 4:(tc + 1) * 4],
                    p_mx[:, tc * CAP:(tc + 1) * CAP],
                    start=(tc == 0), stop=(tc == NT - 1)),
                    inc="mm" if tc == NT - 1 else None)
            st_ev["meta_mm_prev"] = mev
            k.w("vector", mev)
            e_m32 = k.e("vector", lambda E: E.tensor_copy(
                meta32[0:4, 0:CAP], psml[0:4, 0:CAP]))
            k.w("tensor", e_m32)
            for g in range(NG):
                tev = k.e("tensor", lambda E, g=g: E.transpose(
                    psml[0:128, g * 4:(g + 1) * 4],
                    meta32[0:4, g * 128:(g + 1) * 128],
                    identf[0:4, 0:4]), inc="mm" if g == NG - 1 else None)
            k.w("vector", tev)
            k.e("vector", lambda E: E.tensor_copy(
                mtt[:, 0:4 * NG], psml[:, 0:4 * NG]))
            k.e("vector", lambda E: E.scalar_tensor_tensor(
                ids1f[:, 0:NG], mtt[:, 0:4 * NG:4], 32.0,
                mtt[:, 1:4 * NG:4], ALU.mult, ALU.add))
            k.e("vector", lambda E: E.tensor_single_scalar(
                eqf[:, 0:NG], ids1f[:, 0:NG], cp(CHALF), ALU.is_lt))
            k.e("vector", lambda E, mt=mt: E.tensor_scalar_add(
                idxf[:, 0:NG], ids1f[:, 0:NG], float(mt * M - 1)))
            k.e("vector", lambda E: E.scalar_tensor_tensor(
                idxf[:, 0:NG], eqf[:, 0:NG], float(tokens + 1), idxf[:, 0:NG],
                ALU.mult, ALU.add))
            e_idx = k.e("vector", lambda E, e=e: E.tensor_copy(
                idx_all[:, e * NG:(e + 1) * NG], idxf[:, 0:NG]))
            k.e("vector", lambda E: E.scalar_tensor_tensor(
                idxf[:, 0:NG], ids1f[:, 0:NG], 2.0, mtt[:, 3:4 * NG:4],
                ALU.mult, ALU.add))
            k.e("vector", lambda E, mt=mt: E.tensor_scalar_add(
                idxf[:, 0:NG], idxf[:, 0:NG], float(2 * mt * M - 2)))
            k.e("vector", lambda E: E.scalar_tensor_tensor(
                idxf[:, 0:NG], eqf[:, 0:NG], float(2 * tokens + 4), idxf[:, 0:NG],
                ALU.mult, ALU.add))
            k.e("vector", lambda E, e=e: E.tensor_copy(
                ixs_all[:, e * NG:(e + 1) * NG], idxf[:, 0:NG]))
            e_sg = k.e("vector", lambda E, e=e: E.tensor_copy(
                sg_all[:, e * NG:(e + 1) * NG], mtt[:, 2:4 * NG:4]))
            st_ev["psml_free"] = e_sg
            st_ev["tbl_evt"][e] = e_sg

        def stage_D(mt, aof_done):
            D_end = None
            acc_evts = []
            it = [0]
            for j in range(2):
                buf, rdy = slabs.load(w2d(woutT_d, j * 512))
                k.w("tensor", rdy)
                if j == 0:
                    k.w("tensor", aof_done)
                last_mm = None
                for o in range(4):
                    for n in range(NN):
                        bi, bsl = mmB.acq(cols=NW)
                        for c in range(8):
                            ev = k.e("tensor", lambda E, b=buf, c=c, o=o, n=n, bsl=bsl: E.matmul(
                                bsl, wsl[b][:, c * 512 + o * 128:c * 512 + (o + 1) * 128],
                                aof[:, c * M + n * 512:c * M + n * 512 + NW],
                                start=(c == 0), stop=(c == 7)),
                                inc="mm" if c == 7 else None)
                        last_mm = ev
                        D_end = ev
                        oc = j * 4 + o
                        k.w("scalar", ev)
                        if j == 0 and o == 0 and n == 0:
                            if st_ev["H_mm_end"] is not None:
                                k.w("scalar", st_ev["H_mm_end"])
                            if mt == 0:
                                k.w("scalar", e_bo)
                        e_s = k.e("scalar", lambda E, bi=bi, oc=oc, n=n: E.activation(
                            acc[:, oc * M + n * 512:oc * M + n * 512 + NW],
                            mmB.sl(bi, 0, NW), AF.Identity,
                            bias=bout_sb[:, oc:oc + 1]), inc="sc")
                        mmB.rel(bi, e_s)
                        acc_evts.append(e_s)
                        if it[0] % 2 == 0 and it[0] // 2 < EXP:
                            table_build(mt, it[0] // 2)
                        it[0] += 1
                slabs.free_evt[buf] = last_mm
            st_ev["D_mm_end"] = D_end
            return acc_evts[-1]

        def stage_E(mt):
            srow_done = None
            for tt in range(NT):
                slot = tt % 2
                e_nm = k.e("vector", lambda E, tt=tt: E.reduce_max(
                    cp(CNRM), logits[:, tt * 8:(tt + 1) * 8], axis=AX, negate=True),
                    inc="ev")
                k.w("scalar", e_nm)
                e_p = k.e("scalar", lambda E, tt=tt: E.activation(
                    p32[:, 0:8],
                    logits[:, tt * 8:(tt + 1) * 8], AF.Exp, bias=cp(CNRM),
                    accum_out=cp(CRSUM)), inc="sc")
                k.w("vector", e_p)
                e_rc = k.e("vector", lambda E: E.reciprocal(cp(CRDEN), cp(CRSUM)),
                           inc="ev")
                k.w("vector", e_rc)
                k.e("vector", lambda E: E.tensor_scalar_mul(
                    sm32[:, 0:8], p32[:, 0:8], cp(CRDEN)))
                k.e("vector", lambda E: E.reduce_max(cp(CT1), sm32[:, 0:8], axis=AX))
                k.e("vector", lambda E: E.tensor_single_scalar(
                    sm32[:, 8:16], sm32[:, 0:8], cp(CT1), ALU.is_ge))
                k.e("vector", lambda E: E.scalar_tensor_tensor(
                    sm32[:, 16:24], sm32[:, 8:16], -1.0e30, sm32[:, 0:8],
                    ALU.mult, ALU.add))
                k.e("vector", lambda E: E.reduce_max(cp(CT2), sm32[:, 16:24], axis=AX))
                k.e("vector", lambda E: E.tensor_single_scalar(
                    sm32[:, 8:16], sm32[:, 0:8], cp(CT2), ALU.is_ge))
                e_s = k.e("vector", lambda E, tt=tt: E.tensor_mul(
                    s_tokall[:, tt * 8:tt * 8 + 8], sm32[:, 0:8], sm32[:, 8:16]))
                e_m = k.e("vector", lambda E, tt=tt: E.tensor_single_scalar(
                    m_tokall[:, tt * 8:tt * 8 + 8],
                    s_tokall[:, tt * 8:tt * 8 + 8],
                    zerocol[:, 0:1], ALU.is_gt))
                k.e("vector", lambda E, tt=tt: E.tensor_tensor_scan(
                    slot_tokall[:, tt * 8:tt * 8 + 8],
                    m_tokall[:, tt * 8:tt * 8 + 8],
                    m_tokall[:, tt * 8:tt * 8 + 8], 0.0, ALU.add, ALU.bypass))
                k.e("vector", lambda E, tt=tt: E.tensor_sub(
                    slot_tokall[:, tt * 8:tt * 8 + 8],
                    slot_tokall[:, tt * 8:tt * 8 + 8],
                    m_tokall[:, tt * 8:tt * 8 + 8]))
                k.w("tensor", e_m)
                if st_ev["psml_free"] is not None:
                    k.w("tensor", st_ev["psml_free"])
                ev = k.e("tensor", lambda E, tt=tt: E.transpose(
                    psml[0:8, 0:128], m_tokall[:, tt * 8:tt * 8 + 8],
                    identf[:]), inc="mm")
                k.w("vector", ev)
                e_r = k.e("vector", lambda E, tt=tt: E.tensor_copy(
                    m_em8[0:8, tt * 128:(tt + 1) * 128], psml[0:8, 0:128]))
                st_ev["psml_free"] = e_r
            k.e("vector", lambda E: E.tensor_tensor_scan(
                rk_em8[0:8, :], m_em8[0:8, :], m_em8[0:8, :], 0.0,
                ALU.add, ALU.bypass))
            e_r0 = k.e("vector", lambda E: E.tensor_sub(
                rk_em8[0:8, :], rk_em8[0:8, :], m_em8[0:8, :]))
            rt_done = None
            for tt in range(NT):
                k.w("tensor", e_r0)
                k.w("tensor", st_ev["psml_free"])
                ev = k.e("tensor", lambda E, tt=tt: E.transpose(
                    psml[0:128, 0:8], rk_em8[0:8, tt * 128:(tt + 1) * 128],
                    identf[0:8, 0:8]), inc="mm")
                k.w("vector", ev)
                rt_done = k.e("vector", lambda E, tt=tt: E.tensor_copy(
                    r_tokall[:, tt * 8:(tt + 1) * 8], psml[0:128, 0:8]))
                st_ev["psml_free"] = rt_done
            return rt_done

        def stage_F(mt, routing_done, xnf_done):
            NG = CAP // 128
            sd_after = {}
            sxn_gbuf = [sxn_g, scr[:, 49 * CAP:57 * CAP]]

            def prologue1(e):
                k.w("gpsimd", st_ev["tbl_evt"][e])
                k.wmax("gpsimd", st_ev["xt_evts"])
                if st_ev["xng_free"] is not None:
                    k.w("gpsimd", st_ev["xng_free"])
                if e == 0:
                    k.w("gpsimd", st_ev["attn_mm_end"])
                for g in range(NG):
                    k.e("gpsimd", lambda E, g=g, e=e: E.indirect_dma_start(
                        out=xng_tok[:, g * 1024:(g + 1) * 1024], out_offset=None,
                        in_=xt_d,
                        in_offset=IndirectOffsetOnAxis(
                            ap=idx_all[:, e * NG + g:e * NG + g + 1], axis=0),
                        bounds_check=bc_reg, oob_is_err=False), inc="gd", amt=16)
                return ("gd", k.cnt["gd"])

            def prologue2(e, gd_tot):
                slot = e % 2
                sxn = sxn_gbuf[slot]
                k.w("vector", gd_tot)
                if st_ev["sxng_rd_end"] is not None:
                    k.w("vector", st_ev["sxng_rd_end"])
                if e == 0:
                    k.w("vector", st_ev["G_mm_end"])
                e_sc = None
                for g in range(NG):
                    e_sc = k.e("vector", lambda E, g=g, e=e: E.tensor_scalar_mul(
                        sxng_tok[:, g * 1024:(g + 1) * 1024],
                        xng_tok[:, g * 1024:(g + 1) * 1024],
                        sg_all[:, e * NG + g:e * NG + g + 1]))
                st_ev["xng_free"] = e_sc
                k.w("tensor", e_sc)
                if e == 0 and mt == 0:
                    k.w("tensor", e_idb)
                last_sxn_wr = None
                for dc in range(8):
                    bi, _ = btB.acq()
                    for g in range(NG):
                        tev = k.e("tensor", lambda E, dc=dc, g=g, bi=bi: E.transpose(
                            pbt[:, bi * 1024 + g * 128:bi * 1024 + (g + 1) * 128],
                            sxng_tok[:, g * 1024 + dc * 128:g * 1024 + (dc + 1) * 128],
                            identb[:]), inc="mm" if g == NG - 1 else None)
                    k.w("vector", tev)
                    if dc == 0:
                        if e == 0:
                            k.w("vector", st_ev["attn_mm_end"])
                        k.w("vector", st_ev["sxn_rd_end2"][slot])
                    last_sxn_wr = k.e("vector", lambda E, dc=dc, bi=bi, sxn=sxn: E.tensor_copy(
                        sxn[:, dc * CAP:(dc + 1) * CAP],
                        pbt[:, bi * 1024:bi * 1024 + CAP]))
                    btB.rel(bi, last_sxn_wr)
                st_ev["sxng_rd_end"] = tev
                return last_sxn_wr

            prep = {}
            w1pre = {0: [slabs.load(w3d(w1T_d, 0, 0))]}
            gpre = []
            gd0 = prologue1(0)
            prep[0] = prologue2(0, gd0)
            for e in range(EXP):
                slot = e % 2
                sxn = sxn_gbuf[slot]
                last_sxn_wr = prep.pop(e)
                h_done = None
                pre1 = w1pre.pop(e, [])
                for j in range(2):
                    buf, rdy = pre1[j] if j < len(pre1) else slabs.load(w3d(w1T_d, e, j * 512))
                    k.w("tensor", rdy)
                    if j == 0:
                        k.w("tensor", last_sxn_wr)
                    for o in range(4):
                        oc = j * 4 + o
                        bi, bsl = mmB.acq(cols=CAP)
                        for c in range(8):
                            ev = k.e("tensor", lambda E, b=buf, c=c, o=o, bsl=bsl, sxn=sxn: E.matmul(
                                bsl, wsl[b][:, c * 512 + o * 128:c * 512 + (o + 1) * 128],
                                sxn[:, c * CAP:c * CAP + CAP],
                                start=(c == 0), stop=(c == 7)),
                                inc="mm" if c == 7 else None)
                        k.w("scalar", ev)
                        if mt == 0 and e == 0 and oc == 0:
                            k.w("scalar", e_b1)
                            k.w("scalar", e_b2)
                            k.w("scalar", e_b3)
                            k.w("scalar", st_ev["attn_mm_end"])
                        e_h = k.e("scalar", lambda E, bi=bi, oc=oc, e=e: E.activation(
                            h_g[:, oc * CAP:oc * CAP + CAP],
                            mmB.sl(bi, 0, CAP), AF.Silu,
                            bias=b1_sb[:, oc * 8 + e:oc * 8 + e + 1]), inc="sc")
                        mmB.rel(bi, e_h)
                        h_done = e_h
                    slabs.free_evt[buf] = ev
                w3bufs = {}
                w2bufs = {}
                prod_done = None
                gd_next = None
                for oc in range(8):
                    j = oc // 4
                    if j not in w3bufs:
                        w3bufs[j] = slabs.load(w3d(w3T_d, e, j * 512))
                        w2bufs[j] = slabs.load(w3d(w2T_d, e, j * 512))
                    b3buf, rdy3 = w3bufs[j]
                    b2buf, rdy2 = w2bufs[j]
                    bi3, bsl3 = mmB.acq(cols=CAP)
                    if oc % 4 == 0:
                        k.w("tensor", rdy3)
                    for c in range(8):
                        ev3 = k.e("tensor", lambda E, b=b3buf, c=c, oc=oc, bsl=bsl3, sxn=sxn: E.matmul(
                            bsl, wsl[b][:, c * 512 + (oc % 4) * 128:c * 512 + (oc % 4 + 1) * 128],
                            sxn[:, c * CAP:c * CAP + CAP],
                            start=(c == 0), stop=(c == 7)),
                            inc="mm" if c == 7 else None)
                    bi2, bsl2 = mmB.acq(cols=CAP)
                    if oc % 4 == 0:
                        k.w("tensor", rdy2)
                    if oc == 0:
                        k.w("tensor", h_done)
                    for c in range(8):
                        ev2 = k.e("tensor", lambda E, b=b2buf, c=c, oc=oc, bsl=bsl2: E.matmul(
                            bsl, wsl[b][:, c * 512 + (oc % 4) * 128:c * 512 + (oc % 4 + 1) * 128],
                            h_g[:, c * CAP:c * CAP + CAP],
                            start=(c == 0), stop=(c == 7)),
                            inc="mm" if c == 7 else None)
                    i2 = oc % 2
                    k.w("scalar", ev3)
                    if st_ev["tmp_free"][i2] is not None:
                        k.w("scalar", st_ev["tmp_free"][i2])
                    e_x3 = k.e("scalar", lambda E, bi3=bi3, oc=oc, e=e, i2=i2: E.activation(
                        tmp3[i2][:, 0:CAP], mmB.sl(bi3, 0, CAP), AF.Identity,
                        bias=b3_sb[:, oc * 8 + e:oc * 8 + e + 1]), inc="sc")
                    mmB.rel(bi3, e_x3)
                    k.w("scalar", ev2)
                    e_x2 = k.e("scalar", lambda E, bi2=bi2, oc=oc, e=e, i2=i2: E.activation(
                        tmp2[i2][:, 0:CAP], mmB.sl(bi2, 0, CAP), AF.Identity,
                        bias=b2_sb[:, oc * 8 + e:oc * 8 + e + 1]), inc="sc")
                    mmB.rel(bi2, e_x2)
                    k.w("vector", e_x3)
                    k.w("vector", e_x2)
                    if oc == 0 and e == 0:
                        k.w("vector", st_ev["D_mm_end"])
                    e_pr = k.e("vector", lambda E, i2=i2: E.tensor_mul(
                        prodtile[:, 0:CAP], tmp3[i2][:, 0:CAP], tmp2[i2][:, 0:CAP]))
                    st_ev["tmp_free"][i2] = e_pr
                    k.w("tensor", e_pr)
                    bi, _ = btB.acq()
                    for g in range(NG):
                        tev = k.e("tensor", lambda E, g=g, bi=bi: E.transpose(
                            pbt[:, bi * 1024 + g * 128:bi * 1024 + (g + 1) * 128],
                            prodtile[:, g * 128:(g + 1) * 128],
                            identb[:]), inc="mm" if g == NG - 1 else None)
                    k.w("vector", tev)
                    if oc == 0:
                        if e == 0:
                            if st_ev["sd_prev"] is not None:
                                k.w("vector", st_ev["sd_prev"])
                        elif sd_after.get(e - 1) is not None:
                            k.w("vector", sd_after[e - 1])
                    prod_done = k.e("vector", lambda E, oc=oc, bi=bi: E.tensor_copy(
                        prodtok[:].rearrange("p (g n) -> p g n", g=NG)
                        [:, :, oc * 128:(oc + 1) * 128],
                        pbt[:, bi * 1024:bi * 1024 + CAP]
                        .rearrange("p (g n) -> p g n", g=NG)))
                    btB.rel(bi, prod_done)
                    if oc % 4 == 3:
                        slabs.free_evt[b3buf] = ev3
                        slabs.free_evt[b2buf] = ev2
                    if e + 1 < EXP:
                        if oc == 1:
                            gd_next = prologue1(e + 1)
                        elif oc == 3:
                            w1pre[e + 1] = [slabs.load(w3d(w1T_d, e + 1, 0))]
                        elif oc == 4:
                            prep[e + 1] = prologue2(e + 1, gd_next)
                    elif oc == 5:
                        gpre.append(slabs.load(w2d(sw1T_d, 0)))
                st_ev["sxn_rd_end"] = ev3
                st_ev["sxn_rd_end2"][slot] = ev3
                k.w("gpsimd", prod_done)
                if mt == 0 and e == 0:
                    k.w("gpsimd", e_ztot)
                for g in range(NG):
                    k.e("gpsimd", lambda E, g=g, e=e: E.indirect_dma_start(
                        out=yt_d,
                        out_offset=IndirectOffsetOnAxis(
                            ap=ixs_all[:, e * NG + g:e * NG + g + 1], axis=0),
                        in_=prodtok[:, g * 1024:(g + 1) * 1024], in_offset=None,
                        bounds_check=bc2_reg, oob_is_err=False), inc="sd", amt=16)
                sd_after[e] = ("sd", k.cnt["sd"])
            st_ev["sd_prev"] = ("sd", k.cnt["sd"])
            st_ev["gd_prev"] = ("gd", k.cnt["gd"])
            return ("sd", k.cnt["sd"]), gpre

        def stage_G(mt, sd_tot, gpre):
            zi = [0]
            zz_done = None
            for j in range(4):
                if j == 0 and len(gpre) == 1:
                    b1buf, rdy1 = gpre[0]
                    b2buf, rdy2 = slabs.load(w2d(sw2T_d, 0))
                else:
                    b1buf, rdy1 = slabs.load(w2d(sw1T_d, j * 512))
                    b2buf, rdy2 = slabs.load(w2d(sw2T_d, j * 512))
                for it_in in range(4):
                    it = j * 4 + it_in
                    for n in range(NN):
                        bia, bsla = mmB.acq(cols=NW)
                        if it_in == 0 and n == 0:
                            k.w("tensor", rdy1)
                        for c in range(8):
                            ev1 = k.e("tensor", lambda E, b=b1buf, c=c, it_in=it_in, n=n, bsl=bsla: E.matmul(
                                bsl, wsl[b][:, c * 512 + it_in * 128:c * 512 + (it_in + 1) * 128],
                                xn_f[:, c * M + n * 512:c * M + n * 512 + NW],
                                start=(c == 0), stop=(c == 7)),
                                inc="mm" if c == 7 else None)
                        bib, bslb = mmB.acq(cols=NW)
                        if it_in == 0 and n == 0:
                            k.w("tensor", rdy2)
                        for c in range(8):
                            ev2 = k.e("tensor", lambda E, b=b2buf, c=c, it_in=it_in, n=n, bsl=bslb: E.matmul(
                                bsl, wsl[b][:, c * 512 + it_in * 128:c * 512 + (it_in + 1) * 128],
                                xn_f[:, c * M + n * 512:c * M + n * 512 + NW],
                                start=(c == 0), stop=(c == 7)),
                                inc="mm" if c == 7 else None)
                        i2 = zi[0] % 2
                        k.w("scalar", ev1)
                        if st_ev["z1t_free"][i2] is not None:
                            k.w("scalar", st_ev["z1t_free"][i2])
                        e_z1 = k.e("scalar", lambda E, bia=bia, i2=i2: E.activation(
                            z1t[i2][:, 0:NW], mmB.sl(bia, 0, NW), AF.Silu), inc="sc")
                        mmB.rel(bia, e_z1)
                        k.w("vector", ev2)
                        k.w("vector", e_z1)
                        if it == 0 and n == 0:
                            k.w("vector", st_ev["D_mm_end"])
                            k.w("vector", sd_tot)
                        e_zz = k.e("vector", lambda E, bib=bib, it=it, n=n, i2=i2: E.tensor_mul(
                            zz[:, it * M + n * 512:it * M + n * 512 + NW],
                            z1t[i2][:, 0:NW], mmB.sl(bib, 0, NW)), inc="ev")
                        mmB.rel(bib, e_zz)
                        st_ev["z1t_free"][i2] = e_zz
                        zz_done = e_zz
                        zi[0] += 1
                slabs.free_evt[b1buf] = ev1
                slabs.free_evt[b2buf] = ev2
            if g_level < 2:
                st_ev["G_mm_end"] = ev2
                return zz_done
            G_end = None
            acc_done = None
            for j in range(2):
                br0, rdy0 = slabs.load(wrow(sw3T_d, 0, j * 512))
                br1, rdy1 = slabs.load(wrow(sw3T_d, 1024, j * 512))
                for o in range(4):
                    oc = j * 4 + o
                    for n in range(NN):
                        bi, bsl = mmB.acq(cols=NW)
                        if o == 0 and n == 0:
                            k.w("tensor", rdy0)
                            k.w("tensor", rdy1)
                            k.w("tensor", zz_done)
                        for half, bb in ((0, br0), (1, br1)):
                            for c in range(8):
                                i = half * 8 + c
                                ev = k.e("tensor", lambda E, b=bb, c=c, o=o, n=n, i=i, bsl=bsl: E.matmul(
                                    bsl, wsl[b][:, c * 512 + o * 128:c * 512 + (o + 1) * 128],
                                    zz[:, i * M + n * 512:i * M + n * 512 + NW],
                                    start=(i == 0), stop=(i == 15)),
                                    inc="mm" if i == 15 else None)
                        G_end = ev
                        k.w("vector", ev)
                        e_a = k.e("vector", lambda E, bi=bi, oc=oc, n=n: E.tensor_add(
                            acc[:, oc * M + n * 512:oc * M + n * 512 + NW],
                            acc[:, oc * M + n * 512:oc * M + n * 512 + NW],
                            mmB.sl(bi, 0, NW)), inc="ev")
                        mmB.rel(bi, e_a)
                        acc_done = e_a
                slabs.free_evt[br0] = G_end
                slabs.free_evt[br1] = G_end
            st_ev["G_mm_end"] = G_end
            return acc_done

        def stage_H(mt, acc_done, sd_tot):
            H_end = None
            for tt in range(NT):
                row0 = mt * M + tt * 128
                slot = tt % 2
                k.w("sync", st_ev["xfin_free"][slot])
                ld = k.e("sync", lambda E, r=row0, s=slot: E.dma_start(
                    out=xfin[s][:], in_=x_d[r:r + 128, :]), inc=f"xf{slot}", amt=16)
                if tt == 0:
                    k.w("sync", sd_tot)
                if st_ev.get("ytk_free", [None, None])[slot] is not None:
                    k.w("sync", st_ev["ytk_free"][slot])
                ld_y = k.e("sync", lambda E, r=row0, s=slot: E.dma_start(
                    out=ytk_w[s][:].rearrange("p (two n) -> p two n", two=2),
                    in_=yt_d.rearrange("(t two) n -> t two n", two=2)[r:r + 128]),
                    inc=f"yd{slot}", amt=16)
                k.w("tensor", acc_done)
                k.w("vector", ld)
                k.w("vector", ld_y)
                if st_ev["otok_store"][slot] is not None:
                    k.w("vector", st_ev["otok_store"][slot])
                for half in range(2):
                    bi, _ = trB.acq()
                    for cc in range(4):
                        c = half * 4 + cc
                        ev = k.e("tensor", lambda E, c=c, tt=tt, bi=bi, cc=cc: E.transpose(
                            trB.sl(bi, cc * 128, cc * 128 + 128),
                            acc[:, c * M + tt * 128:c * M + (tt + 1) * 128],
                            identf[:]), inc="mm" if cc == 3 else None)
                    H_end = ev
                    k.w("vector", ev)
                    k.e("vector", lambda E, bi=bi, h=half, s=slot: E.tensor_add(
                        otok[s][:, h * 512:(h + 1) * 512], trB.sl(bi, 0, 512),
                        xfin[s][:, h * 512:(h + 1) * 512]))
                    k.e("vector", lambda E, h=half, s=slot: E.tensor_add(
                        otok[s][:, h * 512:(h + 1) * 512],
                        otok[s][:, h * 512:(h + 1) * 512],
                        ytk_w[s][:, h * 512:(h + 1) * 512]), inc="ev")
                    e_o = k.e("vector", lambda E, h=half, s=slot: E.tensor_add(
                        otok[s][:, h * 512:(h + 1) * 512],
                        otok[s][:, h * 512:(h + 1) * 512],
                        ytk_w[s][:, 1024 + h * 512:1024 + (h + 1) * 512]), inc="ev")
                    trB.rel(bi, e_o)
                st_ev["xfin_free"][slot] = e_o
                st_ev.setdefault("ytk_free", [None, None])[slot] = e_o
                k.w("sync", e_o)
                st_out = k.e("sync", lambda E, r=row0, s=slot: E.dma_start(
                    out=y_d[r:r + 128, :], in_=otok[s][:]), inc=f"od{slot}", amt=16)
                st_ev["otok_store"][slot] = st_out
                n_store[0] += 1
            st_ev["H_mm_end"] = H_end

        for mt in range(NMT):
            xnf_done = qk_done = v_done = aof_done = rt_done = acc_done = None
            sd_tot = None
            pre = []
            if "B" in stages:
                pre = preload_B(mt)
            if "A" in stages:
                xnf_done = stage_A(mt)
            if "B" in stages:
                qk_done, v_done = stage_B(mt, xnf_done, pre)
            if "C" in stages:
                aof_done = stage_C(mt, qk_done, v_done)
            if "E" in stages:
                rt_done = stage_E(mt)
                st_ev["routing_done"] = rt_done
            if "D" in stages:
                stage_D(mt, aof_done)
            gpre = []
            if "F" in stages:
                sd_tot, gpre = stage_F(mt, rt_done, xnf_done)
            if "G" in stages:
                acc_done = stage_G(mt, sd_tot, gpre)
            if "H" in stages:
                stage_H(mt, acc_done, sd_tot)

        if debug:
            k.w("gpsimd", ("od0", k.cnt["od0"]))
            k.w("gpsimd", ("od1", k.cnt["od1"]))
            k.e("gpsimd", lambda E: E.dma_start(out=dbg["dbg_xnf"], in_=xn_f[:]),
                inc="dbg", amt=16)
            k.e("gpsimd", lambda E: E.dma_start(out=dbg["dbg_logits"], in_=logits[:]),
                inc="dbg", amt=16)
            k.e("gpsimd", lambda E: E.dma_start(out=dbg["dbg_srow"], in_=s_tokall[:]),
                inc="dbg", amt=16)
            k.e("gpsimd", lambda E: E.dma_start(out=dbg["dbg_qf"], in_=scr[:, 0:SM]),
                inc="dbg", amt=16)
            k.e("gpsimd", lambda E: E.dma_start(out=dbg["dbg_aof"], in_=scr[:, 3 * SM:4 * SM]),
                inc="dbg", amt=16)
            k.e("gpsimd", lambda E: E.dma_start(out=dbg["dbg_acc"], in_=acc[:]),
                inc="dbg", amt=16)
            k.e("gpsimd", lambda E: E.dma_start(out=dbg["dbg_hsb"], in_=scr[:, SM:2 * SM]),
                inc="dbg", amt=16)
            k.e("gpsimd", lambda E: E.dma_start(out=dbg["dbg_zz"], in_=scr[:, 2 * SM:4 * SM]),
                inc="dbg", amt=16)
            k.w("gpsimd", ("dbg", k.cnt["dbg"]))

        k.w("sync", ("od0", k.cnt["od0"]))
        k.w("sync", ("od1", k.cnt["od1"]))

        with nc.Block() as block:
            @block.tensor
            def _(E):
                for f in k.plan["tensor"]:
                    f(E)

            @block.vector
            def _(E):
                for f in k.plan["vector"]:
                    f(E)

            @block.scalar
            def _(E):
                for f in k.plan["scalar"]:
                    f(E)

            @block.gpsimd
            def _(E):
                for f in k.plan["gpsimd"]:
                    f(E)

            @block.sync
            def _(E):
                for f in k.plan["sync"]:
                    f(E)

    return nc


def make_inmaps(inputs, tokens_per_core, n_cores):
    x = np.ascontiguousarray(np.asarray(inputs["L0_attn_x"], dtype=np.float32)
                             .reshape(-1, D))
    wqkv = np.ascontiguousarray(np.asarray(inputs["L0_attn_W_qkv"], np.float32))
    woutT = np.ascontiguousarray(np.asarray(inputs["L0_attn_W_out"], np.float32).T)
    bout = np.ascontiguousarray(np.asarray(inputs["L0_attn_b_out"], np.float32))
    gwT = np.ascontiguousarray(np.asarray(inputs["L0_ffn_gate_w"], np.float32).T)
    w1T = np.ascontiguousarray(np.asarray(inputs["L0_ffn_w1"], np.float32).transpose(0, 2, 1))
    w2T = np.ascontiguousarray(np.asarray(inputs["L0_ffn_w2"], np.float32).transpose(0, 2, 1))
    w3T = np.ascontiguousarray(np.asarray(inputs["L0_ffn_w3"], np.float32).transpose(0, 2, 1))
    b1T = np.ascontiguousarray(np.asarray(inputs["L0_ffn_b1"], np.float32).T)
    b2T = np.ascontiguousarray(np.asarray(inputs["L0_ffn_b2"], np.float32).T)
    b3T = np.ascontiguousarray(np.asarray(inputs["L0_ffn_b3"], np.float32).T)
    sw1T = np.ascontiguousarray(np.asarray(inputs["L0_ffn_sw1"], np.float32).T)
    sw2T = np.ascontiguousarray(np.asarray(inputs["L0_ffn_sw2"], np.float32).T)
    sw3T = np.ascontiguousarray(np.asarray(inputs["L0_ffn_sw3"], np.float32).T)
    shared = dict(wqkv=wqkv, woutT=woutT, bout=bout, gwT=gwT, w1T=w1T, w2T=w2T,
                  w3T=w3T, b1T=b1T, b2T=b2T, b3T=b3T, sw1T=sw1T, sw2T=sw2T,
                  sw3T=sw3T)
    in_maps = []
    for c in range(n_cores):
        m = dict(shared)
        m["x"] = np.ascontiguousarray(
            x[c * tokens_per_core:(c + 1) * tokens_per_core])
        in_maps.append(m)
    return in_maps


def kernel(**inputs):
    n_cores = 8
    tokens_per_core = 4096
    in_maps = make_inmaps(inputs, tokens_per_core, n_cores)
    nc = build_nc(tokens_per_core)
    res = run_bass_kernel_spmd(nc, in_maps, core_ids=list(range(n_cores)))
    out = np.concatenate([r["y"] for r in res.results], axis=0)
    return out.reshape(16, 2048, D).astype(np.float32)


# version 14
# speedup vs baseline: 1.0995x; 1.0836x over previous
import contextlib
import numpy as np

import concourse.bass as bass
import concourse.mybir as mybir
from concourse.bass import IndirectOffsetOnAxis
from concourse.bass_utils import run_bass_kernel_spmd

F32 = mybir.dt.float32
BF16 = mybir.dt.bfloat16
AX = mybir.AxisListType.X
AF = mybir.ActivationFunctionType
ALU = mybir.AluOpType

D = 1024
EXP = 8
II = 2048
SCALE = 1.0 / 32.0
NBUF = 3
CAP = 512
ENGS = ("tensor", "vector", "scalar", "sync", "gpsimd")


class K:

    def __init__(self):
        self.plan = {e: [] for e in ENGS}
        self.cnt = {}
        self.sems = {}

    def sem(self, name, stack, nc):
        self.sems[name] = stack.enter_context(nc.semaphore(name))
        self.cnt[name] = 0

    def w(self, eng, evt):
        if evt is None:
            return
        s, v = evt
        sem = self.sems[s]
        self.plan[eng].append(lambda E, sem=sem, v=v: E.wait_ge(sem, v))

    def wmax(self, eng, evts):
        by = {}
        for e in evts:
            if e is None:
                continue
            by[e[0]] = max(by.get(e[0], 0), e[1])
        for s, v in by.items():
            self.w(eng, (s, v))

    def e(self, eng, fn, inc=None, amt=1):
        if eng == "vector":
            self.w("vector", ("ev", self.cnt["ev"]))
            if inc is None:
                inc, amt = "ev", 1
        if inc is None:
            self.plan[eng].append(fn)
            return None
        self.cnt[inc] += amt
        v = self.cnt[inc]
        sem = self.sems[inc]
        self.plan[eng].append(lambda E, fn=fn, sem=sem, amt=amt: fn(E).then_inc(sem, amt))
        return (inc, v)


class Banks:

    def __init__(self, k, ap, nbanks, width):
        self.k, self.ap, self.n, self.w = k, ap, nbanks, width
        self.free = [[] for _ in range(nbanks)]
        self.rr = 0

    def acq(self, cols=None):
        i = self.rr % self.n
        self.rr += 1
        self.k.wmax("tensor", self.free[i])
        self.free[i] = []
        sl = self.ap[:, i * self.w:(i + 1) * self.w if cols is None else i * self.w + cols]
        return i, sl

    def sl(self, i, c0, c1):
        return self.ap[:, i * self.w + c0:i * self.w + c1]

    def rel(self, i, evt):
        self.free[i].append(evt)


def build_nc(tokens, debug=False, stages='ABCDEFGH', a_level=4, f_level=9, g_level=9):
    M = min(1024, tokens)
    NMT = tokens // M
    NT = M // 128
    NN = max(1, M // 512)
    NW = min(512, M)
    SM = 8 * M
    assert tokens >= 1024, "sparse MoE layout assumes M=1024 macro-tiles"

    nc = bass.Bass("TRN2", target_bir_lowering=False,
                   detect_race_conditions=False)
    k = K()

    x_d = nc.dram_tensor("x", [tokens, D], F32, kind="ExternalInput").ap()
    y_d = nc.dram_tensor("y", [tokens, D], F32, kind="ExternalOutput").ap()
    wqkv_d = nc.dram_tensor("wqkv", [D, 3 * D], F32, kind="ExternalInput").ap()
    woutT_d = nc.dram_tensor("woutT", [D, D], F32, kind="ExternalInput").ap()
    gwT_d = nc.dram_tensor("gwT", [D, EXP], F32, kind="ExternalInput").ap()
    w1T_d = nc.dram_tensor("w1T", [EXP, D, D], F32, kind="ExternalInput").ap()
    w2T_d = nc.dram_tensor("w2T", [EXP, D, D], F32, kind="ExternalInput").ap()
    w3T_d = nc.dram_tensor("w3T", [EXP, D, D], F32, kind="ExternalInput").ap()
    sw1T_d = nc.dram_tensor("sw1T", [D, II], F32, kind="ExternalInput").ap()
    sw2T_d = nc.dram_tensor("sw2T", [D, II], F32, kind="ExternalInput").ap()
    sw3T_d = nc.dram_tensor("sw3T", [II, D], F32, kind="ExternalInput").ap()
    b1T_d = nc.dram_tensor("b1T", [D, EXP], F32, kind="ExternalInput").ap()
    b2T_d = nc.dram_tensor("b2T", [D, EXP], F32, kind="ExternalInput").ap()
    b3T_d = nc.dram_tensor("b3T", [D, EXP], F32, kind="ExternalInput").ap()
    bout_d = nc.dram_tensor("bout", [D], F32, kind="ExternalInput").ap()
    xt_d = nc.dram_tensor("xt_scr", [tokens, D], BF16, kind="ExternalOutput").ap()
    yt_d = nc.dram_tensor("yt_scr", [2 * tokens, D], BF16, kind="ExternalOutput").ap()
    dbg = {}
    if debug:
        for nm, shp in [("dbg_xnf", [128, SM]), ("dbg_logits", [128, NT * 8]),
                        ("dbg_srow", [128, 8 * NT]), ("dbg_qf", [128, SM]),
                        ("dbg_aof", [128, SM]), ("dbg_acc", [128, SM]),
                        ("dbg_hsb", [128, SM]), ("dbg_zz", [128, 16 * M])]:
            dbg[nm] = nc.dram_tensor(nm, shp, F32, kind="ExternalOutput").ap()

    with contextlib.ExitStack() as st:
        def sb(name, shape, dt):
            return st.enter_context(nc.sbuf_tensor(name, shape, dt))

        def ps(name, shape, dt):
            return st.enter_context(nc.psum_tensor(name, shape, dt))

        scr = sb("scr", [128, 32 * M], BF16)
        qf = scr[:, 0:SM]
        kf = scr[:, SM:2 * SM]
        vtok = scr[:, 2 * SM:2 * SM + NT * 1024]
        aof = scr[:, 3 * SM:4 * SM]
        sxn_g = scr[:, 0:8 * CAP]
        h_g = scr[:, 8 * CAP:16 * CAP]
        p_mx = scr[:, 16 * CAP:16 * CAP + NT * CAP]
        xng_tok = scr[:, 24 * CAP:32 * CAP]
        sxng_tok = scr[:, 32 * CAP:40 * CAP]
        prodtok = scr[:, 40 * CAP:48 * CAP]
        prodtile = [scr[:, 48 * CAP:49 * CAP],
                    scr[:, 57 * CAP:58 * CAP]]
        zero_zone = scr[:, 0:4096]
        h_sb = scr[:, SM:2 * SM]
        zz = scr[:, 2 * SM:4 * SM]
        xn_f = sb("xn_f", [128, SM], BF16)
        acc = sb("acc", [128, SM], F32)
        wsl = [sb(f"wsl{i}", [128, 8 * 512], BF16) for i in range(NBUF)]
        xa = [sb(f"xa{i}", [128, D], F32) for i in range(2)]
        xfin = [sb(f"xfin{i}", [128, D], F32) for i in range(2)]
        otok = [sb(f"otok{i}", [128, D], F32) for i in range(2)]
        xn32 = sb("xn32", [128, D], F32)
        xntok = [sb(f"xntok{i}", [128, D], F32) for i in range(2)]
        s_tokall = sb("s_tokall", [128, 8 * NT], F32)
        slot_tokall = sb("slot_tokall", [128, 8 * NT], F32)
        m_tokall = sb("m_tokall", [128, 8 * NT], F32)
        r_tokall = sb("r_tokall", [128, 8 * NT], F32)
        m_em8 = sb("m_em8", [8, M], BF16)
        rk_em8 = sb("rk_em8", [8, M], F32)
        iotaC = sb("iotaC", [128, CAP], F32)
        idf8 = sb("idf8", [128, 8], mybir.dt.int32)
        idlo_i = sb("idlo_i", [128, 8], mybir.dt.int32)
        c31 = sb("c31", [128, 1], mybir.dt.int32)
        idhi = sb("idhi", [128, 8], BF16)
        idlo = sb("idlo", [128, 8], BF16)
        meta = sb("meta", [128, 32], BF16)
        meta32 = sb("meta32", [4, CAP], F32)
        idx_all = sb("idx_all", [128, 8 * (CAP // 128)], mybir.dt.int32)
        ixs_all = sb("ixs_all", [128, 8 * (CAP // 128)], mybir.dt.int32)
        sg_all = sb("sg_all", [128, 8 * (CAP // 128)], F32)
        ytk_w = [sb(f"ytkw{i}", [128, 2 * D], BF16) for i in range(2)]
        idxf = sb("idxf", [128, 4], F32)
        ids1f = sb("ids1f", [128, 4], F32)
        eqf = sb("eqf", [128, 4], F32)
        mtt = sb("mtt", [128, 16], F32)
        logits = sb("logits", [128, NT * 8], F32)
        sm32 = sb("sm32", [128, 256], F32)
        p32 = sb("p32", [128, 256], F32)
        pn_sb = sb("pn_sb", [128, 256], BF16)
        pt_sb = sb("pt_sb", [128, 256], BF16)
        tmp3 = [sb(f"tmp3{i}", [128, 512], BF16) for i in range(2)]
        tmp2 = [sb(f"tmp2{i}", [128, 512], BF16) for i in range(2)]
        z1t = [sb(f"z1t{i}", [128, 512], BF16) for i in range(2)]
        gw_sb = sb("gw_sb", [128, 8 * EXP], F32)
        b1_sb = sb("b1_sb", [128, 8 * EXP], F32)
        b2_sb = sb("b2_sb", [128, 8 * EXP], F32)
        b3_sb = sb("b3_sb", [128, 8 * EXP], F32)
        bout_sb = sb("bout_sb", [128, 8], F32)
        identf = sb("identf", [128, 128], F32)
        identb = sb("identb", [128, 128], BF16)
        amask = sb("amask", [128, 128], F32)

        zerocol = sb("zerocol", [128, 1], F32)
        colp = sb("colp", [128, 24], F32)
        (CSX, CMEAN, CSXX, CEX2, CM2, CVAR, CSRT, CY0, CT1, CT2, CT3, CRSTD,
         CNRM, CRSUM, CRDEN, C32, CHALF, CT4, CNRM2, CRSUM2, CRDEN2) = range(21)

        def cp(i):
            return colp[:, i:i + 1]

        pmm = ps("pmm", [128, 4 * 512], F32)
        ptr = ps("ptr", [128, 512], F32)
        pbt = ps("pbt", [128, 2048], BF16)
        psml = ps("psml", [128, 512], F32)

        for s in ("wd0", "wd1", "wd2", "ws", "xa0", "xa1", "xf0",
                  "xf1", "od0", "od1", "dbg", "mm", "ev", "sc", "gp",
                  "xt0", "xt1", "gd", "sd", "zd", "yd0", "yd1"):
            k.sem(s, st, nc)

        mmB = Banks(k, pmm, 4, 512)
        trB = Banks(k, ptr, 1, 512)
        btB = Banks(k, pbt, 2, 1024)

        e1 = k.e("gpsimd", lambda E: E.memset(identf[:], 1.0), inc="gp")
        k.w("gpsimd", e1)
        e_idf = k.e("gpsimd", lambda E: E.affine_select(
            identf[:], identf[:], [[-1, 128]], ALU.is_equal, 0.0,
            base=0, channel_multiplier=1), inc="gp")
        e2 = k.e("gpsimd", lambda E: E.memset(amask[:], 0.0), inc="gp")
        k.w("gpsimd", e2)
        e_msk = k.e("gpsimd", lambda E: E.affine_select(
            amask[:], amask[:], [[-1, 128]], ALU.is_ge, -1.0e30,
            base=0, channel_multiplier=1), inc="gp")
        e_ioc = k.e("gpsimd", lambda E: E.iota(
            iotaC[:], [[1, CAP]], base=0, channel_multiplier=0,
            allow_small_or_imprecise_dtypes=True), inc="gp")
        e_idf8 = k.e("gpsimd", lambda E: E.iota(
            idf8[:], [[128, 8]], base=1, channel_multiplier=1), inc="gp")
        k.w("vector", e_idf)
        e_idb = k.e("vector", lambda E: E.tensor_copy(identb[:], identf[:]), inc="ev")
        k.e("vector", lambda E: E.memset(zerocol[:], 0.0))
        k.e("vector", lambda E: E.memset(cp(C32), 32.0))
        k.e("vector", lambda E: E.memset(cp(CHALF), 0.5))
        k.w("vector", e_idf8)
        k.e("vector", lambda E: E.memset(c31[:], 31))
        k.e("vector", lambda E: E.tensor_single_scalar(
            idlo_i[:], idf8[:], c31[:, 0:1], ALU.bitwise_and))
        k.e("vector", lambda E: E.tensor_copy(m_tokall[:, 0:8], idf8[:]))
        k.e("vector", lambda E: E.tensor_copy(m_tokall[:, 8:16], idlo_i[:]))
        k.e("vector", lambda E: E.tensor_copy(idlo[:], idlo_i[:]))
        k.e("vector", lambda E: E.tensor_sub(
            m_tokall[:, 0:8], m_tokall[:, 0:8], m_tokall[:, 8:16]))
        k.e("vector", lambda E: E.tensor_scalar_mul(idhi[:], m_tokall[:, 0:8], 1.0 / 32.0))
        k.e("vector", lambda E: E.tensor_copy(
            meta[:].rearrange("p (t k) -> p t k", k=4)[:, 0:8, 0:1],
            idhi[:, 0:8].rearrange("p (t u) -> p t u", u=1)))
        k.e("vector", lambda E: E.tensor_copy(
            meta[:].rearrange("p (t k) -> p t k", k=4)[:, 0:8, 1:2],
            idlo[:, 0:8].rearrange("p (t u) -> p t u", u=1)))


        def sdma(dst, src):
            return k.e("gpsimd", lambda E, d=dst, s=src: E.dma_start(
                out=d, in_=s, allow_slow_non_contiguous=True), inc="ws", amt=16)

        e_gw = sdma(gw_sb[:].rearrange("p (c e) -> p c e", c=8),
                    gwT_d.rearrange("(c p) e -> p c e", p=128))
        e_b1 = sdma(b1_sb[:].rearrange("p (t e) -> p t e", t=8),
                    b1T_d.rearrange("(t p) e -> p t e", p=128))
        e_b2 = sdma(b2_sb[:].rearrange("p (t e) -> p t e", t=8),
                    b2T_d.rearrange("(t p) e -> p t e", p=128))
        e_b3 = sdma(b3_sb[:].rearrange("p (t e) -> p t e", t=8),
                    b3T_d.rearrange("(t p) e -> p t e", p=128))
        e_bo = sdma(bout_sb[:], bout_d.rearrange("(t p) -> p t", p=128))
        e_gw = e_b1 = e_b2 = e_b3 = e_bo = ("ws", k.cnt["ws"])

        e_zz0 = k.e("vector", lambda E: E.memset(zero_zone, 0.0))
        k.w("sync", e_zz0)
        for kk in range(2 * tokens // 512):
            k.e("sync", lambda E, kk=kk: E.dma_start(
                out=yt_d.rearrange("(k g p) n -> k p g n", k=2 * tokens // 512, p=128)[kk],
                in_=zero_zone.rearrange("p (g n) -> p g n", g=4)), inc="zd", amt=16)
        for kk in range(tokens // 512):
            k.e("sync", lambda E, kk=kk: E.dma_start(
                out=xt_d.rearrange("(k g p) n -> k p g n", k=tokens // 512, p=128)[kk],
                in_=zero_zone.rearrange("p (g n) -> p g n", g=4)), inc="zd", amt=16)
        e_ztot = ("zd", k.cnt["zd"])

        bc_reg = nc.alloc_register(mybir.EngineType.Pool, "bc_reg")
        k.e("gpsimd", lambda E: nc.reg_mov(bc_reg, tokens - 1))
        bc2_reg = nc.alloc_register(mybir.EngineType.Pool, "bc2_reg")
        k.e("gpsimd", lambda E: nc.reg_mov(bc2_reg, 2 * tokens - 1))

        class Slabs:
            def __init__(self):
                self.kk = 0
                self.free_evt = [None] * NBUF

            def load(self, src3d):
                buf = self.kk % NBUF
                k.w("gpsimd", self.free_evt[buf])
                evt = k.e("gpsimd", lambda E, a=src3d, b=buf: E.dma_start(
                    out=wsl[b][:].rearrange("p (c n) -> p c n", c=8), in_=a),
                    inc=f"wd{buf}", amt=16)
                self.kk += 1
                return buf, evt

        slabs = Slabs()

        def w2d(ap2d, c0):
            return ap2d[:, c0:c0 + 512].rearrange("(c p) n -> p c n", p=128)

        def w3d(ap3d, e, c0):
            return ap3d[e, :, c0:c0 + 512].rearrange("(c p) n -> p c n", p=128)

        def wrow(ap2d, r0, c0):
            return ap2d[r0:r0 + 1024, c0:c0 + 512].rearrange("(c p) n -> p c n", p=128)

        st_ev = {
            "G_mm_end": None,
            "H_mm_end": None,
            "psml_free": None,
            "attn_mm_end": None,
            "D_mm_end": None,
            "xa_free": [None, None],
            "xfin_free": [None, None],
            "otok_store": [None, None],
            "xntok_free": [None, None],
            "sxn_rd_end": None,
            "sxn_rd_end2": [None, None],
            "tmp_free": [None, None],
            "ptile_free": [None, None],
            "z1t_free": [None, None],
            "gate_prev": None,
            "xt_evts": [],
            "meta_mm_prev": None,
            "routing_done": None,
            "gd_prev": None,
            "tbl_evt": {},
            "xng_free": None,
            "sxng_rd_end": None,
            "sd_prev": None,
        }

        n_store = [0]

        def stage_A(mt):
            xnf_done = None
            x32evts = []
            xt_evts = st_ev["xt_evts"]
            xt_evts.clear()
            for tt in range(NT):
                row0 = mt * M + tt * 128
                slot = tt % 2
                k.w("sync", st_ev["xa_free"][slot])
                ld = k.e("sync", lambda E, r=row0, s=slot: E.dma_start(
                    out=xa[s][:], in_=x_d[r:r + 128, :]), inc=f"xa{slot}", amt=16)
                k.w("scalar", ld)
                if st_ev["gate_prev"] is not None:
                    k.w("scalar", st_ev["gate_prev"])
                e_sq = k.e("scalar", lambda E, s=slot: E.activation(
                    xn32[:], xa[s][:], AF.Square, accum_out=cp(CSXX)), inc="sc")
                k.w("vector", ld)
                k.e("vector", lambda E, s=slot: E.reduce_sum(cp(CSX), xa[s][:], axis=AX))
                k.e("vector", lambda E: E.tensor_scalar_mul(cp(CMEAN), cp(CSX), 1.0 / D))
                k.w("vector", e_sq)
                k.e("vector", lambda E: E.tensor_scalar_mul(cp(CEX2), cp(CSXX), 1.0 / D))
                k.e("vector", lambda E: E.tensor_mul(cp(CM2), cp(CMEAN), cp(CMEAN)))
                k.e("vector", lambda E: E.tensor_sub(cp(CVAR), cp(CEX2), cp(CM2)))
                e_var = k.e("vector", lambda E: E.tensor_scalar_add(
                    cp(CT1), cp(CVAR), 1e-5), inc="ev")
                k.w("scalar", e_var)
                e_srt = k.e("scalar", lambda E: E.activation(
                    cp(CSRT), cp(CT1), AF.Sqrt), inc="sc")
                k.w("vector", e_srt)
                e_rc = k.e("vector", lambda E: E.reciprocal(cp(CY0), cp(CSRT)),
                           inc="ev")
                k.w("vector", e_rc)
                k.e("vector", lambda E: E.tensor_mul(cp(CT2), cp(CY0), cp(CY0)))
                k.e("vector", lambda E: E.tensor_mul(cp(CT2), cp(CT2), cp(CT1)))
                k.e("vector", lambda E: E.tensor_scalar(cp(CT3), cp(CT2), -0.5, 1.5, ALU.mult, ALU.add))
                k.e("vector", lambda E: E.tensor_mul(cp(CRSTD), cp(CY0), cp(CT3)))
                k.wmax("vector", st_ev["xntok_free"][slot] or [])
                e_xn = k.e("vector", lambda E, s=slot: E.tensor_scalar(
                    xntok[s][:], xa[s][:], cp(CMEAN), cp(CRSTD),
                    ALU.subtract, ALU.mult), inc="ev")
                st_ev["xa_free"][slot] = e_xn
                k.w("gpsimd", e_xn)
                if mt == 0 and tt == 0:
                    k.w("gpsimd", e_ztot)
                e_xt = k.e("gpsimd", lambda E, r=row0, s=slot: E.dma_start(
                    out=xt_d[r:r + 128, :], in_=xntok[s][:]),
                    inc=f"xt{slot}", amt=16)
                xt_evts.append(e_xt)
                if a_level < 2:
                    continue
                k.w("tensor", e_xn)
                if mt == 0 and tt == 0:
                    k.w("tensor", e_idf)
                ev_l = None
                for half in range(2):
                    bi, _ = trB.acq()
                    for cc in range(4):
                        c = half * 4 + cc
                        fn = (lambda E, s=slot, c=c, bi=bi, cc=cc: E.transpose(
                            trB.sl(bi, cc * 128, cc * 128 + 128),
                            xntok[s][:, c * 128:(c + 1) * 128], identf[:]))
                        ev = k.e("tensor", fn, inc="mm" if cc == 3 else None)
                    ev_l = ev
                    if a_level < 3:
                        trB.rel(bi, ev)
                        continue
                    k.w("vector", ev)
                    if half == 0 and st_ev["gate_prev"] is not None:
                        k.w("vector", st_ev["gate_prev"])
                        k.w("vector", xnf_done)
                    e_v = k.e("vector", lambda E, bi=bi, h=half: E.tensor_copy(
                        xn32[:, h * 512:(h + 1) * 512], trB.sl(bi, 0, 512)), inc="ev")
                    k.w("scalar", e_v)
                    if tt == 0 and half == 0 and st_ev["G_mm_end"] is not None:
                        k.w("scalar", st_ev["G_mm_end"])
                    e_s = k.e("scalar", lambda E, h=half, tt=tt: E.activation(
                        xn_f[:, 0:SM].rearrange("p (c t) -> p c t", c=8)
                        [:, h * 4:(h + 1) * 4, tt * 128:(tt + 1) * 128],
                        xn32[:, h * 512:(h + 1) * 512].rearrange("p (c t) -> p c t", c=4),
                        AF.Copy), inc="sc")
                    trB.rel(bi, e_v)
                    x32evts.append(e_v)
                    xnf_done = e_s
                st_ev["xntok_free"][slot] = [ev_l, xt_evts[-1]]
                if a_level < 3:
                    continue
                if a_level < 4:
                    continue
                k.wmax("tensor", x32evts[-2:])
                if mt == 0 and tt == 0:
                    k.w("tensor", e_gw)
                if st_ev["psml_free"] is not None:
                    k.w("tensor", st_ev["psml_free"])
                for c in range(8):
                    ev = k.e("tensor", lambda E, c=c: E.matmul(
                        psml[:, 0:8], xn32[:, c * 128:(c + 1) * 128],
                        gw_sb[:, c * 8:(c + 1) * 8],
                        start=(c == 0), stop=(c == 7)),
                        inc="mm" if c == 7 else None)
                k.w("vector", ev)
                e_lg = k.e("vector", lambda E, tt=tt: E.tensor_copy(
                    logits[:, tt * 8:(tt + 1) * 8], psml[:, 0:8]), inc="ev")
                st_ev["psml_free"] = e_lg
                st_ev["gate_prev"] = ev
            return xnf_done

        def preload_B(mt):
            return [slabs.load(w2d(wqkv_d, j * 512)) for j in range(3)]

        def stage_B(mt, xnf_done, pre):
            qk_done = None
            v_done = None
            for j in range(6):
                buf, rdy = pre[j] if j < len(pre) else slabs.load(w2d(wqkv_d, j * 512))
                k.w("tensor", rdy)
                if j == 0:
                    k.w("tensor", xnf_done)
                last_mm = None
                if j < 4:
                    dest = qf if j < 2 else kf
                    base_oc = (j % 2) * 4
                    for o in range(4):
                        for n in range(NN):
                            bi, bsl = mmB.acq(cols=NW)
                            for c in range(8):
                                ev = k.e("tensor", lambda E, b=buf, c=c, o=o, n=n, bsl=bsl: E.matmul(
                                    bsl, wsl[b][:, c * 512 + o * 128:c * 512 + (o + 1) * 128],
                                    xn_f[:, c * M + n * 512:c * M + n * 512 + NW],
                                    start=(c == 0), stop=(c == 7)),
                                    inc="mm" if c == 7 else None)
                            last_mm = ev
                            k.w("vector", ev)
                            if mt == 0 and j == 0 and o == 0 and n == 0:
                                k.w("vector", e_ztot)
                            oc = base_oc + o
                            e_v = k.e("vector", lambda E, bi=bi, oc=oc, n=n, d=dest: E.tensor_copy(
                                d[:, oc * M + n * 512:oc * M + n * 512 + NW],
                                mmB.sl(bi, 0, NW)), inc="ev")
                            mmB.rel(bi, e_v)
                            qk_done = e_v
                else:
                    for tt in range(NT):
                        bi, bsl = mmB.acq()
                        for c in range(8):
                            ev = k.e("tensor", lambda E, b=buf, c=c, tt=tt, bsl=bsl: E.matmul(
                                bsl, xn_f[:, c * M + tt * 128:c * M + (tt + 1) * 128],
                                wsl[b][:, c * 512:(c + 1) * 512],
                                start=(c == 0), stop=(c == 7)),
                                inc="mm" if c == 7 else None)
                        last_mm = ev
                        k.w("vector", ev)
                        if j == 4 and tt == 0 and st_ev["sd_prev"] is not None:
                            k.w("vector", st_ev["sd_prev"])
                        e_v = k.e("vector", lambda E, bi=bi, tt=tt, j=j: E.tensor_copy(
                            vtok[:, tt * 1024 + (j - 4) * 512:tt * 1024 + (j - 4) * 512 + 512],
                            mmB.sl(bi, 0, 512)), inc="ev")
                        mmB.rel(bi, e_v)
                        v_done = e_v
                slabs.free_evt[buf] = last_mm
            return qk_done, v_done

        def stage_C(mt, qk_done, v_done):
            aof_done = None
            attn_end = None
            state = {}

            def scores(b):
                bi, _ = mmB.acq(cols=128)
                if b == 0:
                    k.w("tensor", qk_done)
                for c in range(8):
                    ev = k.e("tensor", lambda E, c=c, b=b, bi=bi: E.matmul(
                        mmB.sl(bi, 0, 128),
                        qf[:, c * M + b * 128:c * M + (b + 1) * 128],
                        kf[:, c * M + b * 128:c * M + (b + 1) * 128],
                        start=(c == 0), stop=(c == 7)),
                        inc="mm" if c == 7 else None)
                state[b] = (bi, ev)
                return ev

            def softmax(b):
                par = b % 2
                bi, ev = state[b]
                NRM = CNRM if par == 0 else CNRM2
                RSUM = CRSUM if par == 0 else CRSUM2
                RDEN = CRDEN if par == 0 else CRDEN2
                smv = sm32[:, par * 128:(par + 1) * 128]
                pv = p32[:, par * 128:(par + 1) * 128]
                pnv = pn_sb[:, par * 128:(par + 1) * 128]
                k.w("vector", ev)
                if b == 0 and mt == 0:
                    k.w("vector", (("gp", k.cnt["gp"])))
                k.e("vector", lambda E, bi=bi, smv=smv: E.scalar_tensor_tensor(
                    smv, mmB.sl(bi, 0, 128), SCALE, amask[:],
                    ALU.mult, ALU.add))
                e_nm = k.e("vector", lambda E, smv=smv, NRM=NRM: E.reduce_max(
                    cp(NRM), smv, axis=AX, negate=True))
                mmB.rel(bi, e_nm)
                k.w("scalar", e_nm)
                e_p = k.e("scalar", lambda E, smv=smv, pv=pv, NRM=NRM, RSUM=RSUM: E.activation(
                    pv, smv, AF.Exp, bias=cp(NRM),
                    accum_out=cp(RSUM)), inc="sc")
                k.w("vector", e_p)
                k.e("vector", lambda E, RDEN=RDEN, RSUM=RSUM: E.tensor_scalar_add(
                    cp(RDEN), cp(RSUM), 1e-6))
                e_rc = k.e("vector", lambda E, RDEN=RDEN: E.reciprocal(
                    cp(RDEN), cp(RDEN)))
                k.w("vector", e_rc)
                e_pn = k.e("vector", lambda E, pv=pv, pnv=pnv, RDEN=RDEN: E.tensor_scalar_mul(
                    pnv, pv, cp(RDEN)))
                return e_pn

            def pv_block(b, e_pn):
                nonlocal aof_done, attn_end
                sloti = b % 2
                pnv = pn_sb[:, sloti * 128:(sloti + 1) * 128]
                k.w("tensor", e_pn)
                if mt == 0 and b < 2:
                    k.w("tensor", e_idb)
                k.wmax("tensor", btB.free[sloti])
                btB.free[sloti] = []
                e_tp = k.e("tensor", lambda E, s=sloti, pnv=pnv: E.transpose(
                    pbt[:, s * 1024:s * 1024 + 128], pnv, identb[:]), inc="mm")
                k.w("vector", e_tp)
                e_pt = k.e("vector", lambda E, s=sloti: E.tensor_copy(
                    pt_sb[:, s * 128:(s + 1) * 128], pbt[:, s * 1024:s * 1024 + 128]))
                btB.free[sloti].append(e_pt)
                k.w("tensor", e_pt)
                if b == 0:
                    k.w("tensor", v_done)
                for half in range(2):
                    bi2, _ = mmB.acq()
                    for dt in range(4):
                        ev = k.e("tensor", lambda E, bi2=bi2, dt=dt, half=half, b=b, s=sloti: E.matmul(
                            mmB.sl(bi2, dt * 128, (dt + 1) * 128),
                            vtok[:, b * 1024 + (half * 4 + dt) * 128:b * 1024 + (half * 4 + dt + 1) * 128],
                            pt_sb[:, s * 128:(s + 1) * 128],
                            start=True, stop=True),
                            inc="mm" if dt == 3 else None)
                    attn_end = ev
                    k.w("scalar", ev)
                    e_a = k.e("scalar", lambda E, bi2=bi2, half=half, b=b: E.activation(
                        aof[:, 0:SM].rearrange("p (c t) -> p c t", c=8)
                        [:, half * 4:(half + 1) * 4, b * 128:(b + 1) * 128],
                        mmB.sl(bi2, 0, 512).rearrange("p (c t) -> p c t", c=4),
                        AF.Copy), inc="sc")
                    mmB.rel(bi2, e_a)
                    aof_done = e_a

            scores(0)
            pend = softmax(0)
            for b in range(1, NT):
                scores(b)
                pv_block(b - 1, pend)
                pend = softmax(b)
            pv_block(NT - 1, pend)
            st_ev["attn_mm_end"] = attn_end
            return aof_done

        def table_build(mt, e):
            NG = CAP // 128
            if e == 0:
                k.w("vector", st_ev["routing_done"])
                k.w("vector", st_ev["attn_mm_end"])
                if st_ev["sd_prev"] is not None:
                    k.w("vector", st_ev["sd_prev"])
                if st_ev["gd_prev"] is not None:
                    k.w("vector", st_ev["gd_prev"])
            else:
                k.w("vector", st_ev["meta_mm_prev"])
            for tc in range(NT):
                e_pm = k.e("vector", lambda E, tc=tc, e=e: E.tensor_scalar(
                    p_mx[:, tc * CAP:(tc + 1) * CAP], iotaC[:, 0:CAP],
                    r_tokall[:, tc * 8 + e:tc * 8 + e + 1],
                    m_tokall[:, tc * 8 + e:tc * 8 + e + 1],
                    ALU.is_equal, ALU.mult))
            k.e("vector", lambda E, e=e: E.tensor_copy(
                meta[:].rearrange("p (t k) -> p t k", k=4)[:, 0:NT, 2:3],
                s_tokall[:].rearrange("p (t q) -> p t q", q=8)[:, 0:NT, e:e + 1]))
            e_meta = k.e("vector", lambda E, e=e: E.tensor_copy(
                meta[:].rearrange("p (t k) -> p t k", k=4)[:, 0:NT, 3:4],
                slot_tokall[:].rearrange("p (t q) -> p t q", q=8)[:, 0:NT, e:e + 1]))
            k.w("tensor", e_pm)
            k.w("tensor", e_meta)
            if st_ev["psml_free"] is not None:
                k.w("tensor", st_ev["psml_free"])
            for tc in range(NT):
                mev = k.e("tensor", lambda E, tc=tc: E.matmul(
                    psml[0:4, 0:CAP], meta[:, tc * 4:(tc + 1) * 4],
                    p_mx[:, tc * CAP:(tc + 1) * CAP],
                    start=(tc == 0), stop=(tc == NT - 1)),
                    inc="mm" if tc == NT - 1 else None)
            st_ev["meta_mm_prev"] = mev
            k.w("vector", mev)
            e_m32 = k.e("vector", lambda E: E.tensor_copy(
                meta32[0:4, 0:CAP], psml[0:4, 0:CAP]))
            k.w("tensor", e_m32)
            for g in range(NG):
                tev = k.e("tensor", lambda E, g=g: E.transpose(
                    psml[0:128, g * 4:(g + 1) * 4],
                    meta32[0:4, g * 128:(g + 1) * 128],
                    identf[0:4, 0:4]), inc="mm" if g == NG - 1 else None)
            k.w("vector", tev)
            k.e("vector", lambda E: E.tensor_copy(
                mtt[:, 0:4 * NG], psml[:, 0:4 * NG]))
            k.e("vector", lambda E: E.scalar_tensor_tensor(
                ids1f[:, 0:NG], mtt[:, 0:4 * NG:4], 32.0,
                mtt[:, 1:4 * NG:4], ALU.mult, ALU.add))
            k.e("vector", lambda E: E.tensor_single_scalar(
                eqf[:, 0:NG], ids1f[:, 0:NG], cp(CHALF), ALU.is_lt))
            k.e("vector", lambda E, mt=mt: E.tensor_scalar_add(
                idxf[:, 0:NG], ids1f[:, 0:NG], float(mt * M - 1)))
            k.e("vector", lambda E: E.scalar_tensor_tensor(
                idxf[:, 0:NG], eqf[:, 0:NG], float(tokens + 1), idxf[:, 0:NG],
                ALU.mult, ALU.add))
            e_idx = k.e("vector", lambda E, e=e: E.tensor_copy(
                idx_all[:, e * NG:(e + 1) * NG], idxf[:, 0:NG]))
            k.e("vector", lambda E: E.scalar_tensor_tensor(
                idxf[:, 0:NG], ids1f[:, 0:NG], 2.0, mtt[:, 3:4 * NG:4],
                ALU.mult, ALU.add))
            k.e("vector", lambda E, mt=mt: E.tensor_scalar_add(
                idxf[:, 0:NG], idxf[:, 0:NG], float(2 * mt * M - 2)))
            k.e("vector", lambda E: E.scalar_tensor_tensor(
                idxf[:, 0:NG], eqf[:, 0:NG], float(2 * tokens + 4), idxf[:, 0:NG],
                ALU.mult, ALU.add))
            k.e("vector", lambda E, e=e: E.tensor_copy(
                ixs_all[:, e * NG:(e + 1) * NG], idxf[:, 0:NG]))
            e_sg = k.e("vector", lambda E, e=e: E.tensor_copy(
                sg_all[:, e * NG:(e + 1) * NG], mtt[:, 2:4 * NG:4]))
            st_ev["psml_free"] = e_sg
            st_ev["tbl_evt"][e] = e_sg

        def stage_D(mt, aof_done):
            D_end = None
            acc_evts = []
            it = [0]
            for j in range(2):
                buf, rdy = slabs.load(w2d(woutT_d, j * 512))
                k.w("tensor", rdy)
                if j == 0:
                    k.w("tensor", aof_done)
                last_mm = None
                for o in range(4):
                    for n in range(NN):
                        bi, bsl = mmB.acq(cols=NW)
                        for c in range(8):
                            ev = k.e("tensor", lambda E, b=buf, c=c, o=o, n=n, bsl=bsl: E.matmul(
                                bsl, wsl[b][:, c * 512 + o * 128:c * 512 + (o + 1) * 128],
                                aof[:, c * M + n * 512:c * M + n * 512 + NW],
                                start=(c == 0), stop=(c == 7)),
                                inc="mm" if c == 7 else None)
                        last_mm = ev
                        D_end = ev
                        oc = j * 4 + o
                        k.w("scalar", ev)
                        if j == 0 and o == 0 and n == 0:
                            if st_ev["H_mm_end"] is not None:
                                k.w("scalar", st_ev["H_mm_end"])
                            if mt == 0:
                                k.w("scalar", e_bo)
                        e_s = k.e("scalar", lambda E, bi=bi, oc=oc, n=n: E.activation(
                            acc[:, oc * M + n * 512:oc * M + n * 512 + NW],
                            mmB.sl(bi, 0, NW), AF.Identity,
                            bias=bout_sb[:, oc:oc + 1]), inc="sc")
                        mmB.rel(bi, e_s)
                        acc_evts.append(e_s)
                        if it[0] % 2 == 0 and it[0] // 2 < EXP:
                            table_build(mt, it[0] // 2)
                        it[0] += 1
                slabs.free_evt[buf] = last_mm
            st_ev["D_mm_end"] = D_end
            return acc_evts[-1]

        def stage_E(mt):
            srow_done = None
            for tt in range(NT):
                slot = tt % 2
                e_nm = k.e("vector", lambda E, tt=tt: E.reduce_max(
                    cp(CNRM), logits[:, tt * 8:(tt + 1) * 8], axis=AX, negate=True),
                    inc="ev")
                k.w("scalar", e_nm)
                e_p = k.e("scalar", lambda E, tt=tt: E.activation(
                    p32[:, 0:8],
                    logits[:, tt * 8:(tt + 1) * 8], AF.Exp, bias=cp(CNRM),
                    accum_out=cp(CRSUM)), inc="sc")
                k.w("vector", e_p)
                e_rc = k.e("vector", lambda E: E.reciprocal(cp(CRDEN), cp(CRSUM)),
                           inc="ev")
                k.w("vector", e_rc)
                k.e("vector", lambda E: E.tensor_scalar_mul(
                    sm32[:, 0:8], p32[:, 0:8], cp(CRDEN)))
                k.e("vector", lambda E: E.reduce_max(cp(CT1), sm32[:, 0:8], axis=AX))
                k.e("vector", lambda E: E.tensor_single_scalar(
                    sm32[:, 8:16], sm32[:, 0:8], cp(CT1), ALU.is_ge))
                k.e("vector", lambda E: E.scalar_tensor_tensor(
                    sm32[:, 16:24], sm32[:, 8:16], -1.0e30, sm32[:, 0:8],
                    ALU.mult, ALU.add))
                k.e("vector", lambda E: E.reduce_max(cp(CT2), sm32[:, 16:24], axis=AX))
                k.e("vector", lambda E: E.tensor_single_scalar(
                    sm32[:, 8:16], sm32[:, 0:8], cp(CT2), ALU.is_ge))
                e_s = k.e("vector", lambda E, tt=tt: E.tensor_mul(
                    s_tokall[:, tt * 8:tt * 8 + 8], sm32[:, 0:8], sm32[:, 8:16]))
                e_m = k.e("vector", lambda E, tt=tt: E.tensor_single_scalar(
                    m_tokall[:, tt * 8:tt * 8 + 8],
                    s_tokall[:, tt * 8:tt * 8 + 8],
                    zerocol[:, 0:1], ALU.is_gt))
                k.e("vector", lambda E, tt=tt: E.tensor_tensor_scan(
                    slot_tokall[:, tt * 8:tt * 8 + 8],
                    m_tokall[:, tt * 8:tt * 8 + 8],
                    m_tokall[:, tt * 8:tt * 8 + 8], 0.0, ALU.add, ALU.bypass))
                k.e("vector", lambda E, tt=tt: E.tensor_sub(
                    slot_tokall[:, tt * 8:tt * 8 + 8],
                    slot_tokall[:, tt * 8:tt * 8 + 8],
                    m_tokall[:, tt * 8:tt * 8 + 8]))
                k.w("tensor", e_m)
                if st_ev["psml_free"] is not None:
                    k.w("tensor", st_ev["psml_free"])
                ev = k.e("tensor", lambda E, tt=tt: E.transpose(
                    psml[0:8, 0:128], m_tokall[:, tt * 8:tt * 8 + 8],
                    identf[:]), inc="mm")
                k.w("vector", ev)
                e_r = k.e("vector", lambda E, tt=tt: E.tensor_copy(
                    m_em8[0:8, tt * 128:(tt + 1) * 128], psml[0:8, 0:128]))
                st_ev["psml_free"] = e_r
            k.e("vector", lambda E: E.tensor_tensor_scan(
                rk_em8[0:8, :], m_em8[0:8, :], m_em8[0:8, :], 0.0,
                ALU.add, ALU.bypass))
            e_r0 = k.e("vector", lambda E: E.tensor_sub(
                rk_em8[0:8, :], rk_em8[0:8, :], m_em8[0:8, :]))
            rt_done = None
            for tt in range(NT):
                k.w("tensor", e_r0)
                k.w("tensor", st_ev["psml_free"])
                ev = k.e("tensor", lambda E, tt=tt: E.transpose(
                    psml[0:128, 0:8], rk_em8[0:8, tt * 128:(tt + 1) * 128],
                    identf[0:8, 0:8]), inc="mm")
                k.w("vector", ev)
                rt_done = k.e("vector", lambda E, tt=tt: E.tensor_copy(
                    r_tokall[:, tt * 8:(tt + 1) * 8], psml[0:128, 0:8]))
                st_ev["psml_free"] = rt_done
            return rt_done

        def stage_F(mt, routing_done, xnf_done):
            NG = CAP // 128
            sd_after = {}
            sxn_gbuf = [sxn_g, scr[:, 49 * CAP:57 * CAP]]

            def prologue1(e):
                k.w("gpsimd", st_ev["tbl_evt"][e])
                k.wmax("gpsimd", st_ev["xt_evts"])
                if st_ev["xng_free"] is not None:
                    k.w("gpsimd", st_ev["xng_free"])
                if e == 0:
                    k.w("gpsimd", st_ev["attn_mm_end"])
                for g in range(NG):
                    k.e("gpsimd", lambda E, g=g, e=e: E.indirect_dma_start(
                        out=xng_tok[:, g * 1024:(g + 1) * 1024], out_offset=None,
                        in_=xt_d,
                        in_offset=IndirectOffsetOnAxis(
                            ap=idx_all[:, e * NG + g:e * NG + g + 1], axis=0),
                        bounds_check=bc_reg, oob_is_err=False), inc="gd", amt=16)
                return ("gd", k.cnt["gd"])

            def prologue2(e, gd_tot):
                slot = e % 2
                sxn = sxn_gbuf[slot]
                k.w("vector", gd_tot)
                if st_ev["sxng_rd_end"] is not None:
                    k.w("vector", st_ev["sxng_rd_end"])
                if e == 0:
                    k.w("vector", st_ev["G_mm_end"])
                e_sc = None
                for g in range(NG):
                    e_sc = k.e("vector", lambda E, g=g, e=e: E.tensor_scalar_mul(
                        sxng_tok[:, g * 1024:(g + 1) * 1024],
                        xng_tok[:, g * 1024:(g + 1) * 1024],
                        sg_all[:, e * NG + g:e * NG + g + 1]))
                st_ev["xng_free"] = e_sc
                k.w("tensor", e_sc)
                if e == 0 and mt == 0:
                    k.w("tensor", e_idb)
                last_sxn_wr = None
                for dc in range(8):
                    bi, _ = btB.acq()
                    for g in range(NG):
                        tev = k.e("tensor", lambda E, dc=dc, g=g, bi=bi: E.transpose(
                            pbt[:, bi * 1024 + g * 128:bi * 1024 + (g + 1) * 128],
                            sxng_tok[:, g * 1024 + dc * 128:g * 1024 + (dc + 1) * 128],
                            identb[:]), inc="mm" if g == NG - 1 else None)
                    k.w("vector", tev)
                    if dc == 0:
                        if e == 0:
                            k.w("vector", st_ev["attn_mm_end"])
                        k.w("vector", st_ev["sxn_rd_end2"][slot])
                    last_sxn_wr = k.e("vector", lambda E, dc=dc, bi=bi, sxn=sxn: E.tensor_copy(
                        sxn[:, dc * CAP:(dc + 1) * CAP],
                        pbt[:, bi * 1024:bi * 1024 + CAP]))
                    btB.rel(bi, last_sxn_wr)
                st_ev["sxng_rd_end"] = tev
                return last_sxn_wr

            prep = {}
            ptr_pend = []
            w1pre = {0: [slabs.load(w3d(w1T_d, 0, 0))]}
            gpre = []
            gd0 = prologue1(0)
            prep[0] = prologue2(0, gd0)
            for e in range(EXP):
                slot = e % 2
                sxn = sxn_gbuf[slot]
                last_sxn_wr = prep.pop(e)
                h_done = None
                pre1 = w1pre.pop(e, [])
                for j in range(2):
                    buf, rdy = pre1[j] if j < len(pre1) else slabs.load(w3d(w1T_d, e, j * 512))
                    k.w("tensor", rdy)
                    if j == 0:
                        k.w("tensor", last_sxn_wr)
                    for o in range(4):
                        oc = j * 4 + o
                        bi, bsl = mmB.acq(cols=CAP)
                        for c in range(8):
                            ev = k.e("tensor", lambda E, b=buf, c=c, o=o, bsl=bsl, sxn=sxn: E.matmul(
                                bsl, wsl[b][:, c * 512 + o * 128:c * 512 + (o + 1) * 128],
                                sxn[:, c * CAP:c * CAP + CAP],
                                start=(c == 0), stop=(c == 7)),
                                inc="mm" if c == 7 else None)
                        k.w("scalar", ev)
                        if mt == 0 and e == 0 and oc == 0:
                            k.w("scalar", e_b1)
                            k.w("scalar", e_b2)
                            k.w("scalar", e_b3)
                            k.w("scalar", st_ev["attn_mm_end"])
                        e_h = k.e("scalar", lambda E, bi=bi, oc=oc, e=e: E.activation(
                            h_g[:, oc * CAP:oc * CAP + CAP],
                            mmB.sl(bi, 0, CAP), AF.Silu,
                            bias=b1_sb[:, oc * 8 + e:oc * 8 + e + 1]), inc="sc")
                        mmB.rel(bi, e_h)
                        h_done = e_h
                    slabs.free_evt[buf] = ev
                w3bufs = {}
                w2bufs = {}
                prod_done = None
                gd_next = None
                for oc in range(8):
                    j = oc // 4
                    if j not in w3bufs:
                        w3bufs[j] = slabs.load(w3d(w3T_d, e, j * 512))
                        w2bufs[j] = slabs.load(w3d(w2T_d, e, j * 512))
                    b3buf, rdy3 = w3bufs[j]
                    b2buf, rdy2 = w2bufs[j]
                    bi3, bsl3 = mmB.acq(cols=CAP)
                    if oc % 4 == 0:
                        k.w("tensor", rdy3)
                    for c in range(8):
                        ev3 = k.e("tensor", lambda E, b=b3buf, c=c, oc=oc, bsl=bsl3, sxn=sxn: E.matmul(
                            bsl, wsl[b][:, c * 512 + (oc % 4) * 128:c * 512 + (oc % 4 + 1) * 128],
                            sxn[:, c * CAP:c * CAP + CAP],
                            start=(c == 0), stop=(c == 7)),
                            inc="mm" if c == 7 else None)
                    bi2, bsl2 = mmB.acq(cols=CAP)
                    if oc % 4 == 0:
                        k.w("tensor", rdy2)
                    if oc == 0:
                        k.w("tensor", h_done)
                    for c in range(8):
                        ev2 = k.e("tensor", lambda E, b=b2buf, c=c, oc=oc, bsl=bsl2: E.matmul(
                            bsl, wsl[b][:, c * 512 + (oc % 4) * 128:c * 512 + (oc % 4 + 1) * 128],
                            h_g[:, c * CAP:c * CAP + CAP],
                            start=(c == 0), stop=(c == 7)),
                            inc="mm" if c == 7 else None)
                    i2 = oc % 2
                    k.w("scalar", ev3)
                    if st_ev["tmp_free"][i2] is not None:
                        k.w("scalar", st_ev["tmp_free"][i2])
                    e_x3 = k.e("scalar", lambda E, bi3=bi3, oc=oc, e=e, i2=i2: E.activation(
                        tmp3[i2][:, 0:CAP], mmB.sl(bi3, 0, CAP), AF.Identity,
                        bias=b3_sb[:, oc * 8 + e:oc * 8 + e + 1]), inc="sc")
                    mmB.rel(bi3, e_x3)
                    k.w("scalar", ev2)
                    e_x2 = k.e("scalar", lambda E, bi2=bi2, oc=oc, e=e, i2=i2: E.activation(
                        tmp2[i2][:, 0:CAP], mmB.sl(bi2, 0, CAP), AF.Identity,
                        bias=b2_sb[:, oc * 8 + e:oc * 8 + e + 1]), inc="sc")
                    mmB.rel(bi2, e_x2)
                    k.w("vector", e_x3)
                    k.w("vector", e_x2)
                    if oc == 0 and e == 0:
                        k.w("vector", st_ev["D_mm_end"])
                    if st_ev["ptile_free"][i2] is not None:
                        k.w("vector", st_ev["ptile_free"][i2])
                    e_pr = k.e("vector", lambda E, i2=i2: E.tensor_mul(
                        prodtile[i2][:, 0:CAP], tmp3[i2][:, 0:CAP], tmp2[i2][:, 0:CAP]))
                    st_ev["tmp_free"][i2] = e_pr

                    def emit_ptr(oc=oc, i2=i2, e_pr=e_pr, e=e):
                        k.w("tensor", e_pr)
                        bi, _ = btB.acq()
                        for g in range(NG):
                            tev = k.e("tensor", lambda E, g=g, bi=bi, i2=i2: E.transpose(
                                pbt[:, bi * 1024 + g * 128:bi * 1024 + (g + 1) * 128],
                                prodtile[i2][:, g * 128:(g + 1) * 128],
                                identb[:]), inc="mm" if g == NG - 1 else None)
                        k.w("scalar", tev)
                        if oc == 0:
                            if e == 0:
                                if st_ev["sd_prev"] is not None:
                                    k.w("scalar", st_ev["sd_prev"])
                            elif sd_after.get(e - 1) is not None:
                                k.w("scalar", sd_after[e - 1])
                        pd = k.e("scalar", lambda E, oc=oc, bi=bi: E.copy(
                            prodtok[:].rearrange("p (g n) -> p g n", g=NG)
                            [:, :, oc * 128:(oc + 1) * 128],
                            pbt[:, bi * 1024:bi * 1024 + CAP]
                            .rearrange("p (g n) -> p g n", g=NG)), inc="sc")
                        btB.rel(bi, pd)
                        st_ev["ptile_free"][i2] = pd
                        return pd

                    ptr_pend.append(emit_ptr)
                    if len(ptr_pend) > 1:
                        prod_done = ptr_pend.pop(0)()
                    if oc % 4 == 3:
                        slabs.free_evt[b3buf] = ev3
                        slabs.free_evt[b2buf] = ev2
                    if e + 1 < EXP:
                        if oc == 0:
                            gd_next = prologue1(e + 1)
                        elif oc == 2:
                            prep[e + 1] = prologue2(e + 1, gd_next)
                        elif oc == 3:
                            w1pre[e + 1] = [slabs.load(w3d(w1T_d, e + 1, 0))]
                    elif oc == 5:
                        gpre.append(slabs.load(w2d(sw1T_d, 0)))
                while ptr_pend:
                    prod_done = ptr_pend.pop(0)()
                st_ev["sxn_rd_end"] = ev3
                st_ev["sxn_rd_end2"][slot] = ev3
                k.w("gpsimd", prod_done)
                if mt == 0 and e == 0:
                    k.w("gpsimd", e_ztot)
                for g in range(NG):
                    k.e("gpsimd", lambda E, g=g, e=e: E.indirect_dma_start(
                        out=yt_d,
                        out_offset=IndirectOffsetOnAxis(
                            ap=ixs_all[:, e * NG + g:e * NG + g + 1], axis=0),
                        in_=prodtok[:, g * 1024:(g + 1) * 1024], in_offset=None,
                        bounds_check=bc2_reg, oob_is_err=False), inc="sd", amt=16)
                sd_after[e] = ("sd", k.cnt["sd"])
            st_ev["sd_prev"] = ("sd", k.cnt["sd"])
            st_ev["gd_prev"] = ("gd", k.cnt["gd"])
            return ("sd", k.cnt["sd"]), gpre

        def stage_G(mt, sd_tot, gpre):
            zi = [0]
            zz_done = None
            for j in range(4):
                if j == 0 and len(gpre) == 1:
                    b1buf, rdy1 = gpre[0]
                    b2buf, rdy2 = slabs.load(w2d(sw2T_d, 0))
                else:
                    b1buf, rdy1 = slabs.load(w2d(sw1T_d, j * 512))
                    b2buf, rdy2 = slabs.load(w2d(sw2T_d, j * 512))
                for it_in in range(4):
                    it = j * 4 + it_in
                    for n in range(NN):
                        bia, bsla = mmB.acq(cols=NW)
                        if it_in == 0 and n == 0:
                            k.w("tensor", rdy1)
                        for c in range(8):
                            ev1 = k.e("tensor", lambda E, b=b1buf, c=c, it_in=it_in, n=n, bsl=bsla: E.matmul(
                                bsl, wsl[b][:, c * 512 + it_in * 128:c * 512 + (it_in + 1) * 128],
                                xn_f[:, c * M + n * 512:c * M + n * 512 + NW],
                                start=(c == 0), stop=(c == 7)),
                                inc="mm" if c == 7 else None)
                        bib, bslb = mmB.acq(cols=NW)
                        if it_in == 0 and n == 0:
                            k.w("tensor", rdy2)
                        for c in range(8):
                            ev2 = k.e("tensor", lambda E, b=b2buf, c=c, it_in=it_in, n=n, bsl=bslb: E.matmul(
                                bsl, wsl[b][:, c * 512 + it_in * 128:c * 512 + (it_in + 1) * 128],
                                xn_f[:, c * M + n * 512:c * M + n * 512 + NW],
                                start=(c == 0), stop=(c == 7)),
                                inc="mm" if c == 7 else None)
                        i2 = zi[0] % 2
                        k.w("scalar", ev1)
                        if st_ev["z1t_free"][i2] is not None:
                            k.w("scalar", st_ev["z1t_free"][i2])
                        e_z1 = k.e("scalar", lambda E, bia=bia, i2=i2: E.activation(
                            z1t[i2][:, 0:NW], mmB.sl(bia, 0, NW), AF.Silu), inc="sc")
                        mmB.rel(bia, e_z1)
                        k.w("vector", ev2)
                        k.w("vector", e_z1)
                        if it == 0 and n == 0:
                            k.w("vector", st_ev["D_mm_end"])
                            k.w("vector", sd_tot)
                        e_zz = k.e("vector", lambda E, bib=bib, it=it, n=n, i2=i2: E.tensor_mul(
                            zz[:, it * M + n * 512:it * M + n * 512 + NW],
                            z1t[i2][:, 0:NW], mmB.sl(bib, 0, NW)), inc="ev")
                        mmB.rel(bib, e_zz)
                        st_ev["z1t_free"][i2] = e_zz
                        zz_done = e_zz
                        zi[0] += 1
                slabs.free_evt[b1buf] = ev1
                slabs.free_evt[b2buf] = ev2
            if g_level < 2:
                st_ev["G_mm_end"] = ev2
                return zz_done
            G_end = None
            acc_done = None
            for j in range(2):
                br0, rdy0 = slabs.load(wrow(sw3T_d, 0, j * 512))
                br1, rdy1 = slabs.load(wrow(sw3T_d, 1024, j * 512))
                for o in range(4):
                    oc = j * 4 + o
                    for n in range(NN):
                        bi, bsl = mmB.acq(cols=NW)
                        if o == 0 and n == 0:
                            k.w("tensor", rdy0)
                            k.w("tensor", rdy1)
                            k.w("tensor", zz_done)
                        for half, bb in ((0, br0), (1, br1)):
                            for c in range(8):
                                i = half * 8 + c
                                ev = k.e("tensor", lambda E, b=bb, c=c, o=o, n=n, i=i, bsl=bsl: E.matmul(
                                    bsl, wsl[b][:, c * 512 + o * 128:c * 512 + (o + 1) * 128],
                                    zz[:, i * M + n * 512:i * M + n * 512 + NW],
                                    start=(i == 0), stop=(i == 15)),
                                    inc="mm" if i == 15 else None)
                        G_end = ev
                        k.w("vector", ev)
                        e_a = k.e("vector", lambda E, bi=bi, oc=oc, n=n: E.tensor_add(
                            acc[:, oc * M + n * 512:oc * M + n * 512 + NW],
                            acc[:, oc * M + n * 512:oc * M + n * 512 + NW],
                            mmB.sl(bi, 0, NW)), inc="ev")
                        mmB.rel(bi, e_a)
                        acc_done = e_a
                slabs.free_evt[br0] = G_end
                slabs.free_evt[br1] = G_end
            st_ev["G_mm_end"] = G_end
            return acc_done

        def stage_H(mt, acc_done, sd_tot):
            H_end = None
            for tt in range(NT):
                row0 = mt * M + tt * 128
                slot = tt % 2
                k.w("sync", st_ev["xfin_free"][slot])
                ld = k.e("sync", lambda E, r=row0, s=slot: E.dma_start(
                    out=xfin[s][:], in_=x_d[r:r + 128, :]), inc=f"xf{slot}", amt=16)
                if tt == 0:
                    k.w("sync", sd_tot)
                if st_ev.get("ytk_free", [None, None])[slot] is not None:
                    k.w("sync", st_ev["ytk_free"][slot])
                ld_y = k.e("sync", lambda E, r=row0, s=slot: E.dma_start(
                    out=ytk_w[s][:].rearrange("p (two n) -> p two n", two=2),
                    in_=yt_d.rearrange("(t two) n -> t two n", two=2)[r:r + 128]),
                    inc=f"yd{slot}", amt=16)
                k.w("tensor", acc_done)
                k.w("vector", ld)
                k.w("vector", ld_y)
                if st_ev["otok_store"][slot] is not None:
                    k.w("vector", st_ev["otok_store"][slot])
                for half in range(2):
                    bi, _ = trB.acq()
                    for cc in range(4):
                        c = half * 4 + cc
                        ev = k.e("tensor", lambda E, c=c, tt=tt, bi=bi, cc=cc: E.transpose(
                            trB.sl(bi, cc * 128, cc * 128 + 128),
                            acc[:, c * M + tt * 128:c * M + (tt + 1) * 128],
                            identf[:]), inc="mm" if cc == 3 else None)
                    H_end = ev
                    k.w("vector", ev)
                    k.e("vector", lambda E, bi=bi, h=half, s=slot: E.tensor_add(
                        otok[s][:, h * 512:(h + 1) * 512], trB.sl(bi, 0, 512),
                        xfin[s][:, h * 512:(h + 1) * 512]))
                    k.e("vector", lambda E, h=half, s=slot: E.tensor_add(
                        otok[s][:, h * 512:(h + 1) * 512],
                        otok[s][:, h * 512:(h + 1) * 512],
                        ytk_w[s][:, h * 512:(h + 1) * 512]), inc="ev")
                    e_o = k.e("vector", lambda E, h=half, s=slot: E.tensor_add(
                        otok[s][:, h * 512:(h + 1) * 512],
                        otok[s][:, h * 512:(h + 1) * 512],
                        ytk_w[s][:, 1024 + h * 512:1024 + (h + 1) * 512]), inc="ev")
                    trB.rel(bi, e_o)
                st_ev["xfin_free"][slot] = e_o
                st_ev.setdefault("ytk_free", [None, None])[slot] = e_o
                k.w("sync", e_o)
                st_out = k.e("sync", lambda E, r=row0, s=slot: E.dma_start(
                    out=y_d[r:r + 128, :], in_=otok[s][:]), inc=f"od{slot}", amt=16)
                st_ev["otok_store"][slot] = st_out
                n_store[0] += 1
            st_ev["H_mm_end"] = H_end

        for mt in range(NMT):
            xnf_done = qk_done = v_done = aof_done = rt_done = acc_done = None
            sd_tot = None
            pre = []
            if "B" in stages:
                pre = preload_B(mt)
            if "A" in stages:
                xnf_done = stage_A(mt)
            if "B" in stages:
                qk_done, v_done = stage_B(mt, xnf_done, pre)
            if "C" in stages:
                aof_done = stage_C(mt, qk_done, v_done)
            if "E" in stages:
                rt_done = stage_E(mt)
                st_ev["routing_done"] = rt_done
            if "D" in stages:
                stage_D(mt, aof_done)
            gpre = []
            if "F" in stages:
                sd_tot, gpre = stage_F(mt, rt_done, xnf_done)
            if "G" in stages:
                acc_done = stage_G(mt, sd_tot, gpre)
            if "H" in stages:
                stage_H(mt, acc_done, sd_tot)

        if debug:
            k.w("gpsimd", ("od0", k.cnt["od0"]))
            k.w("gpsimd", ("od1", k.cnt["od1"]))
            k.e("gpsimd", lambda E: E.dma_start(out=dbg["dbg_xnf"], in_=xn_f[:]),
                inc="dbg", amt=16)
            k.e("gpsimd", lambda E: E.dma_start(out=dbg["dbg_logits"], in_=logits[:]),
                inc="dbg", amt=16)
            k.e("gpsimd", lambda E: E.dma_start(out=dbg["dbg_srow"], in_=s_tokall[:]),
                inc="dbg", amt=16)
            k.e("gpsimd", lambda E: E.dma_start(out=dbg["dbg_qf"], in_=scr[:, 0:SM]),
                inc="dbg", amt=16)
            k.e("gpsimd", lambda E: E.dma_start(out=dbg["dbg_aof"], in_=scr[:, 3 * SM:4 * SM]),
                inc="dbg", amt=16)
            k.e("gpsimd", lambda E: E.dma_start(out=dbg["dbg_acc"], in_=acc[:]),
                inc="dbg", amt=16)
            k.e("gpsimd", lambda E: E.dma_start(out=dbg["dbg_hsb"], in_=scr[:, SM:2 * SM]),
                inc="dbg", amt=16)
            k.e("gpsimd", lambda E: E.dma_start(out=dbg["dbg_zz"], in_=scr[:, 2 * SM:4 * SM]),
                inc="dbg", amt=16)
            k.w("gpsimd", ("dbg", k.cnt["dbg"]))

        k.w("sync", ("od0", k.cnt["od0"]))
        k.w("sync", ("od1", k.cnt["od1"]))

        with nc.Block() as block:
            @block.tensor
            def _(E):
                for f in k.plan["tensor"]:
                    f(E)

            @block.vector
            def _(E):
                for f in k.plan["vector"]:
                    f(E)

            @block.scalar
            def _(E):
                for f in k.plan["scalar"]:
                    f(E)

            @block.gpsimd
            def _(E):
                for f in k.plan["gpsimd"]:
                    f(E)

            @block.sync
            def _(E):
                for f in k.plan["sync"]:
                    f(E)

    return nc


def make_inmaps(inputs, tokens_per_core, n_cores):
    x = np.ascontiguousarray(np.asarray(inputs["L0_attn_x"], dtype=np.float32)
                             .reshape(-1, D))
    wqkv = np.ascontiguousarray(np.asarray(inputs["L0_attn_W_qkv"], np.float32))
    woutT = np.ascontiguousarray(np.asarray(inputs["L0_attn_W_out"], np.float32).T)
    bout = np.ascontiguousarray(np.asarray(inputs["L0_attn_b_out"], np.float32))
    gwT = np.ascontiguousarray(np.asarray(inputs["L0_ffn_gate_w"], np.float32).T)
    w1T = np.ascontiguousarray(np.asarray(inputs["L0_ffn_w1"], np.float32).transpose(0, 2, 1))
    w2T = np.ascontiguousarray(np.asarray(inputs["L0_ffn_w2"], np.float32).transpose(0, 2, 1))
    w3T = np.ascontiguousarray(np.asarray(inputs["L0_ffn_w3"], np.float32).transpose(0, 2, 1))
    b1T = np.ascontiguousarray(np.asarray(inputs["L0_ffn_b1"], np.float32).T)
    b2T = np.ascontiguousarray(np.asarray(inputs["L0_ffn_b2"], np.float32).T)
    b3T = np.ascontiguousarray(np.asarray(inputs["L0_ffn_b3"], np.float32).T)
    sw1T = np.ascontiguousarray(np.asarray(inputs["L0_ffn_sw1"], np.float32).T)
    sw2T = np.ascontiguousarray(np.asarray(inputs["L0_ffn_sw2"], np.float32).T)
    sw3T = np.ascontiguousarray(np.asarray(inputs["L0_ffn_sw3"], np.float32).T)
    shared = dict(wqkv=wqkv, woutT=woutT, bout=bout, gwT=gwT, w1T=w1T, w2T=w2T,
                  w3T=w3T, b1T=b1T, b2T=b2T, b3T=b3T, sw1T=sw1T, sw2T=sw2T,
                  sw3T=sw3T)
    in_maps = []
    for c in range(n_cores):
        m = dict(shared)
        m["x"] = np.ascontiguousarray(
            x[c * tokens_per_core:(c + 1) * tokens_per_core])
        in_maps.append(m)
    return in_maps


def kernel(**inputs):
    n_cores = 8
    tokens_per_core = 4096
    in_maps = make_inmaps(inputs, tokens_per_core, n_cores)
    nc = build_nc(tokens_per_core)
    res = run_bass_kernel_spmd(nc, in_maps, core_ids=list(range(n_cores)))
    out = np.concatenate([r["y"] for r in res.results], axis=0)
    return out.reshape(16, 2048, D).astype(np.float32)


# version 15
# speedup vs baseline: 1.1015x; 1.0018x over previous
import contextlib
import numpy as np

import concourse.bass as bass
import concourse.mybir as mybir
from concourse.bass import IndirectOffsetOnAxis
from concourse.bass_utils import run_bass_kernel_spmd

F32 = mybir.dt.float32
BF16 = mybir.dt.bfloat16
AX = mybir.AxisListType.X
AF = mybir.ActivationFunctionType
ALU = mybir.AluOpType

D = 1024
EXP = 8
II = 2048
SCALE = 1.0 / 32.0
NBUF = 3
CAP = 512
ENGS = ("tensor", "vector", "scalar", "sync", "gpsimd")


class K:

    def __init__(self):
        self.plan = {e: [] for e in ENGS}
        self.cnt = {}
        self.sems = {}

    def sem(self, name, stack, nc):
        self.sems[name] = stack.enter_context(nc.semaphore(name))
        self.cnt[name] = 0

    def w(self, eng, evt):
        if evt is None:
            return
        s, v = evt
        sem = self.sems[s]
        self.plan[eng].append(lambda E, sem=sem, v=v: E.wait_ge(sem, v))

    def wmax(self, eng, evts):
        by = {}
        for e in evts:
            if e is None:
                continue
            by[e[0]] = max(by.get(e[0], 0), e[1])
        for s, v in by.items():
            self.w(eng, (s, v))

    def e(self, eng, fn, inc=None, amt=1):
        if eng == "vector":
            self.w("vector", ("ev", self.cnt["ev"]))
            if inc is None:
                inc, amt = "ev", 1
        if inc is None:
            self.plan[eng].append(fn)
            return None
        self.cnt[inc] += amt
        v = self.cnt[inc]
        sem = self.sems[inc]
        self.plan[eng].append(lambda E, fn=fn, sem=sem, amt=amt: fn(E).then_inc(sem, amt))
        return (inc, v)


class Banks:

    def __init__(self, k, ap, nbanks, width):
        self.k, self.ap, self.n, self.w = k, ap, nbanks, width
        self.free = [[] for _ in range(nbanks)]
        self.rr = 0

    def acq(self, cols=None):
        i = self.rr % self.n
        self.rr += 1
        self.k.wmax("tensor", self.free[i])
        self.free[i] = []
        sl = self.ap[:, i * self.w:(i + 1) * self.w if cols is None else i * self.w + cols]
        return i, sl

    def sl(self, i, c0, c1):
        return self.ap[:, i * self.w + c0:i * self.w + c1]

    def rel(self, i, evt):
        self.free[i].append(evt)


def build_nc(tokens, debug=False, stages='ABCDEFGH', a_level=4, f_level=9, g_level=9):
    M = min(1024, tokens)
    NMT = tokens // M
    NT = M // 128
    NN = max(1, M // 512)
    NW = min(512, M)
    SM = 8 * M
    assert tokens >= 1024, "sparse MoE layout assumes M=1024 macro-tiles"

    nc = bass.Bass("TRN2", target_bir_lowering=False,
                   detect_race_conditions=False)
    k = K()

    x_d = nc.dram_tensor("x", [tokens, D], F32, kind="ExternalInput").ap()
    y_d = nc.dram_tensor("y", [tokens, D], F32, kind="ExternalOutput").ap()
    wqkv_d = nc.dram_tensor("wqkv", [D, 3 * D], F32, kind="ExternalInput").ap()
    woutT_d = nc.dram_tensor("woutT", [D, D], F32, kind="ExternalInput").ap()
    gwT_d = nc.dram_tensor("gwT", [D, EXP], F32, kind="ExternalInput").ap()
    w1T_d = nc.dram_tensor("w1T", [EXP, D, D], F32, kind="ExternalInput").ap()
    w2T_d = nc.dram_tensor("w2T", [EXP, D, D], F32, kind="ExternalInput").ap()
    w3T_d = nc.dram_tensor("w3T", [EXP, D, D], F32, kind="ExternalInput").ap()
    sw1T_d = nc.dram_tensor("sw1T", [D, II], F32, kind="ExternalInput").ap()
    sw2T_d = nc.dram_tensor("sw2T", [D, II], F32, kind="ExternalInput").ap()
    sw3T_d = nc.dram_tensor("sw3T", [II, D], F32, kind="ExternalInput").ap()
    b1T_d = nc.dram_tensor("b1T", [D, EXP], F32, kind="ExternalInput").ap()
    b2T_d = nc.dram_tensor("b2T", [D, EXP], F32, kind="ExternalInput").ap()
    b3T_d = nc.dram_tensor("b3T", [D, EXP], F32, kind="ExternalInput").ap()
    bout_d = nc.dram_tensor("bout", [D], F32, kind="ExternalInput").ap()
    xt_d = nc.dram_tensor("xt_scr", [tokens, D], BF16, kind="ExternalOutput").ap()
    yt_d = nc.dram_tensor("yt_scr", [2 * tokens, D], BF16, kind="ExternalOutput").ap()
    dbg = {}
    if debug:
        for nm, shp in [("dbg_xnf", [128, SM]), ("dbg_logits", [128, NT * 8]),
                        ("dbg_srow", [128, 8 * NT]), ("dbg_qf", [128, SM]),
                        ("dbg_aof", [128, SM]), ("dbg_acc", [128, SM]),
                        ("dbg_hsb", [128, SM]), ("dbg_zz", [128, 16 * M])]:
            dbg[nm] = nc.dram_tensor(nm, shp, F32, kind="ExternalOutput").ap()

    with contextlib.ExitStack() as st:
        def sb(name, shape, dt):
            return st.enter_context(nc.sbuf_tensor(name, shape, dt))

        def ps(name, shape, dt):
            return st.enter_context(nc.psum_tensor(name, shape, dt))

        scr = sb("scr", [128, 32 * M], BF16)
        qf = scr[:, 0:SM]
        kf = scr[:, SM:2 * SM]
        vtok = scr[:, 2 * SM:2 * SM + NT * 1024]
        aof = scr[:, 3 * SM:4 * SM]
        sxn_g = scr[:, 0:8 * CAP]
        h_g = scr[:, 8 * CAP:16 * CAP]
        p_mx = scr[:, 16 * CAP:16 * CAP + NT * CAP]
        xng_tok = scr[:, 24 * CAP:32 * CAP]
        sxng_tok = scr[:, 32 * CAP:40 * CAP]
        prodtok = scr[:, 40 * CAP:48 * CAP]
        prodtile = [scr[:, 48 * CAP:49 * CAP],
                    scr[:, 57 * CAP:58 * CAP]]
        zero_zone = scr[:, 0:4096]
        h_sb = scr[:, SM:2 * SM]
        zz = scr[:, 2 * SM:4 * SM]
        xn_f = sb("xn_f", [128, SM], BF16)
        acc = sb("acc", [128, SM], F32)
        wsl = [sb(f"wsl{i}", [128, 8 * 512], BF16) for i in range(NBUF)]
        xa = [sb(f"xa{i}", [128, D], F32) for i in range(2)]
        xfin = [sb(f"xfin{i}", [128, D], F32) for i in range(2)]
        otok = [sb(f"otok{i}", [128, D], F32) for i in range(2)]
        xn32 = sb("xn32", [128, D], F32)
        xntok = [sb(f"xntok{i}", [128, D], F32) for i in range(2)]
        s_tokall = sb("s_tokall", [128, 8 * NT], F32)
        slot_tokall = sb("slot_tokall", [128, 8 * NT], F32)
        m_tokall = sb("m_tokall", [128, 8 * NT], F32)
        r_tokall = sb("r_tokall", [128, 8 * NT], F32)
        m_em8 = sb("m_em8", [8, M], BF16)
        rk_em8 = sb("rk_em8", [8, M], F32)
        iotaC = sb("iotaC", [128, CAP], F32)
        idf8 = sb("idf8", [128, 8], mybir.dt.int32)
        idlo_i = sb("idlo_i", [128, 8], mybir.dt.int32)
        c31 = sb("c31", [128, 1], mybir.dt.int32)
        idhi = sb("idhi", [128, 8], BF16)
        idlo = sb("idlo", [128, 8], BF16)
        meta = sb("meta", [128, 32], BF16)
        meta32 = sb("meta32", [4, CAP], F32)
        idx_all = sb("idx_all", [128, 8 * (CAP // 128)], mybir.dt.int32)
        ixs_all = sb("ixs_all", [128, 8 * (CAP // 128)], mybir.dt.int32)
        sg_all = sb("sg_all", [128, 8 * (CAP // 128)], F32)
        ytk_w = [sb(f"ytkw{i}", [128, 2 * D], BF16) for i in range(2)]
        idxf = sb("idxf", [128, 4], F32)
        ids1f = sb("ids1f", [128, 4], F32)
        eqf = sb("eqf", [128, 4], F32)
        mtt = sb("mtt", [128, 16], F32)
        logits = sb("logits", [128, NT * 8], F32)
        sm32 = sb("sm32", [128, 256], F32)
        p32 = sb("p32", [128, 256], F32)
        pn_sb = sb("pn_sb", [128, 256], BF16)
        pt_sb = sb("pt_sb", [128, 256], BF16)
        tmp3 = [sb(f"tmp3{i}", [128, 512], BF16) for i in range(2)]
        tmp2 = [sb(f"tmp2{i}", [128, 512], BF16) for i in range(2)]
        z1t = [sb(f"z1t{i}", [128, 512], BF16) for i in range(2)]
        gw_sb = sb("gw_sb", [128, 8 * EXP], F32)
        b1_sb = sb("b1_sb", [128, 8 * EXP], F32)
        b2_sb = sb("b2_sb", [128, 8 * EXP], F32)
        b3_sb = sb("b3_sb", [128, 8 * EXP], F32)
        bout_sb = sb("bout_sb", [128, 8], F32)
        identf = sb("identf", [128, 128], F32)
        identb = sb("identb", [128, 128], BF16)
        amask = sb("amask", [128, 128], F32)

        zerocol = sb("zerocol", [128, 1], F32)
        colp = sb("colp", [128, 24], F32)
        (CSX, CMEAN, CSXX, CEX2, CM2, CVAR, CSRT, CY0, CT1, CT2, CT3, CRSTD,
         CNRM, CRSUM, CRDEN, C32, CHALF, CT4, CNRM2, CRSUM2, CRDEN2) = range(21)

        def cp(i):
            return colp[:, i:i + 1]

        pmm = ps("pmm", [128, 4 * 512], F32)
        ptr = ps("ptr", [128, 512], F32)
        pbt = ps("pbt", [128, 2048], BF16)
        psml = ps("psml", [128, 512], F32)

        for s in ("wd0", "wd1", "wd2", "ws", "xa0", "xa1", "xf0",
                  "xf1", "od0", "od1", "dbg", "mm", "ev", "sc", "gp",
                  "xt0", "xt1", "gd", "sd", "zd", "yd0", "yd1"):
            k.sem(s, st, nc)

        mmB = Banks(k, pmm, 4, 512)
        trB = Banks(k, ptr, 1, 512)
        btB = Banks(k, pbt, 2, 1024)

        e1 = k.e("gpsimd", lambda E: E.memset(identf[:], 1.0), inc="gp")
        k.w("gpsimd", e1)
        e_idf = k.e("gpsimd", lambda E: E.affine_select(
            identf[:], identf[:], [[-1, 128]], ALU.is_equal, 0.0,
            base=0, channel_multiplier=1), inc="gp")
        e2 = k.e("gpsimd", lambda E: E.memset(amask[:], 0.0), inc="gp")
        k.w("gpsimd", e2)
        e_msk = k.e("gpsimd", lambda E: E.affine_select(
            amask[:], amask[:], [[-1, 128]], ALU.is_ge, -1.0e30,
            base=0, channel_multiplier=1), inc="gp")
        e_ioc = k.e("gpsimd", lambda E: E.iota(
            iotaC[:], [[1, CAP]], base=0, channel_multiplier=0,
            allow_small_or_imprecise_dtypes=True), inc="gp")
        e_idf8 = k.e("gpsimd", lambda E: E.iota(
            idf8[:], [[128, 8]], base=1, channel_multiplier=1), inc="gp")
        k.w("vector", e_idf)
        e_idb = k.e("vector", lambda E: E.tensor_copy(identb[:], identf[:]), inc="ev")
        k.e("vector", lambda E: E.memset(zerocol[:], 0.0))
        k.e("vector", lambda E: E.memset(cp(C32), 32.0))
        k.e("vector", lambda E: E.memset(cp(CHALF), 0.5))
        k.w("vector", e_idf8)
        k.e("vector", lambda E: E.memset(c31[:], 31))
        k.e("vector", lambda E: E.tensor_single_scalar(
            idlo_i[:], idf8[:], c31[:, 0:1], ALU.bitwise_and))
        k.e("vector", lambda E: E.tensor_copy(m_tokall[:, 0:8], idf8[:]))
        k.e("vector", lambda E: E.tensor_copy(m_tokall[:, 8:16], idlo_i[:]))
        k.e("vector", lambda E: E.tensor_copy(idlo[:], idlo_i[:]))
        k.e("vector", lambda E: E.tensor_sub(
            m_tokall[:, 0:8], m_tokall[:, 0:8], m_tokall[:, 8:16]))
        k.e("vector", lambda E: E.tensor_scalar_mul(idhi[:], m_tokall[:, 0:8], 1.0 / 32.0))
        k.e("vector", lambda E: E.tensor_copy(
            meta[:].rearrange("p (t k) -> p t k", k=4)[:, 0:8, 0:1],
            idhi[:, 0:8].rearrange("p (t u) -> p t u", u=1)))
        k.e("vector", lambda E: E.tensor_copy(
            meta[:].rearrange("p (t k) -> p t k", k=4)[:, 0:8, 1:2],
            idlo[:, 0:8].rearrange("p (t u) -> p t u", u=1)))


        def sdma(dst, src):
            return k.e("gpsimd", lambda E, d=dst, s=src: E.dma_start(
                out=d, in_=s, allow_slow_non_contiguous=True), inc="ws", amt=16)

        e_gw = sdma(gw_sb[:].rearrange("p (c e) -> p c e", c=8),
                    gwT_d.rearrange("(c p) e -> p c e", p=128))
        e_b1 = sdma(b1_sb[:].rearrange("p (t e) -> p t e", t=8),
                    b1T_d.rearrange("(t p) e -> p t e", p=128))
        e_b2 = sdma(b2_sb[:].rearrange("p (t e) -> p t e", t=8),
                    b2T_d.rearrange("(t p) e -> p t e", p=128))
        e_b3 = sdma(b3_sb[:].rearrange("p (t e) -> p t e", t=8),
                    b3T_d.rearrange("(t p) e -> p t e", p=128))
        e_bo = sdma(bout_sb[:], bout_d.rearrange("(t p) -> p t", p=128))
        e_gw = e_b1 = e_b2 = e_b3 = e_bo = ("ws", k.cnt["ws"])

        e_zz0 = k.e("vector", lambda E: E.memset(zero_zone, 0.0))
        k.w("sync", e_zz0)
        for kk in range(2 * tokens // 512):
            k.e("sync", lambda E, kk=kk: E.dma_start(
                out=yt_d.rearrange("(k g p) n -> k p g n", k=2 * tokens // 512, p=128)[kk],
                in_=zero_zone.rearrange("p (g n) -> p g n", g=4)), inc="zd", amt=16)
        for kk in range(tokens // 512):
            k.e("sync", lambda E, kk=kk: E.dma_start(
                out=xt_d.rearrange("(k g p) n -> k p g n", k=tokens // 512, p=128)[kk],
                in_=zero_zone.rearrange("p (g n) -> p g n", g=4)), inc="zd", amt=16)
        e_ztot = ("zd", k.cnt["zd"])

        bc_reg = nc.alloc_register(mybir.EngineType.Pool, "bc_reg")
        k.e("gpsimd", lambda E: nc.reg_mov(bc_reg, tokens - 1))
        bc2_reg = nc.alloc_register(mybir.EngineType.Pool, "bc2_reg")
        k.e("gpsimd", lambda E: nc.reg_mov(bc2_reg, 2 * tokens - 1))

        class Slabs:
            def __init__(self):
                self.kk = 0
                self.free_evt = [None] * NBUF

            def load(self, src3d):
                buf = self.kk % NBUF
                k.w("gpsimd", self.free_evt[buf])
                evt = k.e("gpsimd", lambda E, a=src3d, b=buf: E.dma_start(
                    out=wsl[b][:].rearrange("p (c n) -> p c n", c=8), in_=a),
                    inc=f"wd{buf}", amt=16)
                self.kk += 1
                return buf, evt

        slabs = Slabs()

        def w2d(ap2d, c0):
            return ap2d[:, c0:c0 + 512].rearrange("(c p) n -> p c n", p=128)

        def w3d(ap3d, e, c0):
            return ap3d[e, :, c0:c0 + 512].rearrange("(c p) n -> p c n", p=128)

        def wrow(ap2d, r0, c0):
            return ap2d[r0:r0 + 1024, c0:c0 + 512].rearrange("(c p) n -> p c n", p=128)

        st_ev = {
            "G_mm_end": None,
            "H_mm_end": None,
            "psml_free": None,
            "attn_mm_end": None,
            "D_mm_end": None,
            "xa_free": [None, None],
            "xfin_free": [None, None],
            "otok_store": [None, None],
            "xntok_free": [None, None],
            "sxn_rd_end": None,
            "sxn_rd_end2": [None, None],
            "tmp_free": [None, None],
            "ptile_free": [None, None],
            "z1t_free": [None, None],
            "gate_prev": None,
            "xt_evts": [],
            "meta_mm_prev": None,
            "routing_done": None,
            "gd_prev": None,
            "tbl_evt": {},
            "xng_free": None,
            "sxng_rd_end": None,
            "sd_prev": None,
        }

        n_store = [0]

        def stage_A(mt):
            xnf_done = None
            x32evts = []
            xt_evts = st_ev["xt_evts"]
            xt_evts.clear()
            for tt in range(NT):
                row0 = mt * M + tt * 128
                slot = tt % 2
                k.w("sync", st_ev["xa_free"][slot])
                ld = k.e("sync", lambda E, r=row0, s=slot: E.dma_start(
                    out=xa[s][:], in_=x_d[r:r + 128, :]), inc=f"xa{slot}", amt=16)
                k.w("scalar", ld)
                if st_ev["gate_prev"] is not None:
                    k.w("scalar", st_ev["gate_prev"])
                e_sq = k.e("scalar", lambda E, s=slot: E.activation(
                    xn32[:], xa[s][:], AF.Square, accum_out=cp(CSXX)), inc="sc")
                k.w("vector", ld)
                k.e("vector", lambda E, s=slot: E.reduce_sum(cp(CSX), xa[s][:], axis=AX))
                k.e("vector", lambda E: E.tensor_scalar_mul(cp(CMEAN), cp(CSX), 1.0 / D))
                k.w("vector", e_sq)
                k.e("vector", lambda E: E.tensor_scalar_mul(cp(CEX2), cp(CSXX), 1.0 / D))
                k.e("vector", lambda E: E.tensor_mul(cp(CM2), cp(CMEAN), cp(CMEAN)))
                k.e("vector", lambda E: E.tensor_sub(cp(CVAR), cp(CEX2), cp(CM2)))
                e_var = k.e("vector", lambda E: E.tensor_scalar_add(
                    cp(CT1), cp(CVAR), 1e-5), inc="ev")
                k.w("scalar", e_var)
                e_srt = k.e("scalar", lambda E: E.activation(
                    cp(CSRT), cp(CT1), AF.Sqrt), inc="sc")
                k.w("vector", e_srt)
                e_rc = k.e("vector", lambda E: E.reciprocal(cp(CY0), cp(CSRT)),
                           inc="ev")
                k.w("vector", e_rc)
                k.e("vector", lambda E: E.tensor_mul(cp(CT2), cp(CY0), cp(CY0)))
                k.e("vector", lambda E: E.tensor_mul(cp(CT2), cp(CT2), cp(CT1)))
                k.e("vector", lambda E: E.tensor_scalar(cp(CT3), cp(CT2), -0.5, 1.5, ALU.mult, ALU.add))
                k.e("vector", lambda E: E.tensor_mul(cp(CRSTD), cp(CY0), cp(CT3)))
                k.wmax("vector", st_ev["xntok_free"][slot] or [])
                e_xn = k.e("vector", lambda E, s=slot: E.tensor_scalar(
                    xntok[s][:], xa[s][:], cp(CMEAN), cp(CRSTD),
                    ALU.subtract, ALU.mult), inc="ev")
                st_ev["xa_free"][slot] = e_xn
                k.w("gpsimd", e_xn)
                if mt == 0 and tt == 0:
                    k.w("gpsimd", e_ztot)
                e_xt = k.e("gpsimd", lambda E, r=row0, s=slot: E.dma_start(
                    out=xt_d[r:r + 128, :], in_=xntok[s][:]),
                    inc=f"xt{slot}", amt=16)
                xt_evts.append(e_xt)
                if a_level < 2:
                    continue
                k.w("tensor", e_xn)
                if mt == 0 and tt == 0:
                    k.w("tensor", e_idf)
                ev_l = None
                for half in range(2):
                    bi, _ = trB.acq()
                    for cc in range(4):
                        c = half * 4 + cc
                        fn = (lambda E, s=slot, c=c, bi=bi, cc=cc: E.transpose(
                            trB.sl(bi, cc * 128, cc * 128 + 128),
                            xntok[s][:, c * 128:(c + 1) * 128], identf[:]))
                        ev = k.e("tensor", fn, inc="mm" if cc == 3 else None)
                    ev_l = ev
                    if a_level < 3:
                        trB.rel(bi, ev)
                        continue
                    k.w("vector", ev)
                    if half == 0 and st_ev["gate_prev"] is not None:
                        k.w("vector", st_ev["gate_prev"])
                        k.w("vector", xnf_done)
                    e_v = k.e("vector", lambda E, bi=bi, h=half: E.tensor_copy(
                        xn32[:, h * 512:(h + 1) * 512], trB.sl(bi, 0, 512)), inc="ev")
                    k.w("scalar", e_v)
                    if tt == 0 and half == 0 and st_ev["G_mm_end"] is not None:
                        k.w("scalar", st_ev["G_mm_end"])
                    e_s = k.e("scalar", lambda E, h=half, tt=tt: E.activation(
                        xn_f[:, 0:SM].rearrange("p (c t) -> p c t", c=8)
                        [:, h * 4:(h + 1) * 4, tt * 128:(tt + 1) * 128],
                        xn32[:, h * 512:(h + 1) * 512].rearrange("p (c t) -> p c t", c=4),
                        AF.Copy), inc="sc")
                    trB.rel(bi, e_v)
                    x32evts.append(e_v)
                    xnf_done = e_s
                st_ev["xntok_free"][slot] = [ev_l, xt_evts[-1]]
                if a_level < 3:
                    continue
                if a_level < 4:
                    continue
                k.wmax("tensor", x32evts[-2:])
                if mt == 0 and tt == 0:
                    k.w("tensor", e_gw)
                if st_ev["psml_free"] is not None:
                    k.w("tensor", st_ev["psml_free"])
                for c in range(8):
                    ev = k.e("tensor", lambda E, c=c: E.matmul(
                        psml[:, 0:8], xn32[:, c * 128:(c + 1) * 128],
                        gw_sb[:, c * 8:(c + 1) * 8],
                        start=(c == 0), stop=(c == 7)),
                        inc="mm" if c == 7 else None)
                k.w("vector", ev)
                e_lg = k.e("vector", lambda E, tt=tt: E.tensor_copy(
                    logits[:, tt * 8:(tt + 1) * 8], psml[:, 0:8]), inc="ev")
                st_ev["psml_free"] = e_lg
                st_ev["gate_prev"] = ev
            return xnf_done

        def preload_B(mt):
            return [slabs.load(w2d(wqkv_d, j * 512)) for j in range(3)]

        def stage_B(mt, xnf_done, pre):
            qk_done = None
            v_done = None
            for j in range(6):
                buf, rdy = pre[j] if j < len(pre) else slabs.load(w2d(wqkv_d, j * 512))
                k.w("tensor", rdy)
                if j == 0:
                    k.w("tensor", xnf_done)
                last_mm = None
                if j < 4:
                    dest = qf if j < 2 else kf
                    base_oc = (j % 2) * 4
                    for o in range(4):
                        for n in range(NN):
                            bi, bsl = mmB.acq(cols=NW)
                            for c in range(8):
                                ev = k.e("tensor", lambda E, b=buf, c=c, o=o, n=n, bsl=bsl: E.matmul(
                                    bsl, wsl[b][:, c * 512 + o * 128:c * 512 + (o + 1) * 128],
                                    xn_f[:, c * M + n * 512:c * M + n * 512 + NW],
                                    start=(c == 0), stop=(c == 7)),
                                    inc="mm" if c == 7 else None)
                            last_mm = ev
                            k.w("vector", ev)
                            if mt == 0 and j == 0 and o == 0 and n == 0:
                                k.w("vector", e_ztot)
                            oc = base_oc + o
                            e_v = k.e("vector", lambda E, bi=bi, oc=oc, n=n, d=dest: E.tensor_copy(
                                d[:, oc * M + n * 512:oc * M + n * 512 + NW],
                                mmB.sl(bi, 0, NW)), inc="ev")
                            mmB.rel(bi, e_v)
                            qk_done = e_v
                else:
                    for tt in range(NT):
                        bi, bsl = mmB.acq()
                        for c in range(8):
                            ev = k.e("tensor", lambda E, b=buf, c=c, tt=tt, bsl=bsl: E.matmul(
                                bsl, xn_f[:, c * M + tt * 128:c * M + (tt + 1) * 128],
                                wsl[b][:, c * 512:(c + 1) * 512],
                                start=(c == 0), stop=(c == 7)),
                                inc="mm" if c == 7 else None)
                        last_mm = ev
                        k.w("vector", ev)
                        if j == 4 and tt == 0 and st_ev["sd_prev"] is not None:
                            k.w("vector", st_ev["sd_prev"])
                        e_v = k.e("vector", lambda E, bi=bi, tt=tt, j=j: E.tensor_copy(
                            vtok[:, tt * 1024 + (j - 4) * 512:tt * 1024 + (j - 4) * 512 + 512],
                            mmB.sl(bi, 0, 512)), inc="ev")
                        mmB.rel(bi, e_v)
                        v_done = e_v
                slabs.free_evt[buf] = last_mm
            return qk_done, v_done

        def stage_C(mt, qk_done, v_done):
            aof_done = None
            attn_end = None
            state = {}

            def scores(b):
                bi, _ = mmB.acq(cols=128)
                if b == 0:
                    k.w("tensor", qk_done)
                for c in range(8):
                    ev = k.e("tensor", lambda E, c=c, b=b, bi=bi: E.matmul(
                        mmB.sl(bi, 0, 128),
                        qf[:, c * M + b * 128:c * M + (b + 1) * 128],
                        kf[:, c * M + b * 128:c * M + (b + 1) * 128],
                        start=(c == 0), stop=(c == 7)),
                        inc="mm" if c == 7 else None)
                state[b] = (bi, ev)
                return ev

            def softmax(b):
                par = b % 2
                bi, ev = state[b]
                NRM = CNRM if par == 0 else CNRM2
                RSUM = CRSUM if par == 0 else CRSUM2
                RDEN = CRDEN if par == 0 else CRDEN2
                smv = sm32[:, par * 128:(par + 1) * 128]
                pv = p32[:, par * 128:(par + 1) * 128]
                pnv = pn_sb[:, par * 128:(par + 1) * 128]
                k.w("vector", ev)
                if b == 0 and mt == 0:
                    k.w("vector", (("gp", k.cnt["gp"])))
                k.e("vector", lambda E, bi=bi, smv=smv: E.scalar_tensor_tensor(
                    smv, mmB.sl(bi, 0, 128), SCALE, amask[:],
                    ALU.mult, ALU.add))
                e_nm = k.e("vector", lambda E, smv=smv, NRM=NRM: E.reduce_max(
                    cp(NRM), smv, axis=AX, negate=True))
                mmB.rel(bi, e_nm)
                k.w("scalar", e_nm)
                e_p = k.e("scalar", lambda E, smv=smv, pv=pv, NRM=NRM, RSUM=RSUM: E.activation(
                    pv, smv, AF.Exp, bias=cp(NRM),
                    accum_out=cp(RSUM)), inc="sc")
                k.w("vector", e_p)
                k.e("vector", lambda E, RDEN=RDEN, RSUM=RSUM: E.tensor_scalar_add(
                    cp(RDEN), cp(RSUM), 1e-6))
                e_rc = k.e("vector", lambda E, RDEN=RDEN: E.reciprocal(
                    cp(RDEN), cp(RDEN)))
                k.w("vector", e_rc)
                e_pn = k.e("vector", lambda E, pv=pv, pnv=pnv, RDEN=RDEN: E.tensor_scalar_mul(
                    pnv, pv, cp(RDEN)))
                return e_pn

            def pv_block(b, e_pn):
                nonlocal aof_done, attn_end
                sloti = b % 2
                pnv = pn_sb[:, sloti * 128:(sloti + 1) * 128]
                k.w("tensor", e_pn)
                if mt == 0 and b < 2:
                    k.w("tensor", e_idb)
                k.wmax("tensor", btB.free[sloti])
                btB.free[sloti] = []
                e_tp = k.e("tensor", lambda E, s=sloti, pnv=pnv: E.transpose(
                    pbt[:, s * 1024:s * 1024 + 128], pnv, identb[:]), inc="mm")
                k.w("vector", e_tp)
                e_pt = k.e("vector", lambda E, s=sloti: E.tensor_copy(
                    pt_sb[:, s * 128:(s + 1) * 128], pbt[:, s * 1024:s * 1024 + 128]))
                btB.free[sloti].append(e_pt)
                k.w("tensor", e_pt)
                if b == 0:
                    k.w("tensor", v_done)
                for half in range(2):
                    bi2, _ = mmB.acq()
                    for dt in range(4):
                        ev = k.e("tensor", lambda E, bi2=bi2, dt=dt, half=half, b=b, s=sloti: E.matmul(
                            mmB.sl(bi2, dt * 128, (dt + 1) * 128),
                            vtok[:, b * 1024 + (half * 4 + dt) * 128:b * 1024 + (half * 4 + dt + 1) * 128],
                            pt_sb[:, s * 128:(s + 1) * 128],
                            start=True, stop=True),
                            inc="mm" if dt == 3 else None)
                    attn_end = ev
                    k.w("scalar", ev)
                    e_a = k.e("scalar", lambda E, bi2=bi2, half=half, b=b: E.activation(
                        aof[:, 0:SM].rearrange("p (c t) -> p c t", c=8)
                        [:, half * 4:(half + 1) * 4, b * 128:(b + 1) * 128],
                        mmB.sl(bi2, 0, 512).rearrange("p (c t) -> p c t", c=4),
                        AF.Copy), inc="sc")
                    mmB.rel(bi2, e_a)
                    aof_done = e_a

            scores(0)
            pend = softmax(0)
            for b in range(1, NT):
                scores(b)
                pv_block(b - 1, pend)
                pend = softmax(b)
            pv_block(NT - 1, pend)
            st_ev["attn_mm_end"] = attn_end
            return aof_done

        def table_build(mt, e):
            NG = CAP // 128
            if e == 0:
                k.w("vector", st_ev["routing_done"])
                k.w("vector", st_ev["attn_mm_end"])
                if st_ev["sd_prev"] is not None:
                    k.w("vector", st_ev["sd_prev"])
                if st_ev["gd_prev"] is not None:
                    k.w("vector", st_ev["gd_prev"])
            else:
                k.w("vector", st_ev["meta_mm_prev"])
            for tc in range(NT):
                e_pm = k.e("vector", lambda E, tc=tc, e=e: E.tensor_scalar(
                    p_mx[:, tc * CAP:(tc + 1) * CAP], iotaC[:, 0:CAP],
                    r_tokall[:, tc * 8 + e:tc * 8 + e + 1],
                    m_tokall[:, tc * 8 + e:tc * 8 + e + 1],
                    ALU.is_equal, ALU.mult))
            k.e("vector", lambda E, e=e: E.tensor_copy(
                meta[:].rearrange("p (t k) -> p t k", k=4)[:, 0:NT, 2:3],
                s_tokall[:].rearrange("p (t q) -> p t q", q=8)[:, 0:NT, e:e + 1]))
            e_meta = k.e("vector", lambda E, e=e: E.tensor_copy(
                meta[:].rearrange("p (t k) -> p t k", k=4)[:, 0:NT, 3:4],
                slot_tokall[:].rearrange("p (t q) -> p t q", q=8)[:, 0:NT, e:e + 1]))
            k.w("tensor", e_pm)
            k.w("tensor", e_meta)
            if st_ev["psml_free"] is not None:
                k.w("tensor", st_ev["psml_free"])
            for tc in range(NT):
                mev = k.e("tensor", lambda E, tc=tc: E.matmul(
                    psml[0:4, 0:CAP], meta[:, tc * 4:(tc + 1) * 4],
                    p_mx[:, tc * CAP:(tc + 1) * CAP],
                    start=(tc == 0), stop=(tc == NT - 1)),
                    inc="mm" if tc == NT - 1 else None)
            st_ev["meta_mm_prev"] = mev
            k.w("vector", mev)
            e_m32 = k.e("vector", lambda E: E.tensor_copy(
                meta32[0:4, 0:CAP], psml[0:4, 0:CAP]))
            k.w("tensor", e_m32)
            for g in range(NG):
                tev = k.e("tensor", lambda E, g=g: E.transpose(
                    psml[0:128, g * 4:(g + 1) * 4],
                    meta32[0:4, g * 128:(g + 1) * 128],
                    identf[0:4, 0:4]), inc="mm" if g == NG - 1 else None)
            k.w("vector", tev)
            k.e("vector", lambda E: E.tensor_copy(
                mtt[:, 0:4 * NG], psml[:, 0:4 * NG]))
            k.e("vector", lambda E: E.scalar_tensor_tensor(
                ids1f[:, 0:NG], mtt[:, 0:4 * NG:4], 32.0,
                mtt[:, 1:4 * NG:4], ALU.mult, ALU.add))
            k.e("vector", lambda E: E.tensor_single_scalar(
                eqf[:, 0:NG], ids1f[:, 0:NG], cp(CHALF), ALU.is_lt))
            k.e("vector", lambda E, mt=mt: E.tensor_scalar_add(
                idxf[:, 0:NG], ids1f[:, 0:NG], float(mt * M - 1)))
            k.e("vector", lambda E: E.scalar_tensor_tensor(
                idxf[:, 0:NG], eqf[:, 0:NG], float(tokens + 1), idxf[:, 0:NG],
                ALU.mult, ALU.add))
            e_idx = k.e("vector", lambda E, e=e: E.tensor_copy(
                idx_all[:, e * NG:(e + 1) * NG], idxf[:, 0:NG]))
            k.e("vector", lambda E: E.scalar_tensor_tensor(
                idxf[:, 0:NG], ids1f[:, 0:NG], 2.0, mtt[:, 3:4 * NG:4],
                ALU.mult, ALU.add))
            k.e("vector", lambda E, mt=mt: E.tensor_scalar_add(
                idxf[:, 0:NG], idxf[:, 0:NG], float(2 * mt * M - 2)))
            k.e("vector", lambda E: E.scalar_tensor_tensor(
                idxf[:, 0:NG], eqf[:, 0:NG], float(2 * tokens + 4), idxf[:, 0:NG],
                ALU.mult, ALU.add))
            k.e("vector", lambda E, e=e: E.tensor_copy(
                ixs_all[:, e * NG:(e + 1) * NG], idxf[:, 0:NG]))
            e_sg = k.e("vector", lambda E, e=e: E.tensor_copy(
                sg_all[:, e * NG:(e + 1) * NG], mtt[:, 2:4 * NG:4]))
            st_ev["psml_free"] = e_sg
            st_ev["tbl_evt"][e] = e_sg

        def stage_D(mt, aof_done):
            D_end = None
            acc_evts = []
            it = [0]
            for j in range(2):
                buf, rdy = slabs.load(w2d(woutT_d, j * 512))
                k.w("tensor", rdy)
                if j == 0:
                    k.w("tensor", aof_done)
                last_mm = None
                for o in range(4):
                    for n in range(NN):
                        bi, bsl = mmB.acq(cols=NW)
                        for c in range(8):
                            ev = k.e("tensor", lambda E, b=buf, c=c, o=o, n=n, bsl=bsl: E.matmul(
                                bsl, wsl[b][:, c * 512 + o * 128:c * 512 + (o + 1) * 128],
                                aof[:, c * M + n * 512:c * M + n * 512 + NW],
                                start=(c == 0), stop=(c == 7)),
                                inc="mm" if c == 7 else None)
                        last_mm = ev
                        D_end = ev
                        oc = j * 4 + o
                        k.w("scalar", ev)
                        if j == 0 and o == 0 and n == 0:
                            if st_ev["H_mm_end"] is not None:
                                k.w("scalar", st_ev["H_mm_end"])
                            if mt == 0:
                                k.w("scalar", e_bo)
                        e_s = k.e("scalar", lambda E, bi=bi, oc=oc, n=n: E.activation(
                            acc[:, oc * M + n * 512:oc * M + n * 512 + NW],
                            mmB.sl(bi, 0, NW), AF.Identity,
                            bias=bout_sb[:, oc:oc + 1]), inc="sc")
                        mmB.rel(bi, e_s)
                        acc_evts.append(e_s)
                        if it[0] % 2 == 0 and it[0] // 2 < EXP:
                            table_build(mt, it[0] // 2)
                        it[0] += 1
                slabs.free_evt[buf] = last_mm
            st_ev["D_mm_end"] = D_end
            return acc_evts[-1]

        def stage_E(mt):
            srow_done = None
            for tt in range(NT):
                slot = tt % 2
                e_nm = k.e("vector", lambda E, tt=tt: E.reduce_max(
                    cp(CNRM), logits[:, tt * 8:(tt + 1) * 8], axis=AX, negate=True),
                    inc="ev")
                k.w("scalar", e_nm)
                e_p = k.e("scalar", lambda E, tt=tt: E.activation(
                    p32[:, 0:8],
                    logits[:, tt * 8:(tt + 1) * 8], AF.Exp, bias=cp(CNRM),
                    accum_out=cp(CRSUM)), inc="sc")
                k.w("vector", e_p)
                e_rc = k.e("vector", lambda E: E.reciprocal(cp(CRDEN), cp(CRSUM)),
                           inc="ev")
                k.w("vector", e_rc)
                k.e("vector", lambda E: E.tensor_scalar_mul(
                    sm32[:, 0:8], p32[:, 0:8], cp(CRDEN)))
                k.e("vector", lambda E: E.reduce_max(cp(CT1), sm32[:, 0:8], axis=AX))
                k.e("vector", lambda E: E.tensor_single_scalar(
                    sm32[:, 8:16], sm32[:, 0:8], cp(CT1), ALU.is_ge))
                k.e("vector", lambda E: E.scalar_tensor_tensor(
                    sm32[:, 16:24], sm32[:, 8:16], -1.0e30, sm32[:, 0:8],
                    ALU.mult, ALU.add))
                k.e("vector", lambda E: E.reduce_max(cp(CT2), sm32[:, 16:24], axis=AX))
                k.e("vector", lambda E: E.tensor_single_scalar(
                    sm32[:, 8:16], sm32[:, 0:8], cp(CT2), ALU.is_ge))
                e_s = k.e("vector", lambda E, tt=tt: E.tensor_mul(
                    s_tokall[:, tt * 8:tt * 8 + 8], sm32[:, 0:8], sm32[:, 8:16]))
                e_m = k.e("vector", lambda E, tt=tt: E.tensor_single_scalar(
                    m_tokall[:, tt * 8:tt * 8 + 8],
                    s_tokall[:, tt * 8:tt * 8 + 8],
                    zerocol[:, 0:1], ALU.is_gt))
                k.e("vector", lambda E, tt=tt: E.tensor_tensor_scan(
                    slot_tokall[:, tt * 8:tt * 8 + 8],
                    m_tokall[:, tt * 8:tt * 8 + 8],
                    m_tokall[:, tt * 8:tt * 8 + 8], 0.0, ALU.add, ALU.bypass))
                k.e("vector", lambda E, tt=tt: E.tensor_sub(
                    slot_tokall[:, tt * 8:tt * 8 + 8],
                    slot_tokall[:, tt * 8:tt * 8 + 8],
                    m_tokall[:, tt * 8:tt * 8 + 8]))
                k.w("tensor", e_m)
                if st_ev["psml_free"] is not None:
                    k.w("tensor", st_ev["psml_free"])
                ev = k.e("tensor", lambda E, tt=tt: E.transpose(
                    psml[0:8, 0:128], m_tokall[:, tt * 8:tt * 8 + 8],
                    identf[:]), inc="mm")
                k.w("vector", ev)
                e_r = k.e("vector", lambda E, tt=tt: E.tensor_copy(
                    m_em8[0:8, tt * 128:(tt + 1) * 128], psml[0:8, 0:128]))
                st_ev["psml_free"] = e_r
            k.e("vector", lambda E: E.tensor_tensor_scan(
                rk_em8[0:8, :], m_em8[0:8, :], m_em8[0:8, :], 0.0,
                ALU.add, ALU.bypass))
            e_r0 = k.e("vector", lambda E: E.tensor_sub(
                rk_em8[0:8, :], rk_em8[0:8, :], m_em8[0:8, :]))
            rt_done = None
            for tt in range(NT):
                k.w("tensor", e_r0)
                k.w("tensor", st_ev["psml_free"])
                ev = k.e("tensor", lambda E, tt=tt: E.transpose(
                    psml[0:128, 0:8], rk_em8[0:8, tt * 128:(tt + 1) * 128],
                    identf[0:8, 0:8]), inc="mm")
                k.w("vector", ev)
                rt_done = k.e("vector", lambda E, tt=tt: E.tensor_copy(
                    r_tokall[:, tt * 8:(tt + 1) * 8], psml[0:128, 0:8]))
                st_ev["psml_free"] = rt_done
            return rt_done

        def stage_F(mt, routing_done, xnf_done):
            NG = CAP // 128
            sd_after = {}
            sxn_gbuf = [sxn_g, scr[:, 49 * CAP:57 * CAP]]

            def prologue1(e):
                k.w("gpsimd", st_ev["tbl_evt"][e])
                k.wmax("gpsimd", st_ev["xt_evts"])
                if st_ev["xng_free"] is not None:
                    k.w("gpsimd", st_ev["xng_free"])
                if e == 0:
                    k.w("gpsimd", st_ev["attn_mm_end"])
                for g in range(NG):
                    k.e("gpsimd", lambda E, g=g, e=e: E.indirect_dma_start(
                        out=xng_tok[:, g * 1024:(g + 1) * 1024], out_offset=None,
                        in_=xt_d,
                        in_offset=IndirectOffsetOnAxis(
                            ap=idx_all[:, e * NG + g:e * NG + g + 1], axis=0),
                        bounds_check=bc_reg, oob_is_err=False), inc="gd", amt=16)
                return ("gd", k.cnt["gd"])

            def prologue2(e, gd_tot):
                slot = e % 2
                sxn = sxn_gbuf[slot]
                k.w("vector", gd_tot)
                if st_ev["sxng_rd_end"] is not None:
                    k.w("vector", st_ev["sxng_rd_end"])
                if e == 0:
                    k.w("vector", st_ev["G_mm_end"])
                e_sc = None
                for g in range(NG):
                    e_sc = k.e("vector", lambda E, g=g, e=e: E.tensor_scalar_mul(
                        sxng_tok[:, g * 1024:(g + 1) * 1024],
                        xng_tok[:, g * 1024:(g + 1) * 1024],
                        sg_all[:, e * NG + g:e * NG + g + 1]))
                st_ev["xng_free"] = e_sc
                res = {"last": None, "tev": None}

                def mkround(dc):
                    def em():
                        if dc == 0:
                            k.w("tensor", e_sc)
                            if e == 0 and mt == 0:
                                k.w("tensor", e_idb)
                        bi, _ = btB.acq()
                        for g in range(NG):
                            tev = k.e("tensor", lambda E, dc=dc, g=g, bi=bi: E.transpose(
                                pbt[:, bi * 1024 + g * 128:bi * 1024 + (g + 1) * 128],
                                sxng_tok[:, g * 1024 + dc * 128:g * 1024 + (dc + 1) * 128],
                                identb[:]), inc="mm" if g == NG - 1 else None)
                        res["tev"] = tev
                        k.w("scalar", tev)
                        if dc == 0:
                            if e == 0:
                                k.w("scalar", st_ev["attn_mm_end"])
                            k.w("scalar", st_ev["sxn_rd_end2"][slot])
                        res["last"] = k.e("scalar", lambda E, dc=dc, bi=bi, sxn=sxn: E.copy(
                            sxn[:, dc * CAP:(dc + 1) * CAP],
                            pbt[:, bi * 1024:bi * 1024 + CAP]), inc="sc")
                        btB.rel(bi, res["last"])
                    return em
                return [mkround(dc) for dc in range(8)], res

            prep = {}
            ptr_pend = []
            sxn_pend = []
            w1pre = {0: [slabs.load(w3d(w1T_d, 0, 0))]}
            gpre = []
            gd0 = prologue1(0)
            rounds0, res0 = prologue2(0, gd0)
            for f in rounds0:
                f()
            prep[0] = res0
            for e in range(EXP):
                slot = e % 2
                sxn = sxn_gbuf[slot]
                res_e = prep.pop(e)
                last_sxn_wr = res_e["last"]
                h_done = None
                pre1 = w1pre.pop(e, [])
                for j in range(2):
                    buf, rdy = pre1[j] if j < len(pre1) else slabs.load(w3d(w1T_d, e, j * 512))
                    k.w("tensor", rdy)
                    if j == 0:
                        k.w("tensor", last_sxn_wr)
                    for o in range(4):
                        oc = j * 4 + o
                        bi, bsl = mmB.acq(cols=CAP)
                        for c in range(8):
                            ev = k.e("tensor", lambda E, b=buf, c=c, o=o, bsl=bsl, sxn=sxn: E.matmul(
                                bsl, wsl[b][:, c * 512 + o * 128:c * 512 + (o + 1) * 128],
                                sxn[:, c * CAP:c * CAP + CAP],
                                start=(c == 0), stop=(c == 7)),
                                inc="mm" if c == 7 else None)
                        k.w("scalar", ev)
                        if mt == 0 and e == 0 and oc == 0:
                            k.w("scalar", e_b1)
                            k.w("scalar", e_b2)
                            k.w("scalar", e_b3)
                            k.w("scalar", st_ev["attn_mm_end"])
                        e_h = k.e("scalar", lambda E, bi=bi, oc=oc, e=e: E.activation(
                            h_g[:, oc * CAP:oc * CAP + CAP],
                            mmB.sl(bi, 0, CAP), AF.Silu,
                            bias=b1_sb[:, oc * 8 + e:oc * 8 + e + 1]), inc="sc")
                        mmB.rel(bi, e_h)
                        h_done = e_h
                    slabs.free_evt[buf] = ev
                w3bufs = {}
                w2bufs = {}
                prod_done = None
                gd_next = None
                for oc in range(8):
                    j = oc // 4
                    if j not in w3bufs:
                        w3bufs[j] = slabs.load(w3d(w3T_d, e, j * 512))
                        w2bufs[j] = slabs.load(w3d(w2T_d, e, j * 512))
                    b3buf, rdy3 = w3bufs[j]
                    b2buf, rdy2 = w2bufs[j]
                    bi3, bsl3 = mmB.acq(cols=CAP)
                    if oc % 4 == 0:
                        k.w("tensor", rdy3)
                    for c in range(8):
                        ev3 = k.e("tensor", lambda E, b=b3buf, c=c, oc=oc, bsl=bsl3, sxn=sxn: E.matmul(
                            bsl, wsl[b][:, c * 512 + (oc % 4) * 128:c * 512 + (oc % 4 + 1) * 128],
                            sxn[:, c * CAP:c * CAP + CAP],
                            start=(c == 0), stop=(c == 7)),
                            inc="mm" if c == 7 else None)
                    bi2, bsl2 = mmB.acq(cols=CAP)
                    if oc % 4 == 0:
                        k.w("tensor", rdy2)
                    if oc == 0:
                        k.w("tensor", h_done)
                    for c in range(8):
                        ev2 = k.e("tensor", lambda E, b=b2buf, c=c, oc=oc, bsl=bsl2: E.matmul(
                            bsl, wsl[b][:, c * 512 + (oc % 4) * 128:c * 512 + (oc % 4 + 1) * 128],
                            h_g[:, c * CAP:c * CAP + CAP],
                            start=(c == 0), stop=(c == 7)),
                            inc="mm" if c == 7 else None)
                    i2 = oc % 2
                    k.w("scalar", ev3)
                    if st_ev["tmp_free"][i2] is not None:
                        k.w("scalar", st_ev["tmp_free"][i2])
                    e_x3 = k.e("scalar", lambda E, bi3=bi3, oc=oc, e=e, i2=i2: E.activation(
                        tmp3[i2][:, 0:CAP], mmB.sl(bi3, 0, CAP), AF.Identity,
                        bias=b3_sb[:, oc * 8 + e:oc * 8 + e + 1]), inc="sc")
                    mmB.rel(bi3, e_x3)
                    k.w("scalar", ev2)
                    e_x2 = k.e("scalar", lambda E, bi2=bi2, oc=oc, e=e, i2=i2: E.activation(
                        tmp2[i2][:, 0:CAP], mmB.sl(bi2, 0, CAP), AF.Identity,
                        bias=b2_sb[:, oc * 8 + e:oc * 8 + e + 1]), inc="sc")
                    mmB.rel(bi2, e_x2)
                    k.w("vector", e_x3)
                    k.w("vector", e_x2)
                    if oc == 0 and e == 0:
                        k.w("vector", st_ev["D_mm_end"])
                    if st_ev["ptile_free"][i2] is not None:
                        k.w("vector", st_ev["ptile_free"][i2])
                    e_pr = k.e("vector", lambda E, i2=i2: E.tensor_mul(
                        prodtile[i2][:, 0:CAP], tmp3[i2][:, 0:CAP], tmp2[i2][:, 0:CAP]))
                    st_ev["tmp_free"][i2] = e_pr

                    def emit_ptr(oc=oc, i2=i2, e_pr=e_pr, e=e):
                        k.w("tensor", e_pr)
                        bi, _ = btB.acq()
                        for g in range(NG):
                            tev = k.e("tensor", lambda E, g=g, bi=bi, i2=i2: E.transpose(
                                pbt[:, bi * 1024 + g * 128:bi * 1024 + (g + 1) * 128],
                                prodtile[i2][:, g * 128:(g + 1) * 128],
                                identb[:]), inc="mm" if g == NG - 1 else None)
                        k.w("scalar", tev)
                        if oc == 0:
                            if e == 0:
                                if st_ev["sd_prev"] is not None:
                                    k.w("scalar", st_ev["sd_prev"])
                            elif sd_after.get(e - 1) is not None:
                                k.w("scalar", sd_after[e - 1])
                        pd = k.e("scalar", lambda E, oc=oc, bi=bi: E.copy(
                            prodtok[:].rearrange("p (g n) -> p g n", g=NG)
                            [:, :, oc * 128:(oc + 1) * 128],
                            pbt[:, bi * 1024:bi * 1024 + CAP]
                            .rearrange("p (g n) -> p g n", g=NG)), inc="sc")
                        btB.rel(bi, pd)
                        st_ev["ptile_free"][i2] = pd
                        return pd

                    ptr_pend.append(emit_ptr)
                    if len(ptr_pend) > 1:
                        prod_done = ptr_pend.pop(0)()
                    if oc % 4 == 3:
                        slabs.free_evt[b3buf] = ev3
                        slabs.free_evt[b2buf] = ev2
                    if e + 1 < EXP:
                        if oc == 0:
                            gd_next = prologue1(e + 1)
                        elif oc == 1:
                            rounds, res_n = prologue2(e + 1, gd_next)
                            sxn_pend.extend(rounds)
                            prep[e + 1] = res_n
                        elif oc == 3:
                            w1pre[e + 1] = [slabs.load(w3d(w1T_d, e + 1, 0))]
                    elif oc == 5:
                        gpre.append(slabs.load(w2d(sw1T_d, 0)))
                    if sxn_pend:
                        sxn_pend.pop(0)()
                        if oc >= 5 and sxn_pend:
                            sxn_pend.pop(0)()
                while ptr_pend:
                    prod_done = ptr_pend.pop(0)()
                while sxn_pend:
                    sxn_pend.pop(0)()
                if e + 1 < EXP:
                    st_ev["sxng_rd_end"] = prep[e + 1]["tev"]
                st_ev["sxn_rd_end"] = ev3
                st_ev["sxn_rd_end2"][slot] = ev3
                k.w("gpsimd", prod_done)
                if mt == 0 and e == 0:
                    k.w("gpsimd", e_ztot)
                for g in range(NG):
                    k.e("gpsimd", lambda E, g=g, e=e: E.indirect_dma_start(
                        out=yt_d,
                        out_offset=IndirectOffsetOnAxis(
                            ap=ixs_all[:, e * NG + g:e * NG + g + 1], axis=0),
                        in_=prodtok[:, g * 1024:(g + 1) * 1024], in_offset=None,
                        bounds_check=bc2_reg, oob_is_err=False), inc="sd", amt=16)
                sd_after[e] = ("sd", k.cnt["sd"])
            st_ev["sd_prev"] = ("sd", k.cnt["sd"])
            st_ev["gd_prev"] = ("gd", k.cnt["gd"])
            return ("sd", k.cnt["sd"]), gpre

        def stage_G(mt, sd_tot, gpre):
            zi = [0]
            zz_done = None
            for j in range(4):
                if j == 0 and len(gpre) == 1:
                    b1buf, rdy1 = gpre[0]
                    b2buf, rdy2 = slabs.load(w2d(sw2T_d, 0))
                else:
                    b1buf, rdy1 = slabs.load(w2d(sw1T_d, j * 512))
                    b2buf, rdy2 = slabs.load(w2d(sw2T_d, j * 512))
                for it_in in range(4):
                    it = j * 4 + it_in
                    for n in range(NN):
                        bia, bsla = mmB.acq(cols=NW)
                        if it_in == 0 and n == 0:
                            k.w("tensor", rdy1)
                        for c in range(8):
                            ev1 = k.e("tensor", lambda E, b=b1buf, c=c, it_in=it_in, n=n, bsl=bsla: E.matmul(
                                bsl, wsl[b][:, c * 512 + it_in * 128:c * 512 + (it_in + 1) * 128],
                                xn_f[:, c * M + n * 512:c * M + n * 512 + NW],
                                start=(c == 0), stop=(c == 7)),
                                inc="mm" if c == 7 else None)
                        bib, bslb = mmB.acq(cols=NW)
                        if it_in == 0 and n == 0:
                            k.w("tensor", rdy2)
                        for c in range(8):
                            ev2 = k.e("tensor", lambda E, b=b2buf, c=c, it_in=it_in, n=n, bsl=bslb: E.matmul(
                                bsl, wsl[b][:, c * 512 + it_in * 128:c * 512 + (it_in + 1) * 128],
                                xn_f[:, c * M + n * 512:c * M + n * 512 + NW],
                                start=(c == 0), stop=(c == 7)),
                                inc="mm" if c == 7 else None)
                        i2 = zi[0] % 2
                        k.w("scalar", ev1)
                        if st_ev["z1t_free"][i2] is not None:
                            k.w("scalar", st_ev["z1t_free"][i2])
                        e_z1 = k.e("scalar", lambda E, bia=bia, i2=i2: E.activation(
                            z1t[i2][:, 0:NW], mmB.sl(bia, 0, NW), AF.Silu), inc="sc")
                        mmB.rel(bia, e_z1)
                        k.w("vector", ev2)
                        k.w("vector", e_z1)
                        if it == 0 and n == 0:
                            k.w("vector", st_ev["D_mm_end"])
                            k.w("vector", sd_tot)
                        e_zz = k.e("vector", lambda E, bib=bib, it=it, n=n, i2=i2: E.tensor_mul(
                            zz[:, it * M + n * 512:it * M + n * 512 + NW],
                            z1t[i2][:, 0:NW], mmB.sl(bib, 0, NW)), inc="ev")
                        mmB.rel(bib, e_zz)
                        st_ev["z1t_free"][i2] = e_zz
                        zz_done = e_zz
                        zi[0] += 1
                slabs.free_evt[b1buf] = ev1
                slabs.free_evt[b2buf] = ev2
            if g_level < 2:
                st_ev["G_mm_end"] = ev2
                return zz_done
            G_end = None
            acc_done = None
            for j in range(2):
                br0, rdy0 = slabs.load(wrow(sw3T_d, 0, j * 512))
                br1, rdy1 = slabs.load(wrow(sw3T_d, 1024, j * 512))
                for o in range(4):
                    oc = j * 4 + o
                    for n in range(NN):
                        bi, bsl = mmB.acq(cols=NW)
                        if o == 0 and n == 0:
                            k.w("tensor", rdy0)
                            k.w("tensor", rdy1)
                            k.w("tensor", zz_done)
                        for half, bb in ((0, br0), (1, br1)):
                            for c in range(8):
                                i = half * 8 + c
                                ev = k.e("tensor", lambda E, b=bb, c=c, o=o, n=n, i=i, bsl=bsl: E.matmul(
                                    bsl, wsl[b][:, c * 512 + o * 128:c * 512 + (o + 1) * 128],
                                    zz[:, i * M + n * 512:i * M + n * 512 + NW],
                                    start=(i == 0), stop=(i == 15)),
                                    inc="mm" if i == 15 else None)
                        G_end = ev
                        k.w("vector", ev)
                        e_a = k.e("vector", lambda E, bi=bi, oc=oc, n=n: E.tensor_add(
                            acc[:, oc * M + n * 512:oc * M + n * 512 + NW],
                            acc[:, oc * M + n * 512:oc * M + n * 512 + NW],
                            mmB.sl(bi, 0, NW)), inc="ev")
                        mmB.rel(bi, e_a)
                        acc_done = e_a
                slabs.free_evt[br0] = G_end
                slabs.free_evt[br1] = G_end
            st_ev["G_mm_end"] = G_end
            return acc_done

        def stage_H(mt, acc_done, sd_tot):
            H_end = None
            for tt in range(NT):
                row0 = mt * M + tt * 128
                slot = tt % 2
                k.w("sync", st_ev["xfin_free"][slot])
                ld = k.e("sync", lambda E, r=row0, s=slot: E.dma_start(
                    out=xfin[s][:], in_=x_d[r:r + 128, :]), inc=f"xf{slot}", amt=16)
                if tt == 0:
                    k.w("sync", sd_tot)
                if st_ev.get("ytk_free", [None, None])[slot] is not None:
                    k.w("sync", st_ev["ytk_free"][slot])
                ld_y = k.e("sync", lambda E, r=row0, s=slot: E.dma_start(
                    out=ytk_w[s][:].rearrange("p (two n) -> p two n", two=2),
                    in_=yt_d.rearrange("(t two) n -> t two n", two=2)[r:r + 128]),
                    inc=f"yd{slot}", amt=16)
                k.w("tensor", acc_done)
                k.w("vector", ld)
                k.w("vector", ld_y)
                if st_ev["otok_store"][slot] is not None:
                    k.w("vector", st_ev["otok_store"][slot])
                for half in range(2):
                    bi, _ = trB.acq()
                    for cc in range(4):
                        c = half * 4 + cc
                        ev = k.e("tensor", lambda E, c=c, tt=tt, bi=bi, cc=cc: E.transpose(
                            trB.sl(bi, cc * 128, cc * 128 + 128),
                            acc[:, c * M + tt * 128:c * M + (tt + 1) * 128],
                            identf[:]), inc="mm" if cc == 3 else None)
                    H_end = ev
                    k.w("vector", ev)
                    k.e("vector", lambda E, bi=bi, h=half, s=slot: E.tensor_add(
                        otok[s][:, h * 512:(h + 1) * 512], trB.sl(bi, 0, 512),
                        xfin[s][:, h * 512:(h + 1) * 512]))
                    k.e("vector", lambda E, h=half, s=slot: E.tensor_add(
                        otok[s][:, h * 512:(h + 1) * 512],
                        otok[s][:, h * 512:(h + 1) * 512],
                        ytk_w[s][:, h * 512:(h + 1) * 512]), inc="ev")
                    e_o = k.e("vector", lambda E, h=half, s=slot: E.tensor_add(
                        otok[s][:, h * 512:(h + 1) * 512],
                        otok[s][:, h * 512:(h + 1) * 512],
                        ytk_w[s][:, 1024 + h * 512:1024 + (h + 1) * 512]), inc="ev")
                    trB.rel(bi, e_o)
                st_ev["xfin_free"][slot] = e_o
                st_ev.setdefault("ytk_free", [None, None])[slot] = e_o
                k.w("sync", e_o)
                st_out = k.e("sync", lambda E, r=row0, s=slot: E.dma_start(
                    out=y_d[r:r + 128, :], in_=otok[s][:]), inc=f"od{slot}", amt=16)
                st_ev["otok_store"][slot] = st_out
                n_store[0] += 1
            st_ev["H_mm_end"] = H_end

        for mt in range(NMT):
            xnf_done = qk_done = v_done = aof_done = rt_done = acc_done = None
            sd_tot = None
            pre = []
            if "B" in stages:
                pre = preload_B(mt)
            if "A" in stages:
                xnf_done = stage_A(mt)
            if "B" in stages:
                qk_done, v_done = stage_B(mt, xnf_done, pre)
            if "C" in stages:
                aof_done = stage_C(mt, qk_done, v_done)
            if "E" in stages:
                rt_done = stage_E(mt)
                st_ev["routing_done"] = rt_done
            if "D" in stages:
                stage_D(mt, aof_done)
            gpre = []
            if "F" in stages:
                sd_tot, gpre = stage_F(mt, rt_done, xnf_done)
            if "G" in stages:
                acc_done = stage_G(mt, sd_tot, gpre)
            if "H" in stages:
                stage_H(mt, acc_done, sd_tot)

        if debug:
            k.w("gpsimd", ("od0", k.cnt["od0"]))
            k.w("gpsimd", ("od1", k.cnt["od1"]))
            k.e("gpsimd", lambda E: E.dma_start(out=dbg["dbg_xnf"], in_=xn_f[:]),
                inc="dbg", amt=16)
            k.e("gpsimd", lambda E: E.dma_start(out=dbg["dbg_logits"], in_=logits[:]),
                inc="dbg", amt=16)
            k.e("gpsimd", lambda E: E.dma_start(out=dbg["dbg_srow"], in_=s_tokall[:]),
                inc="dbg", amt=16)
            k.e("gpsimd", lambda E: E.dma_start(out=dbg["dbg_qf"], in_=scr[:, 0:SM]),
                inc="dbg", amt=16)
            k.e("gpsimd", lambda E: E.dma_start(out=dbg["dbg_aof"], in_=scr[:, 3 * SM:4 * SM]),
                inc="dbg", amt=16)
            k.e("gpsimd", lambda E: E.dma_start(out=dbg["dbg_acc"], in_=acc[:]),
                inc="dbg", amt=16)
            k.e("gpsimd", lambda E: E.dma_start(out=dbg["dbg_hsb"], in_=scr[:, SM:2 * SM]),
                inc="dbg", amt=16)
            k.e("gpsimd", lambda E: E.dma_start(out=dbg["dbg_zz"], in_=scr[:, 2 * SM:4 * SM]),
                inc="dbg", amt=16)
            k.w("gpsimd", ("dbg", k.cnt["dbg"]))

        k.w("sync", ("od0", k.cnt["od0"]))
        k.w("sync", ("od1", k.cnt["od1"]))

        with nc.Block() as block:
            @block.tensor
            def _(E):
                for f in k.plan["tensor"]:
                    f(E)

            @block.vector
            def _(E):
                for f in k.plan["vector"]:
                    f(E)

            @block.scalar
            def _(E):
                for f in k.plan["scalar"]:
                    f(E)

            @block.gpsimd
            def _(E):
                for f in k.plan["gpsimd"]:
                    f(E)

            @block.sync
            def _(E):
                for f in k.plan["sync"]:
                    f(E)

    return nc


def make_inmaps(inputs, tokens_per_core, n_cores):
    x = np.ascontiguousarray(np.asarray(inputs["L0_attn_x"], dtype=np.float32)
                             .reshape(-1, D))
    wqkv = np.ascontiguousarray(np.asarray(inputs["L0_attn_W_qkv"], np.float32))
    woutT = np.ascontiguousarray(np.asarray(inputs["L0_attn_W_out"], np.float32).T)
    bout = np.ascontiguousarray(np.asarray(inputs["L0_attn_b_out"], np.float32))
    gwT = np.ascontiguousarray(np.asarray(inputs["L0_ffn_gate_w"], np.float32).T)
    w1T = np.ascontiguousarray(np.asarray(inputs["L0_ffn_w1"], np.float32).transpose(0, 2, 1))
    w2T = np.ascontiguousarray(np.asarray(inputs["L0_ffn_w2"], np.float32).transpose(0, 2, 1))
    w3T = np.ascontiguousarray(np.asarray(inputs["L0_ffn_w3"], np.float32).transpose(0, 2, 1))
    b1T = np.ascontiguousarray(np.asarray(inputs["L0_ffn_b1"], np.float32).T)
    b2T = np.ascontiguousarray(np.asarray(inputs["L0_ffn_b2"], np.float32).T)
    b3T = np.ascontiguousarray(np.asarray(inputs["L0_ffn_b3"], np.float32).T)
    sw1T = np.ascontiguousarray(np.asarray(inputs["L0_ffn_sw1"], np.float32).T)
    sw2T = np.ascontiguousarray(np.asarray(inputs["L0_ffn_sw2"], np.float32).T)
    sw3T = np.ascontiguousarray(np.asarray(inputs["L0_ffn_sw3"], np.float32).T)
    shared = dict(wqkv=wqkv, woutT=woutT, bout=bout, gwT=gwT, w1T=w1T, w2T=w2T,
                  w3T=w3T, b1T=b1T, b2T=b2T, b3T=b3T, sw1T=sw1T, sw2T=sw2T,
                  sw3T=sw3T)
    in_maps = []
    for c in range(n_cores):
        m = dict(shared)
        m["x"] = np.ascontiguousarray(
            x[c * tokens_per_core:(c + 1) * tokens_per_core])
        in_maps.append(m)
    return in_maps


def kernel(**inputs):
    n_cores = 8
    tokens_per_core = 4096
    in_maps = make_inmaps(inputs, tokens_per_core, n_cores)
    nc = build_nc(tokens_per_core)
    res = run_bass_kernel_spmd(nc, in_maps, core_ids=list(range(n_cores)))
    out = np.concatenate([r["y"] for r in res.results], axis=0)
    return out.reshape(16, 2048, D).astype(np.float32)


# version 16
# speedup vs baseline: 1.1510x; 1.0449x over previous
import contextlib
import numpy as np

import concourse.bass as bass
import concourse.mybir as mybir
from concourse.bass import IndirectOffsetOnAxis
from concourse.bass_utils import run_bass_kernel_spmd

F32 = mybir.dt.float32
BF16 = mybir.dt.bfloat16
AX = mybir.AxisListType.X
AF = mybir.ActivationFunctionType
ALU = mybir.AluOpType

D = 1024
EXP = 8
II = 2048
SCALE = 1.0 / 32.0
NBUF = 3
CAP = 384
ENGS = ("tensor", "vector", "scalar", "sync", "gpsimd")


class K:

    def __init__(self):
        self.plan = {e: [] for e in ENGS}
        self.cnt = {}
        self.sems = {}

    def sem(self, name, stack, nc):
        self.sems[name] = stack.enter_context(nc.semaphore(name))
        self.cnt[name] = 0

    def w(self, eng, evt):
        if evt is None:
            return
        s, v = evt
        sem = self.sems[s]
        self.plan[eng].append(lambda E, sem=sem, v=v: E.wait_ge(sem, v))

    def wmax(self, eng, evts):
        by = {}
        for e in evts:
            if e is None:
                continue
            by[e[0]] = max(by.get(e[0], 0), e[1])
        for s, v in by.items():
            self.w(eng, (s, v))

    def e(self, eng, fn, inc=None, amt=1):
        if eng == "vector":
            self.w("vector", ("ev", self.cnt["ev"]))
            if inc is None:
                inc, amt = "ev", 1
        if inc is None:
            self.plan[eng].append(fn)
            return None
        self.cnt[inc] += amt
        v = self.cnt[inc]
        sem = self.sems[inc]
        self.plan[eng].append(lambda E, fn=fn, sem=sem, amt=amt: fn(E).then_inc(sem, amt))
        return (inc, v)


class Banks:

    def __init__(self, k, ap, nbanks, width):
        self.k, self.ap, self.n, self.w = k, ap, nbanks, width
        self.free = [[] for _ in range(nbanks)]
        self.rr = 0

    def acq(self, cols=None):
        i = self.rr % self.n
        self.rr += 1
        self.k.wmax("tensor", self.free[i])
        self.free[i] = []
        sl = self.ap[:, i * self.w:(i + 1) * self.w if cols is None else i * self.w + cols]
        return i, sl

    def sl(self, i, c0, c1):
        return self.ap[:, i * self.w + c0:i * self.w + c1]

    def rel(self, i, evt):
        self.free[i].append(evt)


def build_nc(tokens, debug=False, stages='ABCDEFGH', a_level=4, f_level=9, g_level=9):
    M = min(1024, tokens)
    NMT = tokens // M
    NT = M // 128
    NN = max(1, M // 512)
    NW = min(512, M)
    SM = 8 * M
    assert tokens >= 1024, "sparse MoE layout assumes M=1024 macro-tiles"

    nc = bass.Bass("TRN2", target_bir_lowering=False,
                   detect_race_conditions=False)
    k = K()

    x_d = nc.dram_tensor("x", [tokens, D], F32, kind="ExternalInput").ap()
    y_d = nc.dram_tensor("y", [tokens, D], F32, kind="ExternalOutput").ap()
    wqkv_d = nc.dram_tensor("wqkv", [D, 3 * D], F32, kind="ExternalInput").ap()
    woutT_d = nc.dram_tensor("woutT", [D, D], F32, kind="ExternalInput").ap()
    gwT_d = nc.dram_tensor("gwT", [D, EXP], F32, kind="ExternalInput").ap()
    w1T_d = nc.dram_tensor("w1T", [EXP, D, D], F32, kind="ExternalInput").ap()
    w2T_d = nc.dram_tensor("w2T", [EXP, D, D], F32, kind="ExternalInput").ap()
    w3T_d = nc.dram_tensor("w3T", [EXP, D, D], F32, kind="ExternalInput").ap()
    sw1T_d = nc.dram_tensor("sw1T", [D, II], F32, kind="ExternalInput").ap()
    sw2T_d = nc.dram_tensor("sw2T", [D, II], F32, kind="ExternalInput").ap()
    sw3T_d = nc.dram_tensor("sw3T", [II, D], F32, kind="ExternalInput").ap()
    b1T_d = nc.dram_tensor("b1T", [D, EXP], F32, kind="ExternalInput").ap()
    b2T_d = nc.dram_tensor("b2T", [D, EXP], F32, kind="ExternalInput").ap()
    b3T_d = nc.dram_tensor("b3T", [D, EXP], F32, kind="ExternalInput").ap()
    bout_d = nc.dram_tensor("bout", [D], F32, kind="ExternalInput").ap()
    xt_d = nc.dram_tensor("xt_scr", [tokens, D], BF16, kind="ExternalOutput").ap()
    yt_d = nc.dram_tensor("yt_scr", [2 * tokens, D], BF16, kind="ExternalOutput").ap()
    dbg = {}
    if debug:
        for nm, shp in [("dbg_xnf", [128, SM]), ("dbg_logits", [128, NT * 8]),
                        ("dbg_srow", [128, 8 * NT]), ("dbg_qf", [128, SM]),
                        ("dbg_aof", [128, SM]), ("dbg_acc", [128, SM]),
                        ("dbg_hsb", [128, SM]), ("dbg_zz", [128, 16 * M])]:
            dbg[nm] = nc.dram_tensor(nm, shp, F32, kind="ExternalOutput").ap()

    with contextlib.ExitStack() as st:
        def sb(name, shape, dt):
            return st.enter_context(nc.sbuf_tensor(name, shape, dt))

        def ps(name, shape, dt):
            return st.enter_context(nc.psum_tensor(name, shape, dt))

        scr = sb("scr", [128, 32 * M], BF16)
        qf = scr[:, 0:SM]
        kf = scr[:, SM:2 * SM]
        vtok = scr[:, 2 * SM:2 * SM + NT * 1024]
        aof = scr[:, 3 * SM:4 * SM]
        sxn_g = scr[:, 0:8 * CAP]
        h_g = scr[:, 8 * CAP:16 * CAP]
        p_mx = scr[:, 16 * CAP:16 * CAP + NT * CAP]
        xng_tok = scr[:, 24 * CAP:32 * CAP]
        sxng_tok = scr[:, 32 * CAP:40 * CAP]
        prodtok = scr[:, 40 * CAP:48 * CAP]
        prodtile = [scr[:, 48 * CAP:49 * CAP],
                    scr[:, 57 * CAP:58 * CAP]]
        zero_zone = scr[:, 0:4096]
        h_sb = scr[:, SM:2 * SM]
        zz = scr[:, 2 * SM:4 * SM]
        xn_f = sb("xn_f", [128, SM], BF16)
        acc = sb("acc", [128, SM], F32)
        wsl = [sb(f"wsl{i}", [128, 8 * 512], BF16) for i in range(NBUF)]
        xa = [sb(f"xa{i}", [128, D], F32) for i in range(2)]
        xfin = [sb(f"xfin{i}", [128, D], F32) for i in range(2)]
        otok = [sb(f"otok{i}", [128, D], F32) for i in range(2)]
        xn32 = sb("xn32", [128, D], F32)
        xntok = [sb(f"xntok{i}", [128, D], F32) for i in range(2)]
        s_tokall = sb("s_tokall", [128, 8 * NT], F32)
        slot_tokall = sb("slot_tokall", [128, 8 * NT], F32)
        m_tokall = sb("m_tokall", [128, 8 * NT], F32)
        r_tokall = sb("r_tokall", [128, 8 * NT], F32)
        m_em8 = sb("m_em8", [8, M], BF16)
        rk_em8 = sb("rk_em8", [8, M], F32)
        iotaC = sb("iotaC", [128, CAP], F32)
        idf8 = sb("idf8", [128, 8], mybir.dt.int32)
        idlo_i = sb("idlo_i", [128, 8], mybir.dt.int32)
        c31 = sb("c31", [128, 1], mybir.dt.int32)
        idhi = sb("idhi", [128, 8], BF16)
        idlo = sb("idlo", [128, 8], BF16)
        meta = sb("meta", [128, 32], BF16)
        meta32 = sb("meta32", [4, CAP], F32)
        idx_all = sb("idx_all", [128, 8 * (CAP // 128)], mybir.dt.int32)
        ixs_all = sb("ixs_all", [128, 8 * (CAP // 128)], mybir.dt.int32)
        sg_all = sb("sg_all", [128, 8 * (CAP // 128)], F32)
        ytk_w = [sb(f"ytkw{i}", [128, 2 * D], BF16) for i in range(2)]
        idxf = sb("idxf", [128, 4], F32)
        ids1f = sb("ids1f", [128, 4], F32)
        eqf = sb("eqf", [128, 4], F32)
        mtt = sb("mtt", [128, 16], F32)
        logits = sb("logits", [128, NT * 8], F32)
        sm32 = sb("sm32", [128, 256], F32)
        p32 = sb("p32", [128, 256], F32)
        pn_sb = sb("pn_sb", [128, 256], BF16)
        pt_sb = sb("pt_sb", [128, 256], BF16)
        tmp3 = [sb(f"tmp3{i}", [128, 512], BF16) for i in range(2)]
        tmp2 = [sb(f"tmp2{i}", [128, 512], BF16) for i in range(2)]
        z1t = [sb(f"z1t{i}", [128, 512], BF16) for i in range(2)]
        gw_sb = sb("gw_sb", [128, 8 * EXP], F32)
        b1_sb = sb("b1_sb", [128, 8 * EXP], F32)
        b2_sb = sb("b2_sb", [128, 8 * EXP], F32)
        b3_sb = sb("b3_sb", [128, 8 * EXP], F32)
        bout_sb = sb("bout_sb", [128, 8], F32)
        identf = sb("identf", [128, 128], F32)
        identb = sb("identb", [128, 128], BF16)
        amask = sb("amask", [128, 128], F32)

        zerocol = sb("zerocol", [128, 1], F32)
        colp = sb("colp", [128, 24], F32)
        (CSX, CMEAN, CSXX, CEX2, CM2, CVAR, CSRT, CY0, CT1, CT2, CT3, CRSTD,
         CNRM, CRSUM, CRDEN, C32, CHALF, CT4, CNRM2, CRSUM2, CRDEN2) = range(21)

        def cp(i):
            return colp[:, i:i + 1]

        pmm = ps("pmm", [128, 4 * 512], F32)
        ptr = ps("ptr", [128, 512], F32)
        pbt = ps("pbt", [128, 2048], BF16)
        psml = ps("psml", [128, 512], F32)

        for s in ("wd0", "wd1", "wd2", "ws", "xa0", "xa1", "xf0",
                  "xf1", "od0", "od1", "dbg", "mm", "ev", "sc", "gp",
                  "xt0", "xt1", "gd", "sd", "zd", "yd0", "yd1"):
            k.sem(s, st, nc)

        mmB = Banks(k, pmm, 4, 512)
        trB = Banks(k, ptr, 1, 512)
        btB = Banks(k, pbt, 2, 1024)

        e1 = k.e("gpsimd", lambda E: E.memset(identf[:], 1.0), inc="gp")
        k.w("gpsimd", e1)
        e_idf = k.e("gpsimd", lambda E: E.affine_select(
            identf[:], identf[:], [[-1, 128]], ALU.is_equal, 0.0,
            base=0, channel_multiplier=1), inc="gp")
        e2 = k.e("gpsimd", lambda E: E.memset(amask[:], 0.0), inc="gp")
        k.w("gpsimd", e2)
        e_msk = k.e("gpsimd", lambda E: E.affine_select(
            amask[:], amask[:], [[-1, 128]], ALU.is_ge, -1.0e30,
            base=0, channel_multiplier=1), inc="gp")
        e_ioc = k.e("gpsimd", lambda E: E.iota(
            iotaC[:], [[1, CAP]], base=0, channel_multiplier=0,
            allow_small_or_imprecise_dtypes=True), inc="gp")
        e_idf8 = k.e("gpsimd", lambda E: E.iota(
            idf8[:], [[128, 8]], base=1, channel_multiplier=1), inc="gp")
        k.w("vector", e_idf)
        e_idb = k.e("vector", lambda E: E.tensor_copy(identb[:], identf[:]), inc="ev")
        k.e("vector", lambda E: E.memset(zerocol[:], 0.0))
        k.e("vector", lambda E: E.memset(cp(C32), 32.0))
        k.e("vector", lambda E: E.memset(cp(CHALF), 0.5))
        k.w("vector", e_idf8)
        k.e("vector", lambda E: E.memset(c31[:], 31))
        k.e("vector", lambda E: E.tensor_single_scalar(
            idlo_i[:], idf8[:], c31[:, 0:1], ALU.bitwise_and))
        k.e("vector", lambda E: E.tensor_copy(m_tokall[:, 0:8], idf8[:]))
        k.e("vector", lambda E: E.tensor_copy(m_tokall[:, 8:16], idlo_i[:]))
        k.e("vector", lambda E: E.tensor_copy(idlo[:], idlo_i[:]))
        k.e("vector", lambda E: E.tensor_sub(
            m_tokall[:, 0:8], m_tokall[:, 0:8], m_tokall[:, 8:16]))
        k.e("vector", lambda E: E.tensor_scalar_mul(idhi[:], m_tokall[:, 0:8], 1.0 / 32.0))
        k.e("vector", lambda E: E.tensor_copy(
            meta[:].rearrange("p (t k) -> p t k", k=4)[:, 0:8, 0:1],
            idhi[:, 0:8].rearrange("p (t u) -> p t u", u=1)))
        k.e("vector", lambda E: E.tensor_copy(
            meta[:].rearrange("p (t k) -> p t k", k=4)[:, 0:8, 1:2],
            idlo[:, 0:8].rearrange("p (t u) -> p t u", u=1)))


        def sdma(dst, src):
            return k.e("gpsimd", lambda E, d=dst, s=src: E.dma_start(
                out=d, in_=s, allow_slow_non_contiguous=True), inc="ws", amt=16)

        e_gw = sdma(gw_sb[:].rearrange("p (c e) -> p c e", c=8),
                    gwT_d.rearrange("(c p) e -> p c e", p=128))
        e_b1 = sdma(b1_sb[:].rearrange("p (t e) -> p t e", t=8),
                    b1T_d.rearrange("(t p) e -> p t e", p=128))
        e_b2 = sdma(b2_sb[:].rearrange("p (t e) -> p t e", t=8),
                    b2T_d.rearrange("(t p) e -> p t e", p=128))
        e_b3 = sdma(b3_sb[:].rearrange("p (t e) -> p t e", t=8),
                    b3T_d.rearrange("(t p) e -> p t e", p=128))
        e_bo = sdma(bout_sb[:], bout_d.rearrange("(t p) -> p t", p=128))
        e_gw = e_b1 = e_b2 = e_b3 = e_bo = ("ws", k.cnt["ws"])

        e_zz0 = k.e("vector", lambda E: E.memset(zero_zone, 0.0))
        k.w("sync", e_zz0)
        for kk in range(2 * tokens // 512):
            k.e("sync", lambda E, kk=kk: E.dma_start(
                out=yt_d.rearrange("(k g p) n -> k p g n", k=2 * tokens // 512, p=128)[kk],
                in_=zero_zone.rearrange("p (g n) -> p g n", g=4)), inc="zd", amt=16)
        for kk in range(tokens // 512):
            k.e("sync", lambda E, kk=kk: E.dma_start(
                out=xt_d.rearrange("(k g p) n -> k p g n", k=tokens // 512, p=128)[kk],
                in_=zero_zone.rearrange("p (g n) -> p g n", g=4)), inc="zd", amt=16)
        e_ztot = ("zd", k.cnt["zd"])

        bc_reg = nc.alloc_register(mybir.EngineType.Pool, "bc_reg")
        k.e("gpsimd", lambda E: nc.reg_mov(bc_reg, tokens - 1))
        bc2_reg = nc.alloc_register(mybir.EngineType.Pool, "bc2_reg")
        k.e("gpsimd", lambda E: nc.reg_mov(bc2_reg, 2 * tokens - 1))

        class Slabs:
            def __init__(self):
                self.kk = 0
                self.free_evt = [None] * NBUF

            def load(self, src3d):
                buf = self.kk % NBUF
                k.w("gpsimd", self.free_evt[buf])
                evt = k.e("gpsimd", lambda E, a=src3d, b=buf: E.dma_start(
                    out=wsl[b][:].rearrange("p (c n) -> p c n", c=8), in_=a),
                    inc=f"wd{buf}", amt=16)
                self.kk += 1
                return buf, evt

        slabs = Slabs()

        def w2d(ap2d, c0):
            return ap2d[:, c0:c0 + 512].rearrange("(c p) n -> p c n", p=128)

        def w3d(ap3d, e, c0):
            return ap3d[e, :, c0:c0 + 512].rearrange("(c p) n -> p c n", p=128)

        def wrow(ap2d, r0, c0):
            return ap2d[r0:r0 + 1024, c0:c0 + 512].rearrange("(c p) n -> p c n", p=128)

        st_ev = {
            "G_mm_end": None,
            "H_mm_end": None,
            "psml_free": None,
            "attn_mm_end": None,
            "D_mm_end": None,
            "xa_free": [None, None],
            "xfin_free": [None, None],
            "otok_store": [None, None],
            "xntok_free": [None, None],
            "sxn_rd_end": None,
            "sxn_rd_end2": [None, None],
            "tmp_free": [None, None],
            "ptile_free": [None, None],
            "z1t_free": [None, None],
            "gate_prev": None,
            "xt_evts": [],
            "meta_mm_prev": None,
            "routing_done": None,
            "gd_prev": None,
            "tbl_evt": {},
            "xng_free": None,
            "sxng_rd_end": None,
            "sd_prev": None,
        }

        n_store = [0]

        def stage_A(mt):
            xnf_done = None
            x32evts = []
            xt_evts = st_ev["xt_evts"]
            xt_evts.clear()
            for tt in range(NT):
                row0 = mt * M + tt * 128
                slot = tt % 2
                k.w("sync", st_ev["xa_free"][slot])
                ld = k.e("sync", lambda E, r=row0, s=slot: E.dma_start(
                    out=xa[s][:], in_=x_d[r:r + 128, :]), inc=f"xa{slot}", amt=16)
                k.w("scalar", ld)
                if st_ev["gate_prev"] is not None:
                    k.w("scalar", st_ev["gate_prev"])
                e_sq = k.e("scalar", lambda E, s=slot: E.activation(
                    xn32[:], xa[s][:], AF.Square, accum_out=cp(CSXX)), inc="sc")
                k.w("vector", ld)
                k.e("vector", lambda E, s=slot: E.reduce_sum(cp(CSX), xa[s][:], axis=AX))
                k.e("vector", lambda E: E.tensor_scalar_mul(cp(CMEAN), cp(CSX), 1.0 / D))
                k.w("vector", e_sq)
                k.e("vector", lambda E: E.tensor_scalar_mul(cp(CEX2), cp(CSXX), 1.0 / D))
                k.e("vector", lambda E: E.tensor_mul(cp(CM2), cp(CMEAN), cp(CMEAN)))
                k.e("vector", lambda E: E.tensor_sub(cp(CVAR), cp(CEX2), cp(CM2)))
                e_var = k.e("vector", lambda E: E.tensor_scalar_add(
                    cp(CT1), cp(CVAR), 1e-5), inc="ev")
                k.w("scalar", e_var)
                e_srt = k.e("scalar", lambda E: E.activation(
                    cp(CSRT), cp(CT1), AF.Sqrt), inc="sc")
                k.w("vector", e_srt)
                e_rc = k.e("vector", lambda E: E.reciprocal(cp(CY0), cp(CSRT)),
                           inc="ev")
                k.w("vector", e_rc)
                k.e("vector", lambda E: E.tensor_mul(cp(CT2), cp(CY0), cp(CY0)))
                k.e("vector", lambda E: E.tensor_mul(cp(CT2), cp(CT2), cp(CT1)))
                k.e("vector", lambda E: E.tensor_scalar(cp(CT3), cp(CT2), -0.5, 1.5, ALU.mult, ALU.add))
                k.e("vector", lambda E: E.tensor_mul(cp(CRSTD), cp(CY0), cp(CT3)))
                k.wmax("vector", st_ev["xntok_free"][slot] or [])
                e_xn = k.e("vector", lambda E, s=slot: E.tensor_scalar(
                    xntok[s][:], xa[s][:], cp(CMEAN), cp(CRSTD),
                    ALU.subtract, ALU.mult), inc="ev")
                st_ev["xa_free"][slot] = e_xn
                k.w("gpsimd", e_xn)
                if mt == 0 and tt == 0:
                    k.w("gpsimd", e_ztot)
                e_xt = k.e("gpsimd", lambda E, r=row0, s=slot: E.dma_start(
                    out=xt_d[r:r + 128, :], in_=xntok[s][:]),
                    inc=f"xt{slot}", amt=16)
                xt_evts.append(e_xt)
                if a_level < 2:
                    continue
                k.w("tensor", e_xn)
                if mt == 0 and tt == 0:
                    k.w("tensor", e_idf)
                ev_l = None
                for half in range(2):
                    bi, _ = trB.acq()
                    for cc in range(4):
                        c = half * 4 + cc
                        fn = (lambda E, s=slot, c=c, bi=bi, cc=cc: E.transpose(
                            trB.sl(bi, cc * 128, cc * 128 + 128),
                            xntok[s][:, c * 128:(c + 1) * 128], identf[:]))
                        ev = k.e("tensor", fn, inc="mm" if cc == 3 else None)
                    ev_l = ev
                    if a_level < 3:
                        trB.rel(bi, ev)
                        continue
                    k.w("vector", ev)
                    if half == 0 and st_ev["gate_prev"] is not None:
                        k.w("vector", st_ev["gate_prev"])
                        k.w("vector", xnf_done)
                    e_v = k.e("vector", lambda E, bi=bi, h=half: E.tensor_copy(
                        xn32[:, h * 512:(h + 1) * 512], trB.sl(bi, 0, 512)), inc="ev")
                    k.w("scalar", e_v)
                    if tt == 0 and half == 0 and st_ev["G_mm_end"] is not None:
                        k.w("scalar", st_ev["G_mm_end"])
                    e_s = k.e("scalar", lambda E, h=half, tt=tt: E.activation(
                        xn_f[:, 0:SM].rearrange("p (c t) -> p c t", c=8)
                        [:, h * 4:(h + 1) * 4, tt * 128:(tt + 1) * 128],
                        xn32[:, h * 512:(h + 1) * 512].rearrange("p (c t) -> p c t", c=4),
                        AF.Copy), inc="sc")
                    trB.rel(bi, e_v)
                    x32evts.append(e_v)
                    xnf_done = e_s
                st_ev["xntok_free"][slot] = [ev_l, xt_evts[-1]]
                if a_level < 3:
                    continue
                if a_level < 4:
                    continue
                k.wmax("tensor", x32evts[-2:])
                if mt == 0 and tt == 0:
                    k.w("tensor", e_gw)
                if st_ev["psml_free"] is not None:
                    k.w("tensor", st_ev["psml_free"])
                for c in range(8):
                    ev = k.e("tensor", lambda E, c=c: E.matmul(
                        psml[:, 0:8], xn32[:, c * 128:(c + 1) * 128],
                        gw_sb[:, c * 8:(c + 1) * 8],
                        start=(c == 0), stop=(c == 7)),
                        inc="mm" if c == 7 else None)
                k.w("vector", ev)
                e_lg = k.e("vector", lambda E, tt=tt: E.tensor_copy(
                    logits[:, tt * 8:(tt + 1) * 8], psml[:, 0:8]), inc="ev")
                st_ev["psml_free"] = e_lg
                st_ev["gate_prev"] = ev
            return xnf_done

        def preload_B(mt):
            return [slabs.load(w2d(wqkv_d, j * 512)) for j in range(3)]

        def stage_B(mt, xnf_done, pre):
            qk_done = None
            v_done = None
            for j in range(6):
                buf, rdy = pre[j] if j < len(pre) else slabs.load(w2d(wqkv_d, j * 512))
                k.w("tensor", rdy)
                if j == 0:
                    k.w("tensor", xnf_done)
                last_mm = None
                if j < 4:
                    dest = qf if j < 2 else kf
                    base_oc = (j % 2) * 4
                    for o in range(4):
                        for n in range(NN):
                            bi, bsl = mmB.acq(cols=NW)
                            for c in range(8):
                                ev = k.e("tensor", lambda E, b=buf, c=c, o=o, n=n, bsl=bsl: E.matmul(
                                    bsl, wsl[b][:, c * 512 + o * 128:c * 512 + (o + 1) * 128],
                                    xn_f[:, c * M + n * 512:c * M + n * 512 + NW],
                                    start=(c == 0), stop=(c == 7)),
                                    inc="mm" if c == 7 else None)
                            last_mm = ev
                            k.w("vector", ev)
                            if mt == 0 and j == 0 and o == 0 and n == 0:
                                k.w("vector", e_ztot)
                            oc = base_oc + o
                            e_v = k.e("vector", lambda E, bi=bi, oc=oc, n=n, d=dest: E.tensor_copy(
                                d[:, oc * M + n * 512:oc * M + n * 512 + NW],
                                mmB.sl(bi, 0, NW)), inc="ev")
                            mmB.rel(bi, e_v)
                            qk_done = e_v
                else:
                    for tt in range(NT):
                        bi, bsl = mmB.acq()
                        for c in range(8):
                            ev = k.e("tensor", lambda E, b=buf, c=c, tt=tt, bsl=bsl: E.matmul(
                                bsl, xn_f[:, c * M + tt * 128:c * M + (tt + 1) * 128],
                                wsl[b][:, c * 512:(c + 1) * 512],
                                start=(c == 0), stop=(c == 7)),
                                inc="mm" if c == 7 else None)
                        last_mm = ev
                        k.w("vector", ev)
                        if j == 4 and tt == 0 and st_ev["sd_prev"] is not None:
                            k.w("vector", st_ev["sd_prev"])
                        e_v = k.e("vector", lambda E, bi=bi, tt=tt, j=j: E.tensor_copy(
                            vtok[:, tt * 1024 + (j - 4) * 512:tt * 1024 + (j - 4) * 512 + 512],
                            mmB.sl(bi, 0, 512)), inc="ev")
                        mmB.rel(bi, e_v)
                        v_done = e_v
                slabs.free_evt[buf] = last_mm
            return qk_done, v_done

        def stage_C(mt, qk_done, v_done):
            aof_done = None
            attn_end = None
            state = {}

            def scores(b):
                bi, _ = mmB.acq(cols=128)
                if b == 0:
                    k.w("tensor", qk_done)
                for c in range(8):
                    ev = k.e("tensor", lambda E, c=c, b=b, bi=bi: E.matmul(
                        mmB.sl(bi, 0, 128),
                        qf[:, c * M + b * 128:c * M + (b + 1) * 128],
                        kf[:, c * M + b * 128:c * M + (b + 1) * 128],
                        start=(c == 0), stop=(c == 7)),
                        inc="mm" if c == 7 else None)
                state[b] = (bi, ev)
                return ev

            def softmax(b):
                par = b % 2
                bi, ev = state[b]
                NRM = CNRM if par == 0 else CNRM2
                RSUM = CRSUM if par == 0 else CRSUM2
                RDEN = CRDEN if par == 0 else CRDEN2
                smv = sm32[:, par * 128:(par + 1) * 128]
                pv = p32[:, par * 128:(par + 1) * 128]
                pnv = pn_sb[:, par * 128:(par + 1) * 128]
                k.w("vector", ev)
                if b == 0 and mt == 0:
                    k.w("vector", (("gp", k.cnt["gp"])))
                k.e("vector", lambda E, bi=bi, smv=smv: E.scalar_tensor_tensor(
                    smv, mmB.sl(bi, 0, 128), SCALE, amask[:],
                    ALU.mult, ALU.add))
                e_nm = k.e("vector", lambda E, smv=smv, NRM=NRM: E.reduce_max(
                    cp(NRM), smv, axis=AX, negate=True))
                mmB.rel(bi, e_nm)
                k.w("scalar", e_nm)
                e_p = k.e("scalar", lambda E, smv=smv, pv=pv, NRM=NRM, RSUM=RSUM: E.activation(
                    pv, smv, AF.Exp, bias=cp(NRM),
                    accum_out=cp(RSUM)), inc="sc")
                k.w("vector", e_p)
                k.e("vector", lambda E, RDEN=RDEN, RSUM=RSUM: E.tensor_scalar_add(
                    cp(RDEN), cp(RSUM), 1e-6))
                e_rc = k.e("vector", lambda E, RDEN=RDEN: E.reciprocal(
                    cp(RDEN), cp(RDEN)))
                k.w("vector", e_rc)
                e_pn = k.e("vector", lambda E, pv=pv, pnv=pnv, RDEN=RDEN: E.tensor_scalar_mul(
                    pnv, pv, cp(RDEN)))
                return e_pn

            def pv_block(b, e_pn):
                nonlocal aof_done, attn_end
                sloti = b % 2
                pnv = pn_sb[:, sloti * 128:(sloti + 1) * 128]
                k.w("tensor", e_pn)
                if mt == 0 and b < 2:
                    k.w("tensor", e_idb)
                k.wmax("tensor", btB.free[sloti])
                btB.free[sloti] = []
                e_tp = k.e("tensor", lambda E, s=sloti, pnv=pnv: E.transpose(
                    pbt[:, s * 1024:s * 1024 + 128], pnv, identb[:]), inc="mm")
                k.w("vector", e_tp)
                e_pt = k.e("vector", lambda E, s=sloti: E.tensor_copy(
                    pt_sb[:, s * 128:(s + 1) * 128], pbt[:, s * 1024:s * 1024 + 128]))
                btB.free[sloti].append(e_pt)
                k.w("tensor", e_pt)
                if b == 0:
                    k.w("tensor", v_done)
                for half in range(2):
                    bi2, _ = mmB.acq()
                    for dt in range(4):
                        ev = k.e("tensor", lambda E, bi2=bi2, dt=dt, half=half, b=b, s=sloti: E.matmul(
                            mmB.sl(bi2, dt * 128, (dt + 1) * 128),
                            vtok[:, b * 1024 + (half * 4 + dt) * 128:b * 1024 + (half * 4 + dt + 1) * 128],
                            pt_sb[:, s * 128:(s + 1) * 128],
                            start=True, stop=True),
                            inc="mm" if dt == 3 else None)
                    attn_end = ev
                    k.w("scalar", ev)
                    e_a = k.e("scalar", lambda E, bi2=bi2, half=half, b=b: E.activation(
                        aof[:, 0:SM].rearrange("p (c t) -> p c t", c=8)
                        [:, half * 4:(half + 1) * 4, b * 128:(b + 1) * 128],
                        mmB.sl(bi2, 0, 512).rearrange("p (c t) -> p c t", c=4),
                        AF.Copy), inc="sc")
                    mmB.rel(bi2, e_a)
                    aof_done = e_a

            scores(0)
            pend = softmax(0)
            for b in range(1, NT):
                scores(b)
                pv_block(b - 1, pend)
                pend = softmax(b)
            pv_block(NT - 1, pend)
            st_ev["attn_mm_end"] = attn_end
            return aof_done

        def table_build(mt, e):
            NG = CAP // 128
            if e == 0:
                k.w("vector", st_ev["routing_done"])
                k.w("vector", st_ev["attn_mm_end"])
                if st_ev["sd_prev"] is not None:
                    k.w("vector", st_ev["sd_prev"])
                if st_ev["gd_prev"] is not None:
                    k.w("vector", st_ev["gd_prev"])
            else:
                k.w("vector", st_ev["meta_mm_prev"])
            for tc in range(NT):
                e_pm = k.e("vector", lambda E, tc=tc, e=e: E.tensor_scalar(
                    p_mx[:, tc * CAP:(tc + 1) * CAP], iotaC[:, 0:CAP],
                    r_tokall[:, tc * 8 + e:tc * 8 + e + 1],
                    m_tokall[:, tc * 8 + e:tc * 8 + e + 1],
                    ALU.is_equal, ALU.mult))
            k.e("vector", lambda E, e=e: E.tensor_copy(
                meta[:].rearrange("p (t k) -> p t k", k=4)[:, 0:NT, 2:3],
                s_tokall[:].rearrange("p (t q) -> p t q", q=8)[:, 0:NT, e:e + 1]))
            e_meta = k.e("vector", lambda E, e=e: E.tensor_copy(
                meta[:].rearrange("p (t k) -> p t k", k=4)[:, 0:NT, 3:4],
                slot_tokall[:].rearrange("p (t q) -> p t q", q=8)[:, 0:NT, e:e + 1]))
            k.w("tensor", e_pm)
            k.w("tensor", e_meta)
            if st_ev["psml_free"] is not None:
                k.w("tensor", st_ev["psml_free"])
            for tc in range(NT):
                mev = k.e("tensor", lambda E, tc=tc: E.matmul(
                    psml[0:4, 0:CAP], meta[:, tc * 4:(tc + 1) * 4],
                    p_mx[:, tc * CAP:(tc + 1) * CAP],
                    start=(tc == 0), stop=(tc == NT - 1)),
                    inc="mm" if tc == NT - 1 else None)
            st_ev["meta_mm_prev"] = mev
            k.w("vector", mev)
            e_m32 = k.e("vector", lambda E: E.tensor_copy(
                meta32[0:4, 0:CAP], psml[0:4, 0:CAP]))
            k.w("tensor", e_m32)
            for g in range(NG):
                tev = k.e("tensor", lambda E, g=g: E.transpose(
                    psml[0:128, g * 4:(g + 1) * 4],
                    meta32[0:4, g * 128:(g + 1) * 128],
                    identf[0:4, 0:4]), inc="mm" if g == NG - 1 else None)
            k.w("vector", tev)
            k.e("vector", lambda E: E.tensor_copy(
                mtt[:, 0:4 * NG], psml[:, 0:4 * NG]))
            k.e("vector", lambda E: E.scalar_tensor_tensor(
                ids1f[:, 0:NG], mtt[:, 0:4 * NG:4], 32.0,
                mtt[:, 1:4 * NG:4], ALU.mult, ALU.add))
            k.e("vector", lambda E: E.tensor_single_scalar(
                eqf[:, 0:NG], ids1f[:, 0:NG], cp(CHALF), ALU.is_lt))
            k.e("vector", lambda E, mt=mt: E.tensor_scalar_add(
                idxf[:, 0:NG], ids1f[:, 0:NG], float(mt * M - 1)))
            k.e("vector", lambda E: E.scalar_tensor_tensor(
                idxf[:, 0:NG], eqf[:, 0:NG], float(tokens + 1), idxf[:, 0:NG],
                ALU.mult, ALU.add))
            e_idx = k.e("vector", lambda E, e=e: E.tensor_copy(
                idx_all[:, e * NG:(e + 1) * NG], idxf[:, 0:NG]))
            k.e("vector", lambda E: E.scalar_tensor_tensor(
                idxf[:, 0:NG], ids1f[:, 0:NG], 2.0, mtt[:, 3:4 * NG:4],
                ALU.mult, ALU.add))
            k.e("vector", lambda E, mt=mt: E.tensor_scalar_add(
                idxf[:, 0:NG], idxf[:, 0:NG], float(2 * mt * M - 2)))
            k.e("vector", lambda E: E.scalar_tensor_tensor(
                idxf[:, 0:NG], eqf[:, 0:NG], float(2 * tokens + 4), idxf[:, 0:NG],
                ALU.mult, ALU.add))
            k.e("vector", lambda E, e=e: E.tensor_copy(
                ixs_all[:, e * NG:(e + 1) * NG], idxf[:, 0:NG]))
            e_sg = k.e("vector", lambda E, e=e: E.tensor_copy(
                sg_all[:, e * NG:(e + 1) * NG], mtt[:, 2:4 * NG:4]))
            st_ev["psml_free"] = e_sg
            st_ev["tbl_evt"][e] = e_sg

        def stage_D(mt, aof_done):
            D_end = None
            acc_evts = []
            it = [0]
            for j in range(2):
                buf, rdy = slabs.load(w2d(woutT_d, j * 512))
                k.w("tensor", rdy)
                if j == 0:
                    k.w("tensor", aof_done)
                last_mm = None
                for o in range(4):
                    for n in range(NN):
                        bi, bsl = mmB.acq(cols=NW)
                        for c in range(8):
                            ev = k.e("tensor", lambda E, b=buf, c=c, o=o, n=n, bsl=bsl: E.matmul(
                                bsl, wsl[b][:, c * 512 + o * 128:c * 512 + (o + 1) * 128],
                                aof[:, c * M + n * 512:c * M + n * 512 + NW],
                                start=(c == 0), stop=(c == 7)),
                                inc="mm" if c == 7 else None)
                        last_mm = ev
                        D_end = ev
                        oc = j * 4 + o
                        k.w("scalar", ev)
                        if j == 0 and o == 0 and n == 0:
                            if st_ev["H_mm_end"] is not None:
                                k.w("scalar", st_ev["H_mm_end"])
                            if mt == 0:
                                k.w("scalar", e_bo)
                        e_s = k.e("scalar", lambda E, bi=bi, oc=oc, n=n: E.activation(
                            acc[:, oc * M + n * 512:oc * M + n * 512 + NW],
                            mmB.sl(bi, 0, NW), AF.Identity,
                            bias=bout_sb[:, oc:oc + 1]), inc="sc")
                        mmB.rel(bi, e_s)
                        acc_evts.append(e_s)
                        if it[0] % 2 == 0 and it[0] // 2 < EXP:
                            table_build(mt, it[0] // 2)
                        it[0] += 1
                slabs.free_evt[buf] = last_mm
            st_ev["D_mm_end"] = D_end
            return acc_evts[-1]

        def stage_E(mt):
            srow_done = None
            for tt in range(NT):
                slot = tt % 2
                e_nm = k.e("vector", lambda E, tt=tt: E.reduce_max(
                    cp(CNRM), logits[:, tt * 8:(tt + 1) * 8], axis=AX, negate=True),
                    inc="ev")
                k.w("scalar", e_nm)
                e_p = k.e("scalar", lambda E, tt=tt: E.activation(
                    p32[:, 0:8],
                    logits[:, tt * 8:(tt + 1) * 8], AF.Exp, bias=cp(CNRM),
                    accum_out=cp(CRSUM)), inc="sc")
                k.w("vector", e_p)
                e_rc = k.e("vector", lambda E: E.reciprocal(cp(CRDEN), cp(CRSUM)),
                           inc="ev")
                k.w("vector", e_rc)
                k.e("vector", lambda E: E.tensor_scalar_mul(
                    sm32[:, 0:8], p32[:, 0:8], cp(CRDEN)))
                k.e("vector", lambda E: E.reduce_max(cp(CT1), sm32[:, 0:8], axis=AX))
                k.e("vector", lambda E: E.tensor_single_scalar(
                    sm32[:, 8:16], sm32[:, 0:8], cp(CT1), ALU.is_ge))
                k.e("vector", lambda E: E.scalar_tensor_tensor(
                    sm32[:, 16:24], sm32[:, 8:16], -1.0e30, sm32[:, 0:8],
                    ALU.mult, ALU.add))
                k.e("vector", lambda E: E.reduce_max(cp(CT2), sm32[:, 16:24], axis=AX))
                k.e("vector", lambda E: E.tensor_single_scalar(
                    sm32[:, 8:16], sm32[:, 0:8], cp(CT2), ALU.is_ge))
                e_s = k.e("vector", lambda E, tt=tt: E.tensor_mul(
                    s_tokall[:, tt * 8:tt * 8 + 8], sm32[:, 0:8], sm32[:, 8:16]))
                e_m = k.e("vector", lambda E, tt=tt: E.tensor_single_scalar(
                    m_tokall[:, tt * 8:tt * 8 + 8],
                    s_tokall[:, tt * 8:tt * 8 + 8],
                    zerocol[:, 0:1], ALU.is_gt))
                k.e("vector", lambda E, tt=tt: E.tensor_tensor_scan(
                    slot_tokall[:, tt * 8:tt * 8 + 8],
                    m_tokall[:, tt * 8:tt * 8 + 8],
                    m_tokall[:, tt * 8:tt * 8 + 8], 0.0, ALU.add, ALU.bypass))
                k.e("vector", lambda E, tt=tt: E.tensor_sub(
                    slot_tokall[:, tt * 8:tt * 8 + 8],
                    slot_tokall[:, tt * 8:tt * 8 + 8],
                    m_tokall[:, tt * 8:tt * 8 + 8]))
                k.w("tensor", e_m)
                if st_ev["psml_free"] is not None:
                    k.w("tensor", st_ev["psml_free"])
                ev = k.e("tensor", lambda E, tt=tt: E.transpose(
                    psml[0:8, 0:128], m_tokall[:, tt * 8:tt * 8 + 8],
                    identf[:]), inc="mm")
                k.w("vector", ev)
                e_r = k.e("vector", lambda E, tt=tt: E.tensor_copy(
                    m_em8[0:8, tt * 128:(tt + 1) * 128], psml[0:8, 0:128]))
                st_ev["psml_free"] = e_r
            k.e("vector", lambda E: E.tensor_tensor_scan(
                rk_em8[0:8, :], m_em8[0:8, :], m_em8[0:8, :], 0.0,
                ALU.add, ALU.bypass))
            e_r0 = k.e("vector", lambda E: E.tensor_sub(
                rk_em8[0:8, :], rk_em8[0:8, :], m_em8[0:8, :]))
            rt_done = None
            for tt in range(NT):
                k.w("tensor", e_r0)
                k.w("tensor", st_ev["psml_free"])
                ev = k.e("tensor", lambda E, tt=tt: E.transpose(
                    psml[0:128, 0:8], rk_em8[0:8, tt * 128:(tt + 1) * 128],
                    identf[0:8, 0:8]), inc="mm")
                k.w("vector", ev)
                rt_done = k.e("vector", lambda E, tt=tt: E.tensor_copy(
                    r_tokall[:, tt * 8:(tt + 1) * 8], psml[0:128, 0:8]))
                st_ev["psml_free"] = rt_done
            return rt_done

        def stage_F(mt, routing_done, xnf_done):
            NG = CAP // 128
            sd_after = {}
            sxn_gbuf = [sxn_g, scr[:, 49 * CAP:57 * CAP]]

            def prologue1(e):
                k.w("gpsimd", st_ev["tbl_evt"][e])
                k.wmax("gpsimd", st_ev["xt_evts"])
                if st_ev["xng_free"] is not None:
                    k.w("gpsimd", st_ev["xng_free"])
                if e == 0:
                    k.w("gpsimd", st_ev["attn_mm_end"])
                for g in range(NG):
                    k.e("gpsimd", lambda E, g=g, e=e: E.indirect_dma_start(
                        out=xng_tok[:, g * 1024:(g + 1) * 1024], out_offset=None,
                        in_=xt_d,
                        in_offset=IndirectOffsetOnAxis(
                            ap=idx_all[:, e * NG + g:e * NG + g + 1], axis=0),
                        bounds_check=bc_reg, oob_is_err=False), inc="gd", amt=16)
                return ("gd", k.cnt["gd"])

            def prologue2(e, gd_tot):
                slot = e % 2
                sxn = sxn_gbuf[slot]
                k.w("vector", gd_tot)
                if st_ev["sxng_rd_end"] is not None:
                    k.w("vector", st_ev["sxng_rd_end"])
                if e == 0:
                    k.w("vector", st_ev["G_mm_end"])
                e_sc = None
                for g in range(NG):
                    e_sc = k.e("vector", lambda E, g=g, e=e: E.tensor_scalar_mul(
                        sxng_tok[:, g * 1024:(g + 1) * 1024],
                        xng_tok[:, g * 1024:(g + 1) * 1024],
                        sg_all[:, e * NG + g:e * NG + g + 1]))
                st_ev["xng_free"] = e_sc
                res = {"last": None, "tev": None}

                def mkround(dc):
                    def em():
                        if dc == 0:
                            k.w("tensor", e_sc)
                            if e == 0 and mt == 0:
                                k.w("tensor", e_idb)
                        bi, _ = btB.acq()
                        for g in range(NG):
                            tev = k.e("tensor", lambda E, dc=dc, g=g, bi=bi: E.transpose(
                                pbt[:, bi * 1024 + g * 128:bi * 1024 + (g + 1) * 128],
                                sxng_tok[:, g * 1024 + dc * 128:g * 1024 + (dc + 1) * 128],
                                identb[:]), inc="mm" if g == NG - 1 else None)
                        res["tev"] = tev
                        k.w("scalar", tev)
                        if dc == 0:
                            if e == 0:
                                k.w("scalar", st_ev["attn_mm_end"])
                            k.w("scalar", st_ev["sxn_rd_end2"][slot])
                        res["last"] = k.e("scalar", lambda E, dc=dc, bi=bi, sxn=sxn: E.copy(
                            sxn[:, dc * CAP:(dc + 1) * CAP],
                            pbt[:, bi * 1024:bi * 1024 + CAP]), inc="sc")
                        btB.rel(bi, res["last"])
                    return em
                return [mkround(dc) for dc in range(8)], res

            prep = {}
            ptr_pend = []
            sxn_pend = []
            w1pre = {0: [slabs.load(w3d(w1T_d, 0, 0))]}
            gpre = []
            gd0 = prologue1(0)
            rounds0, res0 = prologue2(0, gd0)
            for f in rounds0:
                f()
            prep[0] = res0
            for e in range(EXP):
                slot = e % 2
                sxn = sxn_gbuf[slot]
                res_e = prep.pop(e)
                last_sxn_wr = res_e["last"]
                h_done = None
                pre1 = w1pre.pop(e, [])
                for j in range(2):
                    buf, rdy = pre1[j] if j < len(pre1) else slabs.load(w3d(w1T_d, e, j * 512))
                    k.w("tensor", rdy)
                    if j == 0:
                        k.w("tensor", last_sxn_wr)
                    for o in range(4):
                        oc = j * 4 + o
                        bi, bsl = mmB.acq(cols=CAP)
                        for c in range(8):
                            ev = k.e("tensor", lambda E, b=buf, c=c, o=o, bsl=bsl, sxn=sxn: E.matmul(
                                bsl, wsl[b][:, c * 512 + o * 128:c * 512 + (o + 1) * 128],
                                sxn[:, c * CAP:c * CAP + CAP],
                                start=(c == 0), stop=(c == 7)),
                                inc="mm" if c == 7 else None)
                        k.w("scalar", ev)
                        if mt == 0 and e == 0 and oc == 0:
                            k.w("scalar", e_b1)
                            k.w("scalar", e_b2)
                            k.w("scalar", e_b3)
                            k.w("scalar", st_ev["attn_mm_end"])
                        e_h = k.e("scalar", lambda E, bi=bi, oc=oc, e=e: E.activation(
                            h_g[:, oc * CAP:oc * CAP + CAP],
                            mmB.sl(bi, 0, CAP), AF.Silu,
                            bias=b1_sb[:, oc * 8 + e:oc * 8 + e + 1]), inc="sc")
                        mmB.rel(bi, e_h)
                        h_done = e_h
                    slabs.free_evt[buf] = ev
                w3bufs = {}
                w2bufs = {}
                prod_done = None
                gd_next = None
                for oc in range(8):
                    j = oc // 4
                    if j not in w3bufs:
                        w3bufs[j] = slabs.load(w3d(w3T_d, e, j * 512))
                        w2bufs[j] = slabs.load(w3d(w2T_d, e, j * 512))
                    b3buf, rdy3 = w3bufs[j]
                    b2buf, rdy2 = w2bufs[j]
                    bi3, bsl3 = mmB.acq(cols=CAP)
                    if oc % 4 == 0:
                        k.w("tensor", rdy3)
                    for c in range(8):
                        ev3 = k.e("tensor", lambda E, b=b3buf, c=c, oc=oc, bsl=bsl3, sxn=sxn: E.matmul(
                            bsl, wsl[b][:, c * 512 + (oc % 4) * 128:c * 512 + (oc % 4 + 1) * 128],
                            sxn[:, c * CAP:c * CAP + CAP],
                            start=(c == 0), stop=(c == 7)),
                            inc="mm" if c == 7 else None)
                    bi2, bsl2 = mmB.acq(cols=CAP)
                    if oc % 4 == 0:
                        k.w("tensor", rdy2)
                    if oc == 0:
                        k.w("tensor", h_done)
                    for c in range(8):
                        ev2 = k.e("tensor", lambda E, b=b2buf, c=c, oc=oc, bsl=bsl2: E.matmul(
                            bsl, wsl[b][:, c * 512 + (oc % 4) * 128:c * 512 + (oc % 4 + 1) * 128],
                            h_g[:, c * CAP:c * CAP + CAP],
                            start=(c == 0), stop=(c == 7)),
                            inc="mm" if c == 7 else None)
                    i2 = oc % 2
                    k.w("scalar", ev3)
                    if st_ev["tmp_free"][i2] is not None:
                        k.w("scalar", st_ev["tmp_free"][i2])
                    e_x3 = k.e("scalar", lambda E, bi3=bi3, oc=oc, e=e, i2=i2: E.activation(
                        tmp3[i2][:, 0:CAP], mmB.sl(bi3, 0, CAP), AF.Identity,
                        bias=b3_sb[:, oc * 8 + e:oc * 8 + e + 1]), inc="sc")
                    mmB.rel(bi3, e_x3)
                    k.w("scalar", ev2)
                    e_x2 = k.e("scalar", lambda E, bi2=bi2, oc=oc, e=e, i2=i2: E.activation(
                        tmp2[i2][:, 0:CAP], mmB.sl(bi2, 0, CAP), AF.Identity,
                        bias=b2_sb[:, oc * 8 + e:oc * 8 + e + 1]), inc="sc")
                    mmB.rel(bi2, e_x2)
                    k.w("vector", e_x3)
                    k.w("vector", e_x2)
                    if oc == 0 and e == 0:
                        k.w("vector", st_ev["D_mm_end"])
                    if st_ev["ptile_free"][i2] is not None:
                        k.w("vector", st_ev["ptile_free"][i2])
                    e_pr = k.e("vector", lambda E, i2=i2: E.tensor_mul(
                        prodtile[i2][:, 0:CAP], tmp3[i2][:, 0:CAP], tmp2[i2][:, 0:CAP]))
                    st_ev["tmp_free"][i2] = e_pr

                    def emit_ptr(oc=oc, i2=i2, e_pr=e_pr, e=e):
                        k.w("tensor", e_pr)
                        bi, _ = btB.acq()
                        for g in range(NG):
                            tev = k.e("tensor", lambda E, g=g, bi=bi, i2=i2: E.transpose(
                                pbt[:, bi * 1024 + g * 128:bi * 1024 + (g + 1) * 128],
                                prodtile[i2][:, g * 128:(g + 1) * 128],
                                identb[:]), inc="mm" if g == NG - 1 else None)
                        k.w("scalar", tev)
                        if oc == 0:
                            if e == 0:
                                if st_ev["sd_prev"] is not None:
                                    k.w("scalar", st_ev["sd_prev"])
                            elif sd_after.get(e - 1) is not None:
                                k.w("scalar", sd_after[e - 1])
                        pd = k.e("scalar", lambda E, oc=oc, bi=bi: E.copy(
                            prodtok[:].rearrange("p (g n) -> p g n", g=NG)
                            [:, :, oc * 128:(oc + 1) * 128],
                            pbt[:, bi * 1024:bi * 1024 + CAP]
                            .rearrange("p (g n) -> p g n", g=NG)), inc="sc")
                        btB.rel(bi, pd)
                        st_ev["ptile_free"][i2] = pd
                        return pd

                    ptr_pend.append(emit_ptr)
                    if len(ptr_pend) > 1:
                        prod_done = ptr_pend.pop(0)()
                    if oc % 4 == 3:
                        slabs.free_evt[b3buf] = ev3
                        slabs.free_evt[b2buf] = ev2
                    if e + 1 < EXP:
                        if oc == 0:
                            gd_next = prologue1(e + 1)
                        elif oc == 1:
                            rounds, res_n = prologue2(e + 1, gd_next)
                            sxn_pend.extend(rounds)
                            prep[e + 1] = res_n
                        elif oc == 3:
                            w1pre[e + 1] = [slabs.load(w3d(w1T_d, e + 1, 0))]
                    elif oc == 5:
                        gpre.append(slabs.load(w2d(sw1T_d, 0)))
                    if sxn_pend:
                        sxn_pend.pop(0)()
                        if oc >= 5 and sxn_pend:
                            sxn_pend.pop(0)()
                while ptr_pend:
                    prod_done = ptr_pend.pop(0)()
                while sxn_pend:
                    sxn_pend.pop(0)()
                if e + 1 < EXP:
                    st_ev["sxng_rd_end"] = prep[e + 1]["tev"]
                st_ev["sxn_rd_end"] = ev3
                st_ev["sxn_rd_end2"][slot] = ev3
                k.w("gpsimd", prod_done)
                if mt == 0 and e == 0:
                    k.w("gpsimd", e_ztot)
                for g in range(NG):
                    k.e("gpsimd", lambda E, g=g, e=e: E.indirect_dma_start(
                        out=yt_d,
                        out_offset=IndirectOffsetOnAxis(
                            ap=ixs_all[:, e * NG + g:e * NG + g + 1], axis=0),
                        in_=prodtok[:, g * 1024:(g + 1) * 1024], in_offset=None,
                        bounds_check=bc2_reg, oob_is_err=False), inc="sd", amt=16)
                sd_after[e] = ("sd", k.cnt["sd"])
            st_ev["sd_prev"] = ("sd", k.cnt["sd"])
            st_ev["gd_prev"] = ("gd", k.cnt["gd"])
            return ("sd", k.cnt["sd"]), gpre

        def stage_G(mt, sd_tot, gpre):
            zi = [0]
            zz_done = None
            for j in range(4):
                if j == 0 and len(gpre) == 1:
                    b1buf, rdy1 = gpre[0]
                    b2buf, rdy2 = slabs.load(w2d(sw2T_d, 0))
                else:
                    b1buf, rdy1 = slabs.load(w2d(sw1T_d, j * 512))
                    b2buf, rdy2 = slabs.load(w2d(sw2T_d, j * 512))
                for it_in in range(4):
                    it = j * 4 + it_in
                    for n in range(NN):
                        bia, bsla = mmB.acq(cols=NW)
                        if it_in == 0 and n == 0:
                            k.w("tensor", rdy1)
                        for c in range(8):
                            ev1 = k.e("tensor", lambda E, b=b1buf, c=c, it_in=it_in, n=n, bsl=bsla: E.matmul(
                                bsl, wsl[b][:, c * 512 + it_in * 128:c * 512 + (it_in + 1) * 128],
                                xn_f[:, c * M + n * 512:c * M + n * 512 + NW],
                                start=(c == 0), stop=(c == 7)),
                                inc="mm" if c == 7 else None)
                        bib, bslb = mmB.acq(cols=NW)
                        if it_in == 0 and n == 0:
                            k.w("tensor", rdy2)
                        for c in range(8):
                            ev2 = k.e("tensor", lambda E, b=b2buf, c=c, it_in=it_in, n=n, bsl=bslb: E.matmul(
                                bsl, wsl[b][:, c * 512 + it_in * 128:c * 512 + (it_in + 1) * 128],
                                xn_f[:, c * M + n * 512:c * M + n * 512 + NW],
                                start=(c == 0), stop=(c == 7)),
                                inc="mm" if c == 7 else None)
                        i2 = zi[0] % 2
                        k.w("scalar", ev1)
                        if st_ev["z1t_free"][i2] is not None:
                            k.w("scalar", st_ev["z1t_free"][i2])
                        e_z1 = k.e("scalar", lambda E, bia=bia, i2=i2: E.activation(
                            z1t[i2][:, 0:NW], mmB.sl(bia, 0, NW), AF.Silu), inc="sc")
                        mmB.rel(bia, e_z1)
                        k.w("vector", ev2)
                        k.w("vector", e_z1)
                        if it == 0 and n == 0:
                            k.w("vector", st_ev["D_mm_end"])
                            k.w("vector", sd_tot)
                        e_zz = k.e("vector", lambda E, bib=bib, it=it, n=n, i2=i2: E.tensor_mul(
                            zz[:, it * M + n * 512:it * M + n * 512 + NW],
                            z1t[i2][:, 0:NW], mmB.sl(bib, 0, NW)), inc="ev")
                        mmB.rel(bib, e_zz)
                        st_ev["z1t_free"][i2] = e_zz
                        zz_done = e_zz
                        zi[0] += 1
                slabs.free_evt[b1buf] = ev1
                slabs.free_evt[b2buf] = ev2
            if g_level < 2:
                st_ev["G_mm_end"] = ev2
                return zz_done
            G_end = None
            acc_done = None
            for j in range(2):
                br0, rdy0 = slabs.load(wrow(sw3T_d, 0, j * 512))
                br1, rdy1 = slabs.load(wrow(sw3T_d, 1024, j * 512))
                for o in range(4):
                    oc = j * 4 + o
                    for n in range(NN):
                        bi, bsl = mmB.acq(cols=NW)
                        if o == 0 and n == 0:
                            k.w("tensor", rdy0)
                            k.w("tensor", rdy1)
                            k.w("tensor", zz_done)
                        for half, bb in ((0, br0), (1, br1)):
                            for c in range(8):
                                i = half * 8 + c
                                ev = k.e("tensor", lambda E, b=bb, c=c, o=o, n=n, i=i, bsl=bsl: E.matmul(
                                    bsl, wsl[b][:, c * 512 + o * 128:c * 512 + (o + 1) * 128],
                                    zz[:, i * M + n * 512:i * M + n * 512 + NW],
                                    start=(i == 0), stop=(i == 15)),
                                    inc="mm" if i == 15 else None)
                        G_end = ev
                        k.w("vector", ev)
                        e_a = k.e("vector", lambda E, bi=bi, oc=oc, n=n: E.tensor_add(
                            acc[:, oc * M + n * 512:oc * M + n * 512 + NW],
                            acc[:, oc * M + n * 512:oc * M + n * 512 + NW],
                            mmB.sl(bi, 0, NW)), inc="ev")
                        mmB.rel(bi, e_a)
                        acc_done = e_a
                slabs.free_evt[br0] = G_end
                slabs.free_evt[br1] = G_end
            st_ev["G_mm_end"] = G_end
            return acc_done

        def stage_H(mt, acc_done, sd_tot):
            H_end = None
            for tt in range(NT):
                row0 = mt * M + tt * 128
                slot = tt % 2
                k.w("sync", st_ev["xfin_free"][slot])
                ld = k.e("sync", lambda E, r=row0, s=slot: E.dma_start(
                    out=xfin[s][:], in_=x_d[r:r + 128, :]), inc=f"xf{slot}", amt=16)
                if tt == 0:
                    k.w("sync", sd_tot)
                if st_ev.get("ytk_free", [None, None])[slot] is not None:
                    k.w("sync", st_ev["ytk_free"][slot])
                ld_y = k.e("sync", lambda E, r=row0, s=slot: E.dma_start(
                    out=ytk_w[s][:].rearrange("p (two n) -> p two n", two=2),
                    in_=yt_d.rearrange("(t two) n -> t two n", two=2)[r:r + 128]),
                    inc=f"yd{slot}", amt=16)
                k.w("tensor", acc_done)
                k.w("vector", ld)
                k.w("vector", ld_y)
                if st_ev["otok_store"][slot] is not None:
                    k.w("vector", st_ev["otok_store"][slot])
                for half in range(2):
                    bi, _ = trB.acq()
                    for cc in range(4):
                        c = half * 4 + cc
                        ev = k.e("tensor", lambda E, c=c, tt=tt, bi=bi, cc=cc: E.transpose(
                            trB.sl(bi, cc * 128, cc * 128 + 128),
                            acc[:, c * M + tt * 128:c * M + (tt + 1) * 128],
                            identf[:]), inc="mm" if cc == 3 else None)
                    H_end = ev
                    k.w("vector", ev)
                    k.e("vector", lambda E, bi=bi, h=half, s=slot: E.tensor_add(
                        otok[s][:, h * 512:(h + 1) * 512], trB.sl(bi, 0, 512),
                        xfin[s][:, h * 512:(h + 1) * 512]))
                    k.e("vector", lambda E, h=half, s=slot: E.tensor_add(
                        otok[s][:, h * 512:(h + 1) * 512],
                        otok[s][:, h * 512:(h + 1) * 512],
                        ytk_w[s][:, h * 512:(h + 1) * 512]), inc="ev")
                    e_o = k.e("vector", lambda E, h=half, s=slot: E.tensor_add(
                        otok[s][:, h * 512:(h + 1) * 512],
                        otok[s][:, h * 512:(h + 1) * 512],
                        ytk_w[s][:, 1024 + h * 512:1024 + (h + 1) * 512]), inc="ev")
                    trB.rel(bi, e_o)
                st_ev["xfin_free"][slot] = e_o
                st_ev.setdefault("ytk_free", [None, None])[slot] = e_o
                k.w("sync", e_o)
                st_out = k.e("sync", lambda E, r=row0, s=slot: E.dma_start(
                    out=y_d[r:r + 128, :], in_=otok[s][:]), inc=f"od{slot}", amt=16)
                st_ev["otok_store"][slot] = st_out
                n_store[0] += 1
            st_ev["H_mm_end"] = H_end

        for mt in range(NMT):
            xnf_done = qk_done = v_done = aof_done = rt_done = acc_done = None
            sd_tot = None
            pre = []
            if "B" in stages:
                pre = preload_B(mt)
            if "A" in stages:
                xnf_done = stage_A(mt)
            if "B" in stages:
                qk_done, v_done = stage_B(mt, xnf_done, pre)
            if "C" in stages:
                aof_done = stage_C(mt, qk_done, v_done)
            if "E" in stages:
                rt_done = stage_E(mt)
                st_ev["routing_done"] = rt_done
            if "D" in stages:
                stage_D(mt, aof_done)
            gpre = []
            if "F" in stages:
                sd_tot, gpre = stage_F(mt, rt_done, xnf_done)
            if "G" in stages:
                acc_done = stage_G(mt, sd_tot, gpre)
            if "H" in stages:
                stage_H(mt, acc_done, sd_tot)

        if debug:
            k.w("gpsimd", ("od0", k.cnt["od0"]))
            k.w("gpsimd", ("od1", k.cnt["od1"]))
            k.e("gpsimd", lambda E: E.dma_start(out=dbg["dbg_xnf"], in_=xn_f[:]),
                inc="dbg", amt=16)
            k.e("gpsimd", lambda E: E.dma_start(out=dbg["dbg_logits"], in_=logits[:]),
                inc="dbg", amt=16)
            k.e("gpsimd", lambda E: E.dma_start(out=dbg["dbg_srow"], in_=s_tokall[:]),
                inc="dbg", amt=16)
            k.e("gpsimd", lambda E: E.dma_start(out=dbg["dbg_qf"], in_=scr[:, 0:SM]),
                inc="dbg", amt=16)
            k.e("gpsimd", lambda E: E.dma_start(out=dbg["dbg_aof"], in_=scr[:, 3 * SM:4 * SM]),
                inc="dbg", amt=16)
            k.e("gpsimd", lambda E: E.dma_start(out=dbg["dbg_acc"], in_=acc[:]),
                inc="dbg", amt=16)
            k.e("gpsimd", lambda E: E.dma_start(out=dbg["dbg_hsb"], in_=scr[:, SM:2 * SM]),
                inc="dbg", amt=16)
            k.e("gpsimd", lambda E: E.dma_start(out=dbg["dbg_zz"], in_=scr[:, 2 * SM:4 * SM]),
                inc="dbg", amt=16)
            k.w("gpsimd", ("dbg", k.cnt["dbg"]))

        k.w("sync", ("od0", k.cnt["od0"]))
        k.w("sync", ("od1", k.cnt["od1"]))

        with nc.Block() as block:
            @block.tensor
            def _(E):
                for f in k.plan["tensor"]:
                    f(E)

            @block.vector
            def _(E):
                for f in k.plan["vector"]:
                    f(E)

            @block.scalar
            def _(E):
                for f in k.plan["scalar"]:
                    f(E)

            @block.gpsimd
            def _(E):
                for f in k.plan["gpsimd"]:
                    f(E)

            @block.sync
            def _(E):
                for f in k.plan["sync"]:
                    f(E)

    return nc


def make_inmaps(inputs, tokens_per_core, n_cores):
    x = np.ascontiguousarray(np.asarray(inputs["L0_attn_x"], dtype=np.float32)
                             .reshape(-1, D))
    wqkv = np.ascontiguousarray(np.asarray(inputs["L0_attn_W_qkv"], np.float32))
    woutT = np.ascontiguousarray(np.asarray(inputs["L0_attn_W_out"], np.float32).T)
    bout = np.ascontiguousarray(np.asarray(inputs["L0_attn_b_out"], np.float32))
    gwT = np.ascontiguousarray(np.asarray(inputs["L0_ffn_gate_w"], np.float32).T)
    w1T = np.ascontiguousarray(np.asarray(inputs["L0_ffn_w1"], np.float32).transpose(0, 2, 1))
    w2T = np.ascontiguousarray(np.asarray(inputs["L0_ffn_w2"], np.float32).transpose(0, 2, 1))
    w3T = np.ascontiguousarray(np.asarray(inputs["L0_ffn_w3"], np.float32).transpose(0, 2, 1))
    b1T = np.ascontiguousarray(np.asarray(inputs["L0_ffn_b1"], np.float32).T)
    b2T = np.ascontiguousarray(np.asarray(inputs["L0_ffn_b2"], np.float32).T)
    b3T = np.ascontiguousarray(np.asarray(inputs["L0_ffn_b3"], np.float32).T)
    sw1T = np.ascontiguousarray(np.asarray(inputs["L0_ffn_sw1"], np.float32).T)
    sw2T = np.ascontiguousarray(np.asarray(inputs["L0_ffn_sw2"], np.float32).T)
    sw3T = np.ascontiguousarray(np.asarray(inputs["L0_ffn_sw3"], np.float32).T)
    shared = dict(wqkv=wqkv, woutT=woutT, bout=bout, gwT=gwT, w1T=w1T, w2T=w2T,
                  w3T=w3T, b1T=b1T, b2T=b2T, b3T=b3T, sw1T=sw1T, sw2T=sw2T,
                  sw3T=sw3T)
    in_maps = []
    for c in range(n_cores):
        m = dict(shared)
        m["x"] = np.ascontiguousarray(
            x[c * tokens_per_core:(c + 1) * tokens_per_core])
        in_maps.append(m)
    return in_maps


def kernel(**inputs):
    n_cores = 8
    tokens_per_core = 4096
    in_maps = make_inmaps(inputs, tokens_per_core, n_cores)
    nc = build_nc(tokens_per_core)
    res = run_bass_kernel_spmd(nc, in_maps, core_ids=list(range(n_cores)))
    out = np.concatenate([r["y"] for r in res.results], axis=0)
    return out.reshape(16, 2048, D).astype(np.float32)
